# version 05
# baseline (speedup 1.0000x reference)
import numpy as np
import ml_dtypes
from contextlib import ExitStack

import concourse.bass as bass
import concourse.mybir as mybir
from concourse.bass_utils import run_bass_kernel_spmd

F32 = mybir.dt.float32
BF16 = mybir.dt.bfloat16
I32 = mybir.dt.int32
AF = mybir.ActivationFunctionType
ALU = mybir.AluOpType
AX = mybir.AxisListType

NCORES = 8
S_TOK = 8192
TPC = S_TOK // NCORES
HID = 2048
NH = HID // 128
CONV_DIM = 4096
NCV = CONV_DIM // 128
HK, HV, DK, DV = 8, 16, 128, 128
CHUNK = 128
NCH = TPC // CHUNK
FFN_I = 8192
NFI = FFN_I // 128
TT = 512
NTT = TPC // TT
INVSQ = float(1.0 / np.sqrt(np.float32(8192.0)))
RMS_EPS = 1e-6
L2_EPS = 1e-6
LN_EPS = 1e-5
Q_EPS = 1e-5
DK_SCALE = float(DK ** -0.5)

ENGINES = ("sync", "tensor", "vector", "scalar", "gpsimd")

SM_BQKV = 0
SM_BZ = 32
SM_BOUT = 48
SM_CONVB = 64
SM_CONVW = 96
SM_NORMW = 224
SM_BB = 225
SM_BA = 226
SM_DTB = 227
SM_ALOG = 228
SM_OH_PREV = 229
SM_OH_SELF = 230
SM_OHB = 232


class Tok:
    __slots__ = ("sem", "val")

    def __init__(self, sem, val):
        self.sem = sem
        self.val = val


class Sched:
    def __init__(self):
        self.ops = {e: [] for e in ENGINES}
        self.cnt = {}
        self.waited = {}
        self.sem_owner = {}
        self.max_wait = {}
        self.n_inst = 0

    def inc_tok(self, name, amt):
        self.cnt[name] = self.cnt.get(name, 0) + amt
        return Tok(name, self.cnt[name])

    def wait(self, eng, toks, force=False):
        if toks is None:
            return
        if isinstance(toks, Tok):
            toks = [toks]
        for t in toks:
            if t is None:
                continue
            assert t.val <= self.cnt.get(t.sem, 0), \
                f"wait {t.sem}>={t.val} > emitted {self.cnt.get(t.sem, 0)}"
            if (self.sem_owner.get(t.sem) == eng and not force
                    and not t.sem.startswith("D") and t.sem != "CC"):
                continue
            key = (eng, t.sem)
            if self.waited.get(key, -1) >= t.val:
                continue
            self.waited[key] = t.val
            self.max_wait[t.sem] = max(self.max_wait.get(t.sem, 0), t.val)
            self.ops[eng].append(
                lambda h, sems, s=t.sem, v=t.val: h.wait_ge(sems[s], v))
            self.n_inst += 1

    SERIAL = ("vector", "scalar")
    N_DSEM = 80

    def stream_sem(self, stream):
        if not hasattr(self, "stream_map"):
            self.stream_map = {}
        if stream not in self.stream_map:
            idx = len(self.stream_map)
            assert idx < self.N_DSEM, "out of DMA stream semaphores"
            self.stream_map[stream] = f"D{idx}"
        return self.stream_map[stream]

    def emit(self, eng, fn, waits=None, inc=None, inc_amt=1):
        if eng in self.SERIAL:
            prev = getattr(self, "last_tok", {}).get(eng)
            if prev is not None:
                self.wait(eng, [prev], force=True)
        self.wait(eng, waits)
        tok = None
        self.n_inst += 1
        if inc is not None:
            owner = self.sem_owner.setdefault(inc, eng)
            assert owner == eng, f"sem {inc}: owners {owner} and {eng}"
            tok = self.inc_tok(inc, inc_amt)

            def wrapped(h, sems, fn=fn, inc=inc, inc_amt=inc_amt):
                fn(h, sems).then_inc(sems[inc], inc_amt)
            self.ops[eng].append(wrapped)
            if eng in self.SERIAL:
                if not hasattr(self, "last_tok"):
                    self.last_tok = {}
                self.last_tok[eng] = tok
        else:
            self.ops[eng].append(lambda h, sems, fn=fn: fn(h, sems))
        return tok

    def finalize_check(self):
        for sem, v in self.max_wait.items():
            assert v <= self.cnt[sem], f"sem {sem}: wait {v} > total {self.cnt[sem]}"


class Banks:
    def __init__(self, aps):
        self.aps = aps
        self.writer = [None] * 8
        self.readers = [[] for _ in range(8)]

    def acquire(self, i):
        w = [t for t in self.readers[i] if t is not None]
        self.readers[i] = []
        return self.aps[i], w

    def wrote(self, i, tok):
        self.writer[i] = tok

    def read_waits(self, i):
        return [self.writer[i]] if self.writer[i] is not None else []

    def read_done(self, i, tok):
        self.readers[i].append(tok)


def _hadamard(n):
    H = np.array([[1.0]], dtype=np.float32)
    while H.shape[0] < n:
        H = np.block([[H, H], [H, -H]])
    return H.astype(np.float32)


def _const_arrays():
    bf = ml_dtypes.bfloat16
    h64 = _hadamard(64)
    h64b = np.zeros((128, 128), dtype=np.float32)
    h64b[:64, :64] = h64
    h64b[64:, 64:] = h64
    tril_s = np.tril(np.ones((128, 128), np.float32), -1)
    return {
        "ident_f32": (np.eye(128, dtype=np.float32), F32),
        "ident_bf16": (np.eye(128, dtype=np.float32).astype(bf), BF16),
        "h128_bf": (_hadamard(128).astype(bf), BF16),
        "h64blk_bf": (h64b.astype(bf), BF16),
        "mask_ls": (tril_s, F32),
        "mask_us": (np.ascontiguousarray(tril_s.T), F32),
        "mask_ui": (np.triu(np.ones((128, 128), np.float32)), F32),
        "ones_row": (np.ones((1, 128), dtype=np.float32), F32),
        "ones_col": (np.ones((128, 1), dtype=np.float32), F32),
    }


def build_program(debug_taps=(), stop_after=None, p3_limit=None, p3_steps=None):
    import os as _os
    p3v = _os.environ.get("P3V", "")
    nc = bass.Bass("TRN2", target_bir_lowering=False)
    S = Sched()
    taps = set(debug_taps)
    dbg_names = []

    x_d = nc.dram_tensor("x_sl", [TPC, HID], F32, kind="ExternalInput").ap()
    wqkvT_d = nc.dram_tensor("wqkvT", [HID, CONV_DIM], F32, kind="ExternalInput").ap()
    wzT_d = nc.dram_tensor("wzT", [HID, HID], F32, kind="ExternalInput").ap()
    woutT_d = nc.dram_tensor("woutT", [HID, HID], F32, kind="ExternalInput").ap()
    wbaT_d = nc.dram_tensor("wbaT", [HID, 32], F32, kind="ExternalInput").ap()
    wgT_d = nc.dram_tensor("wgT_sl", [HID // NCORES, FFN_I], F32, kind="ExternalInput").ap()
    wuT_d = nc.dram_tensor("wuT_sl", [HID // NCORES, FFN_I], F32, kind="ExternalInput").ap()
    wdT_d = nc.dram_tensor("wdT_sl", [FFN_I // NCORES, HID], F32, kind="ExternalInput").ap()
    smalls_d = nc.dram_tensor("smalls", [128, 256], F32, kind="ExternalInput").ap()
    out_d = nc.dram_tensor("out_sl", [TPC, HID], F32, kind="ExternalOutput").ap()

    wqkv_bf = nc.dram_tensor("wqkv_bf", [HID, CONV_DIM], BF16).ap()
    wz_bf = nc.dram_tensor("wz_bf", [HID, HID], BF16).ap()
    wout_bf = nc.dram_tensor("wout_bf", [HID, HID], BF16).ap()
    wba_bf = nc.dram_tensor("wba_bf", [HID, 32], BF16).ap()
    ar_in = nc.dram_tensor("ar_in", [1, 4], F32).ap()
    ar_out = nc.dram_tensor("ar_out", [1, 4], F32).ap()
    agg_in = nc.dram_tensor("agg_in", [HID // NCORES, FFN_I], BF16).ap()
    agg_out = nc.dram_tensor("agg_out", [HID, FFN_I], BF16).ap()
    agu_in = nc.dram_tensor("agu_in", [HID // NCORES, FFN_I], BF16).ap()
    agu_out = nc.dram_tensor("agu_out", [HID, FFN_I], BF16).ap()
    agd_in = nc.dram_tensor("agd_in", [FFN_I // NCORES, HID], BF16).ap()
    agd_out = nc.dram_tensor("agd_out", [FFN_I, HID], BF16).ap()
    xqh = nc.dram_tensor("xqh", [NH, 128, TPC], BF16).ap()
    halo_in = nc.dram_tensor("halo_in", [NCV, 128, 3], F32).ap()
    halo_out = nc.dram_tensor("halo_out", [NCORES * NCV, 128, 3], F32).ap()
    rowsd = nc.dram_tensor("rowsd", [16, 3 * TPC], F32).ap()
    dpk = nc.dram_tensor("dpk", [NCH * HV, 128, 512], BF16).ap()
    mb_in = nc.dram_tensor("mb_in", [HV * 256, 128], BF16).ap()
    mb_out = nc.dram_tensor("mb_out", [NCORES * HV * 256, 128], BF16).ap()
    zh = nc.dram_tensor("zh", [NFI, 128, TPC], BF16).ap()
    zhl = nc.dram_tensor("zhl", [NFI, 128, TPC], BF16).ap()
    qsc = nc.dram_tensor("qsc", [128, 8], F32).ap()
    yqh = nc.dram_tensor("yqh", [NFI, 128, TPC], BF16).ap()
    zsc = nc.dram_tensor("zsc", [HV, 128, TPC], BF16).ap()
    ffnh = nc.dram_tensor("ffnh", [NH, 128, TPC], F32).ap()

    const_arr = _const_arrays()
    const_dram = {k: nc.inline_tensor(np.asarray(v), name="c_" + k)
                  for k, (v, dt) in const_arr.items()}

    with ExitStack() as ctx:
        def sbuf(name, shape, dt):
            return ctx.enter_context(nc.sbuf_tensor(name, shape, dt))

        sm = sbuf("sm", [128, 256], F32)
        sconst = sbuf("sconst", [128, 8], F32)
        cst = {k: sbuf("k_" + k, list(v.shape), dt) for k, (v, dt) in const_arr.items()}
        ones16 = sbuf("ones16", [16, CHUNK], F32)
        bTp = sbuf("bTp", [128, 6 * NCH * 16], F32)
        ln_s = sbuf("ln_s", [128, 8 * 32], F32)
        srl = sbuf("srl", [1, TPC + 512], F32)
        rowbuf = sbuf("rowbuf", [1, 2 * 384], F32)
        bcrow = sbuf("bcrow", [1, NCH * 16], F32)
        edge = sbuf("edge", [128, NCV * 7], F32)
        hblk = sbuf("hblk", [128, 2 * NCV * 3], F32)
        hsel = sbuf("hsel", [128, NCV * 3 + 16], F32)
        misc = sbuf("misc", [128, 64], F32)
        prefb = sbuf("prefb", [128, 2 * 128], BF16)
        expf = sbuf("expf", [128, 4 * 128], F32)

        pb = sbuf("pb", [128, NCV * TPC], BF16)
        pc = sbuf("pc", [128, NH * TPC], BF16)
        pf = sbuf("pf", [128, 6144], F32)

        tmA = sbuf("tmA", [128, 2048], F32)
        tmB = sbuf("tmB", [128, 2048], F32)
        tmI = sbuf("tmI", [128, 1024], I32)
        mixc = sbuf("mixc", [128, 1024], F32)
        cvo = sbuf("cvo", [128, 1024], F32)
        cvo2 = sbuf("cvo2", [128, 1024], F32)
        wbuf = sbuf("wbuf", [128, 2 * 2048], BF16)
        wbuf2 = sbuf("wbuf2", [128, 2 * 2048], BF16)
        intm = sbuf("intm", [128, 2 * TT], BF16)
        yq_t = sbuf("yq_t", [128, 2048], BF16)
        relo = sbuf("relo", [128, 2 * 512], BF16)
        zrel = sbuf("zrel", [128, 1024], BF16)
        s0b = sbuf("s0b", [128, HV * 128], BF16)

        xt = pf[:, 4096:6144]
        xn = pc
        WRK0 = 0
        DSTG0 = 6144
        MBST0 = 7680
        coreb = pc
        PF_SGN = 0
        PF_ABS = 2048
        GB_B = 0
        GB_BHAT = 1024
        GB_NEGB = 2048
        GB_G = 3072
        GB_BETA = 4096
        GB_SCR = 5120
        PF_SGB = 0
        PF_YMX = 3072
        PF_YAB = 3584
        PF_OTM = 0
        PF_FREL = 2048
        PF_XRE = 4096

        psum_t = [ctx.enter_context(nc.psum_tensor(f"pb{i}", [128, 512], F32))
                  for i in range(7)]
        psum_bf = ctx.enter_context(nc.psum_tensor("pbf", [128, 1024], BF16))

        for i, v in enumerate([0.0, 1.0, float(LN_EPS), float(L2_EPS), -1.0, 1e-6,
                               -128.0, 1.0 / 128.0]):
            nc.const_aps.register(F32, v, sconst[:, i:i + 1])

        sems = {}
        for name in ["MM", "VE", "SC", "CC"] + [f"D{i}" for i in range(Sched.N_DSEM)]:
            sems[name] = ctx.enter_context(nc.semaphore(name))

        PS = Banks([p[:, :] for p in psum_t] + [psum_bf[:, :]])

        def dma(eng, out, in_, stream, waits=None, slow=False):
            sem = S.stream_sem(stream)
            prev = S.cnt.get(sem, 0)
            if prev > 0:
                S.wait(eng, [Tok(sem, prev)], force=True)
            return S.emit(eng, lambda h, sems, o=out, i=in_, sl=slow: h.dma_start(
                out=o, in_=i, allow_slow_non_contiguous=sl),
                waits=waits, inc=sem, inc_amt=16)

        def mm(out, lhsT, rhs, start=True, stop=True, waits=None, inc="MM"):
            return S.emit("tensor",
                          lambda h, sems, o=out, l=lhsT, r=rhs, a=start, b=stop:
                          h.matmul(o, l, r, start=a, stop=b),
                          waits=waits, inc=inc)

        def transp(out, in_, ident, waits=None, inc="MM"):
            return S.emit("tensor",
                          lambda h, sems, o=out, i=in_, d=ident: h.transpose(o, i, d),
                          waits=waits, inc=inc)

        def vop(fn, waits=None, inc="VE"):
            return S.emit("vector", fn, waits=waits, inc=inc)

        def act(out, in_, func, bias=0.0, scale=1.0, waits=None, inc="SC"):
            return S.emit("scalar",
                          lambda h, sems, o=out, i=in_, f=func, b=bias, s=scale:
                          h.activation(out=o, in_=i, func=f, bias=b, scale=s),
                          waits=waits, inc=inc)

        def scopy(out, in_, waits=None):
            return S.emit("scalar",
                          lambda h, sems, o=out, i=in_: h.copy(out=o, in_=i),
                          waits=waits, inc="SC")

        def vcopy(out, in_, waits=None):
            return vop(lambda h, sems, o=out, i=in_: h.tensor_copy(out=o, in_=i),
                       waits=waits)

        def vts(out, in0, s1, s2=None, op0=ALU.mult, op1=ALU.bypass, waits=None):
            return vop(lambda h, sems, o=out, i=in0, a=s1, b=s2, p=op0, q=op1:
                       h.tensor_scalar(out=o, in0=i, scalar1=a, scalar2=b, op0=p, op1=q),
                       waits=waits)

        def vtt(out, in0, in1, op=ALU.mult, waits=None):
            return vop(lambda h, sems, o=out, a=in0, b=in1, p=op:
                       h.tensor_tensor(out=o, in0=a, in1=b, op=p), waits=waits)

        def vstt(out, in0, scl, in1, op0=ALU.mult, op1=ALU.add, waits=None):
            return vop(lambda h, sems, o=out, a=in0, s=scl, b=in1, p=op0, q=op1:
                       h.scalar_tensor_tensor(out=o, in0=a, scalar=s, in1=b, op0=p, op1=q),
                       waits=waits)

        def vsem():
            tk = S.emit("vector", lambda h, sems: h.nop(), inc="VE")
            S.wait("vector", [tk], force=True)
            return tk

        def tree_reduce(dst, src, bufA, bufB, n, op=ALU.add, waits=None):
            pp = [bufA, bufB]
            t = vtt(pp[0][:, 0:n // 2], src[:, 0:n // 2], src[:, n // 2:n], op=op,
                    waits=waits)
            m = n // 2
            k = 0
            while m > 4:
                t = vtt(pp[(k + 1) % 2][:, 0:m // 2], pp[k % 2][:, 0:m // 2],
                        pp[k % 2][:, m // 2:m], op=op, waits=[t])
                m //= 2
                k += 1
            b = pp[k % 2]
            o = pp[(k + 1) % 2]
            t1 = vts(o[:, 0:1], b[:, 0:1], b[:, 1:2], None, op, waits=[t])
            t2 = vts(o[:, 1:2], b[:, 2:3], b[:, 3:4], None, op, waits=[t])
            return vts(dst, o[:, 0:1], o[:, 1:2], None, op, waits=[t1, t2])

        def coll(kind, op, ins, outs, waits):
            return S.emit("gpsimd",
                          lambda h, sems, k=kind, o=op, i=ins, u=outs:
                          h.collective_compute(k, o, replica_groups=[list(range(NCORES))],
                                               ins=i, outs=u),
                          waits=waits, inc="CC")

        def tapout(name, shape, dt):
            if name in taps:
                dbg_names.append(name)
                return nc.dram_tensor("dbg_" + name, shape, dt, kind="ExternalOutput").ap()
            return None

        tp_xn = tapout("xn", [128, NH * TPC], BF16)
        tp_xq = tapout("xq", [NH, 128, TPC], BF16)
        tp_qkv = tapout("qkv", [128, NCV * TPC], BF16)
        tp_gb = tapout("gb", [16, 6144], F32)
        tp_z = tapout("z", [HV, 128, TPC], BF16)
        tp_tern = tapout("tern", [HID, FFN_I], BF16)
        tp_core = tapout("core", [128, HV * TPC], BF16)
        tp_dpk = tapout("dpk", [NCH * HV, 128, 512], BF16)
        tp_mb = tapout("mb", [HV * 256, 128], BF16)
        tp_ffn = tapout("ffn", [NH, 128, TPC], F32)
        tp_zh = tapout("zh", [NFI, 128, TPC], BF16)
        tp_rows = tapout("rows", [16, 3 * TPC], F32)
        tp_srow = tapout("srow", [1, TPC], F32)

        t_smalls = dma("sync", sm[:, :], smalls_d, "smalls")
        t_consts = None
        for k in cst:
            t_consts = dma("sync", cst[k][:, :], const_dram[k].ap(), "consts")
        for i, v in enumerate([0.0, 1.0, float(LN_EPS), float(L2_EPS), -1.0, 1e-6,
                               -128.0, 1.0 / 128.0]):
            S.emit("vector", lambda h, sems, i=i, v=v: h.memset(sconst[:, i:i + 1], v),
                   inc="VE")
        S.emit("vector", lambda h, sems: h.memset(ones16[:, :], 1.0), inc="VE")
        t_cz = S.emit("vector", lambda h, sems: h.nop(), inc="VE")
        S.emit("scalar", lambda h, sems: h.nop(), waits=[t_cz], inc="SC")

        t_wcast = None
        for nm, src, dst_ in (("qkv", wqkvT_d, wqkv_bf), ("z", wzT_d, wz_bf),
                              ("out", woutT_d, wout_bf), ("ba", wbaT_d, wba_bf)):
            for r0 in range(0, HID, 512):
                t_wcast = dma("gpsimd", dst_[r0:r0 + 512, :], src[r0:r0 + 512, :],
                              "wcast")
        awt = {"qkv": t_wcast, "z": t_wcast, "out": t_wcast, "ba": t_wcast}

        xt_readers = []
        tb_rot = [2, 3]
        lnb = 6
        t_xq_store = []
        prev_tpxn = None
        prev_tpxq = None
        t_xn_all = None
        for i in range(8):
            c0 = i * 32
            t_x = dma("sync", xt[:, :], x_d[i * 128:(i + 1) * 128, :], "xt",
                      waits=xt_readers)
            xt_readers = []
            t_s = tree_reduce(ln_s[:, c0:c0 + 1], xt, cvo, cvo2, HID,
                              waits=[t_x, t_cz])
            t_m = vts(ln_s[:, c0 + 1:c0 + 2], ln_s[:, c0:c0 + 1], 1.0 / HID, waits=[t_s])
            t_sq = act(tmA[:, :], xt[:, :], AF.Square, waits=[t_x, prev_tpxn])
            t_s2 = tree_reduce(ln_s[:, c0 + 2:c0 + 3], tmA, cvo, cvo2, HID,
                               waits=[t_sq, t_s])
            t_ms = vts(ln_s[:, c0 + 3:c0 + 4], ln_s[:, c0 + 2:c0 + 3], 1.0 / HID,
                       waits=[t_s2])
            t_m2 = vts(ln_s[:, c0 + 4:c0 + 5], ln_s[:, c0 + 1:c0 + 2],
                       ln_s[:, c0 + 1:c0 + 2], waits=[t_m])
            t_var = vtt(ln_s[:, c0 + 5:c0 + 6], ln_s[:, c0 + 3:c0 + 4],
                        ln_s[:, c0 + 4:c0 + 5], op=ALU.subtract, waits=[t_ms, t_m2])
            t_sd = act(ln_s[:, c0 + 16:c0 + 17], ln_s[:, c0 + 5:c0 + 6], AF.Sqrt,
                       bias=float(LN_EPS), waits=[t_var])
            t_rstd = vop(lambda h, sems, c=c0: h.reciprocal(
                out=ln_s[:, c + 7:c + 8], in_=ln_s[:, c + 16:c + 17]), waits=[t_sd])
            t_negm = vts(ln_s[:, c0 + 13:c0 + 14], ln_s[:, c0 + 1:c0 + 2], -1.0,
                         waits=[t_m])
            t_xnt = vts(tmA[:, :], xt[:, :], ln_s[:, c0 + 13:c0 + 14],
                        ln_s[:, c0 + 7:c0 + 8], ALU.add, ALU.mult,
                        waits=[t_negm, t_rstd, t_x, prev_tpxn, t_s2])
            xt_readers = [t_s, t_xnt]
            t_ab = act(tmB[:, :], tmA[:, :], AF.Abs, waits=[t_xnt, prev_tpxq])
            t_am = tree_reduce(ln_s[:, c0 + 9:c0 + 10], tmB, cvo, cvo2, HID,
                               op=ALU.max, waits=[t_ab, t_s2])
            t_amc = vts(ln_s[:, c0 + 10:c0 + 11], ln_s[:, c0 + 9:c0 + 10],
                        float(Q_EPS), None, ALU.max, waits=[t_am])
            t_sinv = vts(ln_s[:, c0 + 11:c0 + 12], ln_s[:, c0 + 10:c0 + 11],
                         1.0 / 128.0, waits=[t_amc])
            t_sc = vop(lambda h, sems, c=c0: h.reciprocal(
                out=ln_s[:, c + 12:c + 13], in_=ln_s[:, c + 11:c + 12]),
                waits=[t_sinv])
            t_q0 = vts(tmB[:, :], tmA[:, :], ln_s[:, c0 + 12:c0 + 13], 127.0,
                       ALU.mult, ALU.min, waits=[t_sc, t_am])
            t_q0 = vts(tmB[:, :], tmB[:, :], 128.0, None, ALU.add, waits=[t_q0])
            t_q5s = []
            fbuf = [cvo, cvo2]
            for hh in range(2):
                sl_b = tmB[:, hh * 1024:(hh + 1) * 1024]
                fb = fbuf[hh][:, 0:1024]
                t_a1q = vcopy(tmI[:, :], sl_b, waits=[t_q0])
                t_a2q = vcopy(fb, tmI[:, :], waits=[t_a1q])
                t_a3q = vtt(sl_b, sl_b, fb, op=ALU.subtract, waits=[t_a2q])
                t_a4q = vts(sl_b, sl_b, 0.5, None, ALU.is_ge, waits=[t_a3q])
                t_a5q = vtt(sl_b, sl_b, fb, op=ALU.add, waits=[t_a4q])
                t_q5 = vts(sl_b, sl_b, -128.0, ln_s[:, c0 + 11:c0 + 12],
                           ALU.add, ALU.mult, waits=[t_a5q])
                t_q5s.append(t_q5)
            t_q5 = t_q5s[-1]
            bb, wb = PS.acquire(lnb)
            t_r = mm(psum_t[lnb][0:1, 0:128], ln_s[:, c0 + 11:c0 + 12],
                     cst["ident_f32"][:, :], waits=wb + [t_sinv, t_consts])
            PS.wrote(lnb, t_r)
            t_rv = vcopy(srl[0:1, i * 128:(i + 1) * 128], psum_t[lnb][0:1, 0:128],
                         waits=PS.read_waits(lnb))
            PS.read_done(lnb, t_rv)
            for half, src_t in ((0, tmA), (1, tmB)):
                src_tok = t_xnt if half == 0 else t_q5
                for g in range(NH // 4):
                    bk = tb_rot[g % 2]
                    bap, wps = PS.acquire(bk)
                    t_tp = None
                    for slot in range(4):
                        c = g * 4 + slot
                        t_tp = transp(psum_t[bk][:, slot * 128:(slot + 1) * 128],
                                      src_t[:, c * 128:(c + 1) * 128],
                                      cst["ident_f32"][:, :],
                                      waits=(wps + [src_tok, t_consts] +
                                             ([t_q5s[0]] if half == 1 else []))
                                      if slot == 0 else None)
                    PS.wrote(bk, t_tp)
                    for slot in range(4):
                        c = g * 4 + slot
                        if half == 0:
                            dst_ap = xn[:, c * TPC + i * 128: c * TPC + (i + 1) * 128]
                            ew = PS.read_waits(bk)
                        else:
                            dst_ap = yq_t[:, c * 128:(c + 1) * 128]
                            ew = PS.read_waits(bk) + \
                                ([t_xq_store[-1]] if t_xq_store else [])
                        t_ev = vcopy(dst_ap, psum_t[bk][:, slot * 128:(slot + 1) * 128],
                                     waits=ew)
                        PS.read_done(bk, t_ev)
                    if half == 0:
                        prev_tpxn = t_tp
                        t_xn_all = t_ev
                    else:
                        prev_tpxq = t_tp
                        last_xq_ev = t_ev
            t_xqs = dma("gpsimd",
                        xqh[:, :, i * 128:(i + 1) * 128].transpose([1, 0, 2]),
                        yq_t[:, 0:2048].rearrange("p (c t) -> p c t", c=NH),
                        "xqst", waits=[last_xq_ev])
            t_xq_store.append(t_xqs)
        t_srow_all = t_rv
        t_xt_free = list(xt_readers)
        if tp_xn is not None:
            dma("gpsimd", tp_xn, xn[:, 0:NH * TPC], "dbg", waits=[t_xn_all])
        if tp_xq is not None:
            dma("gpsimd", tp_xq, xqh, "dbg", waits=[t_xq_store[-1]])
        if tp_srow is not None:
            dma("gpsimd", tp_srow, srl[0:1, 0:TPC], "dbg", waits=[t_srow_all])

        final_toks = []
        if stop_after == "p1":
            t_ag_g = t_ag_u = t_ag_d = None
            t_hag = None
            t_sc_tiles = [[None] * 4]
            t_rows_st = None
            final_toks += [t_xn_all, t_srow_all, t_xq_store[-1]]
        tern_jobs = [(0, wgT_d, agg_in, agg_out, HID // NCORES, FFN_I),
                     (1, wuT_d, agu_in, agu_out, HID // NCORES, FFN_I),
                     (2, wdT_d, agd_in, agd_out, FFN_I // NCORES, HID)]
        stage = [tmA, tmB]
        si = 0
        last_use = [prev_tpxn, prev_tpxq]
        first = [True, True, True]
        t_abs = None
        abscr = pf[:, PF_SGN:PF_SGN + 2048]
        pfA = pf[:, 2048:3072]
        pfB = pf[:, 3072:3584]
        prev_abscr = None
        for wi, src, agi, ago, nrow, ncol in tern_jobs:
            for r0 in range(0, nrow, 128):
                for cc0 in range(0, ncol, 2048):
                    st_t = stage[si % 2]
                    tl = dma("sync", st_t[:, :], src[r0:r0 + 128, cc0:cc0 + 2048],
                             f"tstg{si % 2}", waits=[last_use[si % 2]])
                    ta = act(abscr, st_t[:, :], AF.Abs, waits=[tl, prev_abscr])
                    last_use[si % 2] = ta
                    tr = tree_reduce(misc[:, 4:5], abscr, pfA, pfB, 2048,
                                     waits=[ta, prev_abscr])
                    prev_abscr = tr
                    if first[wi]:
                        t_abs = vcopy(misc[:, wi:wi + 1], misc[:, 4:5], waits=[tr])
                        first[wi] = False
                    else:
                        t_abs = vop(lambda h, sems, wi=wi: h.tensor_add(
                            out=misc[:, wi:wi + 1], in0=misc[:, wi:wi + 1],
                            in1=misc[:, 4:5]), waits=[tr])
                    si += 1
        bA, wA = PS.acquire(0)
        t_mm0 = mm(psum_t[0][0:1, 0:3], cst["ones_col"][:, :], misc[:, 0:3],
                   waits=wA + [t_abs, t_consts])
        PS.wrote(0, t_mm0)
        t_ps0 = vcopy(misc[0:1, 8:11], psum_t[0][0:1, 0:3], waits=PS.read_waits(0))
        PS.read_done(0, t_ps0)
        t_pz = vop(lambda h, sems: h.memset(misc[0:1, 11:12], 0.0), waits=[t_ps0])
        t_st0 = dma("gpsimd", ar_in[0:1, 0:4], misc[0:1, 8:12], "arst",
                    waits=[t_pz])
        t_ar = coll("AllReduce", ALU.add, [ar_in], [ar_out], [t_st0])
        t_lar = dma("sync", misc[0:1, 12:15], ar_out[0:1, 0:3], "arld", waits=[t_ar])
        cntw = float(HID * FFN_I)
        t_mean = vts(misc[0:1, 16:19], misc[0:1, 12:15], 1.0 / cntw, float(Q_EPS),
                     ALU.mult, ALU.max, waits=[t_lar])
        t_thr = vts(misc[0:1, 20:23], misc[0:1, 16:19], 0.5, waits=[t_mean])
        bB, wB = PS.acquire(1)
        t_b1 = None
        for wi in range(3):
            t_b1 = mm(psum_t[1][:, wi:wi + 1], cst["ones_row"][:, :],
                      misc[0:1, 20 + wi:21 + wi],
                      waits=(wB + [t_thr]) if wi == 0 else [t_thr])
            t_b1 = mm(psum_t[1][:, 4 + wi:5 + wi], cst["ones_row"][:, :],
                      misc[0:1, 16 + wi:17 + wi], waits=[t_mean])
        PS.wrote(1, t_b1)
        t_thrv = vcopy(misc[:, 24:27], psum_t[1][:, 0:3], waits=PS.read_waits(1))
        t_sgv = vcopy(misc[:, 28:31], psum_t[1][:, 4:7], waits=PS.read_waits(1))
        PS.read_done(1, t_sgv)

        si = 0
        ag_toks = {}
        t_tstore = None
        prev_tn = None
        sgn_s = pf[:, PF_SGN:PF_SGN + 2048]
        abs_s = pf[:, PF_ABS:PF_ABS + 2048]
        for wi, src, agi, ago, nrow, ncol in tern_jobs:
            for r0 in range(0, nrow, 128):
                for cc0 in range(0, ncol, 2048):
                    st_t = stage[si % 2]
                    tl = dma("sync", st_t[:, :], src[r0:r0 + 128, cc0:cc0 + 2048],
                             f"tstg{si % 2}", waits=[last_use[si % 2]])
                    t_sg = act(sgn_s, st_t[:, :], AF.Sign,
                               waits=[tl, prev_tn, prev_abscr])
                    t_ab = act(abs_s, st_t[:, :], AF.Abs, waits=[tl])
                    last_use[si % 2] = t_ab
                    t_gt = vts(abs_s, abs_s, misc[:, 24 + wi:25 + wi], None,
                               ALU.is_gt, waits=[t_ab, t_thrv])
                    for qh_ in range(2):
                        t_tn = vtt(zrel[:, 0:1024], abs_s[:, qh_ * 1024:(qh_ + 1) * 1024],
                                   sgn_s[:, qh_ * 1024:(qh_ + 1) * 1024], op=ALU.mult,
                                   waits=[t_gt, t_sg, t_tstore])
                        prev_tn = t_tn
                        t_tstore = dma("gpsimd",
                                       agi[r0:r0 + 128, cc0 + qh_ * 1024:cc0 + (qh_ + 1) * 1024],
                                       zrel[:, 0:1024], "ST", waits=[t_tn])
                    si += 1
            ag_toks[wi] = coll("AllGather", ALU.bypass, [agi], [ago], [t_tstore])
        t_ag_g, t_ag_u, t_ag_d = ag_toks[0], ag_toks[1], ag_toks[2]
        t_pf_free = [prev_tn, t_tstore]
        t_zrel_free = t_tstore
        if tp_tern is not None:
            dma("gpsimd", tp_tern, agg_out, "dbg", waits=[t_ag_g])
        if stop_after == "p0d":
            t_hag = None
            t_sc_tiles = [[None] * 4]
            t_rows_st = None
            final_toks += [t_ag_g, t_ag_u, t_ag_d]

        if stop_after not in ("p1", "p0d"):
            def wload(ring, slot, wsrc, ocol0, prev_consumer, nk=NH):
                dst = ring[:, slot * 2048: slot * 2048 + nk * 128]
                srcv = wsrc.rearrange("(fc p) o -> p fc o", p=128)[:, 0:nk, ocol0:ocol0 + 128]
                return dma("sync", dst.rearrange("p (fc c) -> p fc c", fc=nk), srcv,
                           "LD", waits=prev_consumer)

            qkv_ap = pb
            mm_bank_rot = [4, 5]
            t_conv_done = []
            t_edge_saved = []
            wring_consumer = [None, None]
            t_zstore = [None, None]
            prev_mix_readers = []
            prev_sql = None
            prev_cvo = []
            t_l2_bcast = {}

            def proj_mm(oc_abs, wsrc, ocol0, n_k, bank, rtok, extra=None):
                toks = []
                for th in range(2):
                    bap, wps = PS.acquire(bank[th])
                    t_last = None
                    for fc in range(n_k):
                        w0 = (wps + [rtok, t_xn_all]) if fc == 0 else None
                        t_last = mm(psum_t[bank[th]][:, 0:512],
                                    wbuf[:, (oc_abs % 2) * 2048 + fc * 128:
                                         (oc_abs % 2) * 2048 + (fc + 1) * 128],
                                    xn[:, 0:1][:, 0:0] if False else
                                    xn[:, 0:NH * TPC].rearrange("p (c t) -> p c t", c=NH)[:, fc, th * 512:(th + 1) * 512],
                                    start=(fc == 0), stop=(fc == n_k - 1), waits=w0)
                    PS.wrote(bank[th], t_last)
                    toks.append(t_last)
                return toks

            for oc in range(NCV):
                slot = oc % 2
                t_wl = wload(wbuf, slot, wqkv_bf, oc * 128,
                             [wring_consumer[slot], awt["qkv"]])
                bks = [mm_bank_rot[0], mm_bank_rot[1]]
                tmm = proj_mm(oc, wqkv_bf, oc * 128, NH, bks, t_wl)
                wring_consumer[slot] = tmm[-1]
                t_ev = []
                for th in range(2):
                    te = vts(mixc[:, th * 512:(th + 1) * 512], psum_t[bks[th]][:, 0:512],
                             sm[:, SM_BQKV + oc:SM_BQKV + oc + 1], None, ALU.add,
                             waits=PS.read_waits(bks[th]) + prev_mix_readers +
                             ([t_smalls] if oc == 0 else []))
                    PS.read_done(bks[th], te)
                    t_ev.append(te)
                t_eh = vcopy(edge[:, oc * 4:oc * 4 + 4], mixc[:, 0:4], waits=[t_ev[0]])
                t_et = vcopy(edge[:, NCV * 4 + oc * 3:NCV * 4 + oc * 3 + 3],
                             mixc[:, 1021:1024], waits=[t_ev[1]])
                t_edge_saved.append(t_et)
                wc = SM_CONVW + oc * 4
                t_c = vts(cvo[:, 0:1024], mixc[:, 0:1024], sm[:, wc + 3:wc + 4],
                          sm[:, SM_CONVB + oc:SM_CONVB + oc + 1], ALU.mult, ALU.add,
                          waits=[t_ev[0], t_ev[1]] + prev_cvo)
                t_c = vstt(cvo2[:, 1:1024], mixc[:, 0:1023], sm[:, wc + 2:wc + 3],
                           cvo[:, 1:1024], waits=[t_c])
                vsem()
                t_c = vstt(cvo[:, 2:1024], mixc[:, 0:1022], sm[:, wc + 1:wc + 2],
                           cvo2[:, 2:1024], waits=[t_c])
                vsem()
                t_c = vstt(cvo2[:, 3:1024], mixc[:, 0:1021], sm[:, wc:wc + 1],
                           cvo[:, 3:1024], waits=[t_c])
                prev_mix_readers = [t_c, t_eh, t_et]
                t_z0 = S.emit("scalar", lambda h, sems: h.memzero(cvo2[:, 0:3]),
                              waits=[t_c], inc="SC")
                t_si = act(cvo2[:, 3:1024], cvo2[:, 3:1024], AF.Silu, waits=[t_c])
                qsl = qkv_ap[:, oc * TPC:oc * TPC + TPC]
                if oc >= 16:
                    t_q = vcopy(qsl, cvo2[:, 0:1024], waits=[t_si])
                    prev_cvo = [t_q]
                    t_conv_done.append(t_q)
                else:
                    t_q = None
                    for th in range(2):
                        t_s1 = act(cvo[:, th * 512:(th + 1) * 512],
                                   cvo2[:, th * 512:(th + 1) * 512], AF.Square,
                                   waits=[t_si, prev_sql])
                        bk = 6
                        bap, wps = PS.acquire(bk)
                        t_ssq = mm(psum_t[bk][0:1, 0:512], cst["ones_col"][:, :],
                                   cvo[:, th * 512:(th + 1) * 512], waits=wps + [t_s1])
                        PS.wrote(bk, t_ssq)
                        t_sd2 = act(srl[0:1, TPC:TPC + 512],
                                    psum_t[bk][0:1, 0:512], AF.Sqrt, bias=float(L2_EPS),
                                    waits=PS.read_waits(bk))
                        PS.read_done(bk, t_sd2)
                        t_rc = vop(lambda h, sems, th=th: h.reciprocal(
                            out=srl[0:1, TPC:TPC + 512],
                            in_=srl[0:1, TPC:TPC + 512]),
                            waits=[t_sd2])
                        if oc < 8:
                            t_rc = vts(srl[0:1, TPC:TPC + 512],
                                       srl[0:1, TPC:TPC + 512],
                                       DK_SCALE, waits=[t_rc])
                        bk2 = 3
                        bap2, wps2 = PS.acquire(bk2)
                        t_bc = mm(psum_t[bk2][:, 0:512], cst["ones_row"][:, :],
                                  srl[0:1, TPC:TPC + 512],
                                  waits=wps2 + [t_rc])
                        PS.wrote(bk2, t_bc)
                        t_q = vtt(qsl[:, th * 512:(th + 1) * 512],
                                  cvo2[:, th * 512:(th + 1) * 512], psum_t[bk2][:, 0:512],
                                  op=ALU.mult, waits=PS.read_waits(bk2))
                        PS.read_done(bk2, t_q)
                        prev_sql = t_s1
                    prev_cvo = [t_q]
                    t_conv_done.append(t_q)

            t_hst = dma("gpsimd", halo_in.transpose([1, 0, 2]),
                        edge[:, NCV * 4:NCV * 4 + NCV * 3].rearrange("p (c t) -> p c t", c=NCV),
                        "halost", waits=t_edge_saved)
            t_hag = coll("AllGather", ALU.bypass, [halo_in], [halo_out], [t_hst])
            bk = 6
            bap, wps = PS.acquire(bk)
            t_ohr = mm(psum_t[bk][0:1, 0:128], sm[:, SM_OH_PREV:SM_OH_PREV + 1],
                       cst["ident_f32"][:, :], waits=wps + [t_smalls])
            PS.wrote(bk, t_ohr)
            t_ohrv = vcopy(misc[0:1, 32:40], psum_t[bk][0:1, 0:8], waits=PS.read_waits(bk))
            PS.read_done(bk, t_ohrv)
            bap, wps = PS.acquire(bk)
            t_ohb = mm(psum_t[bk][:, 0:8], cst["ones_row"][:, :], misc[0:1, 32:40],
                       waits=wps + [t_ohrv])
            PS.wrote(bk, t_ohb)
            t_ohbv = vcopy(misc[:, 32:40], psum_t[bk][:, 0:8], waits=PS.read_waits(bk))
            PS.read_done(bk, t_ohbv)
            t_hz = vop(lambda h, sems: h.memset(hsel[:, 0:NCV * 3], 0.0), waits=[t_ohbv])
            t_hldp = [None, None]
            for j in range(NCORES):
                hs = hblk[:, (j % 2) * NCV * 3:(j % 2 + 1) * NCV * 3]
                t_hl = dma("sync", hs.rearrange("p (c t) -> p c t", c=NCV),
                           halo_out[j * NCV:(j + 1) * NCV, :, :].transpose([1, 0, 2]),
                           f"hld{j % 2}", waits=[t_hag, t_hldp[j % 2]])
                t_hz = vstt(hsel[:, 0:NCV * 3], hs, misc[:, 32 + j:33 + j],
                            hsel[:, 0:NCV * 3], waits=[t_hl, t_hz])
                t_hldp[j % 2] = t_hz
            t_hsel = t_hz
            t_patch = []
            PH = NCV * 3
            for oc in range(NCV):
                wc = SM_CONVW + oc * 4
                t_p0 = vcopy(hsel[:, PH:PH + 3], hsel[:, oc * 3:oc * 3 + 3],
                             waits=[t_hsel] + t_patch[-1:])
                t_p0 = vcopy(hsel[:, PH + 3:PH + 7], edge[:, oc * 4:oc * 4 + 4],
                             waits=[t_p0])
                pad = hsel[:, PH:PH + 7]
                yA = misc[:, 40:44]
                yB = misc[:, 44:48]
                t_y = vts(yA, pad[:, 3:7], sm[:, wc + 3:wc + 4],
                          sm[:, SM_CONVB + oc:SM_CONVB + oc + 1], ALU.mult, ALU.add,
                          waits=[t_p0])
                t_y = vstt(yB, pad[:, 2:6], sm[:, wc + 2:wc + 3], yA, waits=[t_y])
                t_y = vstt(yA, pad[:, 1:5], sm[:, wc + 1:wc + 2], yB, waits=[t_y])
                t_y = vstt(yB, pad[:, 0:4], sm[:, wc:wc + 1], yA, waits=[t_y])
                t_y = act(misc[:, 48:52], yB, AF.Silu, waits=[t_y])
                qsl = qkv_ap[:, oc * TPC:oc * TPC + 4]
                if oc >= 16:
                    t_w = vcopy(qsl, misc[:, 48:52], waits=[t_y, t_conv_done[oc]])
                else:
                    t_s1 = act(misc[:, 52:56], misc[:, 48:52], AF.Square, waits=[t_y])
                    bap, wps = PS.acquire(bk)
                    t_ssq = mm(psum_t[bk][0:1, 0:4], cst["ones_col"][:, :], misc[:, 52:56],
                               waits=wps + [t_s1])
                    PS.wrote(bk, t_ssq)
                    t_sd2 = act(misc[0:1, 56:60], psum_t[bk][0:1, 0:4], AF.Sqrt,
                                bias=float(L2_EPS), waits=PS.read_waits(bk))
                    PS.read_done(bk, t_sd2)
                    t_rc = vop(lambda h, sems: h.reciprocal(
                        out=misc[0:1, 56:60], in_=misc[0:1, 56:60]), waits=[t_sd2])
                    if oc < 8:
                        t_rc = vts(misc[0:1, 56:60], misc[0:1, 56:60], DK_SCALE,
                                   waits=[t_rc])
                    bap, wps = PS.acquire(bk)
                    t_bc = mm(psum_t[bk][:, 0:4], cst["ones_row"][:, :], misc[0:1, 56:60],
                              waits=wps + [t_rc])
                    PS.wrote(bk, t_bc)
                    t_w = vtt(qsl, misc[:, 48:52], psum_t[bk][:, 0:4], op=ALU.mult,
                              waits=PS.read_waits(bk) + [t_conv_done[oc]])
                    PS.read_done(bk, t_w)
                t_patch.append(t_w)
            t_qkv_done = t_patch[-1]
            if tp_qkv is not None:
                dma("gpsimd", tp_qkv, qkv_ap[:, 0:NCV * TPC], "dbg",
                    waits=[t_qkv_done] + t_conv_done)

            for oc in range(NH):
                slot = oc % 2
                t_wl = wload(wbuf, slot, wz_bf, oc * 128,
                             [wring_consumer[slot], awt["z"]])
                bks = [mm_bank_rot[0], mm_bank_rot[1]]
                tmm = proj_mm(1000 + oc, wz_bf, oc * 128, NH, bks, t_wl)
                wring_consumer[slot] = tmm[-1]
                tze = []
                for th in range(2):
                    te = vts(zrel[:, th * 512:(th + 1) * 512],
                             psum_t[bks[th]][:, 0:512],
                             sm[:, SM_BZ + oc:SM_BZ + oc + 1], None, ALU.add,
                             waits=PS.read_waits(bks[th]) +
                             ([t_zstore[0]] if t_zstore[0] else []) +
                             ([t_zrel_free] if oc == 0 else []))
                    PS.read_done(bks[th], te)
                    tze.append(te)
                t_zstore[0] = dma("gpsimd", zsc[oc, :, :], zrel[:, 0:1024], "ST",
                                  waits=tze)
            t_z_done = [t for t in t_zstore if t is not None]
            if tp_z is not None:
                dma("gpsimd", tp_z, zsc, "dbg", waits=t_z_done)

            gb = pf[0:16, :]
            t_ba = {}
            for which, col0 in (("b", 0), ("a", 16)):
                slot = 0
                t_wl = dma("sync", wbuf[:, 0:NH * 16].rearrange("p (fc c) -> p fc c", fc=NH),
                           wba_bf.rearrange("(fc p) o -> p fc o", p=128)[:, :, col0:col0 + 16],
                           "LD", waits=[wring_consumer[0], awt["ba"]])
                bks = [mm_bank_rot[0], mm_bank_rot[1]]
                toks = []
                for th in range(2):
                    bap, wps = PS.acquire(bks[th])
                    t_last = None
                    for fc in range(NH):
                        w0 = (wps + [t_wl, t_xn_all]) if fc == 0 else None
                        t_last = mm(psum_t[bks[th]][0:16, 0:512],
                                    wbuf[:, fc * 16:(fc + 1) * 16],
                                    xn[:, 0:NH * TPC].rearrange("p (c t) -> p c t", c=NH)
                                    [:, fc, th * 512:(th + 1) * 512],
                                    start=(fc == 0), stop=(fc == NH - 1), waits=w0)
                    PS.wrote(bks[th], t_last)
                    toks.append(t_last)
                wring_consumer[0] = toks[-1]
                dstc = GB_SCR if which == "b" else GB_G
                tev = []
                for th in range(2):
                    te = vcopy(gb[:, dstc + th * 512: dstc + (th + 1) * 512],
                               psum_t[bks[th]][0:16, 0:512],
                               waits=PS.read_waits(bks[th]) + t_pf_free + t_xt_free)
                    PS.read_done(bks[th], te)
                    tev.append(te)
                t_ba[which] = tev[-1]

            bbv = sm[0:16, SM_BB:SM_BB + 1]
            bav = sm[0:16, SM_BA:SM_BA + 1]
            dtv = sm[0:16, SM_DTB:SM_DTB + 1]
            braw = gb[:, GB_SCR:GB_SCR + 1024]
            araw = gb[:, GB_G:GB_G + 1024]
            t_beta = act(gb[:, GB_BETA:GB_BETA + 1024], braw, AF.Sigmoid, bias=bbv,
                         waits=[t_ba["b"]])
            t_p1 = act(gb[:, GB_B:GB_B + 1024], braw, AF.Abs, bias=bbv, waits=[t_ba["b"]])
            t_p2 = act(gb[:, GB_B:GB_B + 1024], gb[:, GB_B:GB_B + 1024], AF.Exp, scale=-1.0,
                       waits=[t_p1])
            t_p3 = act(gb[:, GB_B:GB_B + 1024], gb[:, GB_B:GB_B + 1024], AF.Ln, bias=1.0,
                       waits=[t_p2])
            t_nbb = vts(misc[0:16, 60:61], bbv, -1.0, waits=[t_smalls])
            t_p4 = act(gb[:, GB_NEGB:GB_NEGB + 1024], braw, AF.Relu, scale=-1.0,
                       bias=misc[0:16, 60:61], waits=[t_ba["b"], t_nbb])
            t_pfull = vtt(gb[:, GB_NEGB:GB_NEGB + 1024], gb[:, GB_NEGB:GB_NEGB + 1024],
                          gb[:, GB_B:GB_B + 1024], op=ALU.add, waits=[t_p3, t_p4])
            t_a1 = act(gb[:, GB_BHAT:GB_BHAT + 1024], araw, AF.Abs, bias=dtv,
                       waits=[t_ba["a"]])
            t_a2 = act(gb[:, GB_BHAT:GB_BHAT + 1024], gb[:, GB_BHAT:GB_BHAT + 1024],
                       AF.Exp, scale=-1.0, waits=[t_a1])
            t_a3 = act(gb[:, GB_BHAT:GB_BHAT + 1024], gb[:, GB_BHAT:GB_BHAT + 1024],
                       AF.Ln, bias=1.0, waits=[t_a2])
            t_a4 = act(araw, araw, AF.Relu, bias=dtv, waits=[t_a3])
            t_sp = vtt(gb[:, GB_SCR:GB_SCR + 1024], araw, gb[:, GB_BHAT:GB_BHAT + 1024],
                       op=ALU.add, waits=[t_a4, t_beta, t_p4, t_p1])
            t_eA = act(misc[0:16, 57:58], sm[0:16, SM_ALOG:SM_ALOG + 1], AF.Exp,
                       waits=[t_smalls])
            t_eA = vts(misc[0:16, 57:58], misc[0:16, 57:58], -1.0, waits=[t_eA])
            vsem()
            t_g = vts(gb[:, GB_G:GB_G + 1024], gb[:, GB_SCR:GB_SCR + 1024],
                      misc[0:16, 57:58], None, ALU.mult, waits=[t_sp, t_eA])
            vsem()
            t_b = None
            for cch in range(NCH):
                t_b = vop(lambda h, sems, c=cch: h.tensor_tensor_scan(
                    out=gb[:, GB_B + c * 128:GB_B + (c + 1) * 128],
                    data0=ones16[:, :],
                    data1=gb[:, GB_G + c * 128:GB_G + (c + 1) * 128],
                    initial=0.0, op0=ALU.mult, op1=ALU.add), waits=[t_g, t_pfull])
            t_bh = vtt(gb[:, GB_BHAT:GB_BHAT + 1024], gb[:, GB_B:GB_B + 1024],
                       gb[:, GB_NEGB:GB_NEGB + 1024], op=ALU.subtract,
                       waits=[t_b, t_a3])
            vsem()
            t_nb = vts(gb[:, GB_NEGB:GB_NEGB + 1024], gb[:, GB_B:GB_B + 1024], -1.0,
                       waits=[t_bh])
            t_rows_st = dma("gpsimd", rowsd[:, :], gb[:, 0:3072], "rowsst", waits=[t_nb])
            if tp_gb is not None:
                dma("gpsimd", tp_gb, gb[:, :], "dbg", waits=[t_nb, t_beta, t_g])
            if tp_rows is not None:
                dma("gpsimd", tp_rows, rowsd, "dbg", waits=[t_rows_st])
            t_bcl = None
            P25B = 2
            for cch in range(NCH):
                t_bcl = dma("sync", bcrow[0:1, cch * 16:(cch + 1) * 16],
                            rowsd[:, cch * 128 + 127:cch * 128 + 128].transpose([1, 0]),
                            "bcld", waits=[t_rows_st], slow=True)
            t_sc_tiles = []
            for cch in range(NCH):
                bap, wps = PS.acquire(P25B)
                t_t1 = transp(psum_t[P25B][:, 0:16], gb[:, GB_B + cch * 128:GB_B + (cch + 1) * 128],
                              cst["ident_f32"][0:16, 0:16], waits=wps + [t_nb])
                t_t2 = transp(psum_t[P25B][:, 16:32],
                              gb[:, GB_BETA + cch * 128:GB_BETA + (cch + 1) * 128],
                              cst["ident_f32"][0:16, 0:16], waits=[t_beta])
                PS.wrote(P25B, t_t2)
                t_e1 = vcopy(bTp[:, cch * 16:(cch + 1) * 16], psum_t[P25B][:, 0:16],
                             waits=PS.read_waits(P25B))
                t_e2 = vcopy(bTp[:, 128 + cch * 16:128 + (cch + 1) * 16],
                             psum_t[P25B][:, 16:32], waits=PS.read_waits(P25B))
                PS.read_done(P25B, t_e1)
                PS.read_done(P25B, t_e2)
                t_lam = act(bTp[:, 256 + cch * 16:256 + (cch + 1) * 16],
                            bTp[:, cch * 16:(cch + 1) * 16], AF.Exp, waits=[t_e1])
                t_bl = vtt(bTp[:, 384 + cch * 16:384 + (cch + 1) * 16],
                           bTp[:, 128 + cch * 16:128 + (cch + 1) * 16],
                           bTp[:, 256 + cch * 16:256 + (cch + 1) * 16], op=ALU.mult,
                           waits=[t_lam, t_e2])
                bap, wps = PS.acquire(P25B)
                t_bcb = mm(psum_t[P25B][:, 0:16], cst["ones_row"][:, :],
                           bcrow[0:1, cch * 16:(cch + 1) * 16], waits=wps + [t_bcl])
                PS.wrote(P25B, t_bcb)
                t_lbc = act(bTp[:, 640 + cch * 16:640 + (cch + 1) * 16],
                            psum_t[P25B][:, 0:16], AF.Exp, waits=PS.read_waits(P25B))
                t_kd = vtt(misc[:, 48:64], psum_t[P25B][:, 0:16],
                           bTp[:, cch * 16:(cch + 1) * 16], op=ALU.subtract,
                           waits=PS.read_waits(P25B) + [t_e1, t_lbc])
                PS.read_done(P25B, t_lbc)
                PS.read_done(P25B, t_kd)
                t_kd = act(bTp[:, 512 + cch * 16:512 + (cch + 1) * 16],
                           misc[:, 48:64], AF.Exp, waits=[t_kd])
                t_sc_tiles.append([t_lam, t_bl, t_lbc, t_kd])
            final_toks += [t_sc_tiles[-1][3], t_sc_tiles[-1][0], t_rows_st, t_hag,
                           t_ag_g, t_ag_u, t_ag_d, t_qkv_done] + t_z_done

            if stop_after != "p2":
                t_pc_free = [t_ba["a"], t_ba["b"]]
                WRK = lambda i: pc[:, i * 128:(i + 1) * 128]
                DSTG = lambda par: pc[:, 6144 + par * 512: 6144 + (par + 1) * 512]
                MB_P = lambda h: pc[:, 7680 + h * 512: 7680 + h * 512 + 128]
                MB_PT = lambda h: pc[:, 7680 + h * 512 + 128: 7680 + h * 512 + 256]
                MB_B = lambda h: pc[:, 7680 + h * 512 + 256: 7680 + h * 512 + 384]
                S_KKL, S_KKU, S_KQU, S_KBAR, S_RK, S_RV = 0, 1, 2, 3, 4, 5
                S_A, S_AT, S_LBC, S_LI, S_T1 = 6, 7, 8, 9, 10
                S_PW = 11
                S_X = 23
                S_MT = 27
                SLOT_R = {}
                SLOT_W = {}

                def s_w(slot, extra=None):
                    w = SLOT_R.pop(slot, []) + ([SLOT_W[slot]] if slot in SLOT_W else [])
                    return w + (extra or [])

                def s_wrote(slot, tok):
                    SLOT_W[slot] = tok

                def s_r(slot):
                    return [SLOT_W[slot]] if slot in SLOT_W else []

                def s_read(slot, tok):
                    SLOT_R.setdefault(slot, []).append(tok)

                rb_cons = [None, None]
                first_wrk_wait = list(t_pc_free) + [t_qkv_done] + t_conv_done
                dstg_cons = {0: [], 1: []}
                t_pack_all = []
                hc_pairs = [(h, c) for h in range(HV) for c in range(NCH)]
                if p3_limit is not None:
                    hc_pairs = hc_pairs[:p3_limit]
                for h, c in hc_pairs:
                    qc = h // 2
                    if True:
                        par = (h * NCH + c) % 2
                        idx = c * HV + h
                        qF = qkv_ap[:, qc * TPC + c * 128: qc * TPC + (c + 1) * 128]
                        kF = qkv_ap[:, (8 + qc) * TPC + c * 128: (8 + qc) * TPC + (c + 1) * 128]
                        vF = qkv_ap[:, (16 + h) * TPC + c * 128: (16 + h) * TPC + (c + 1) * 128]
                        colat = lambda r: bTp[:, r * 128 + c * 16 + h: r * 128 + c * 16 + h + 1]
                        t_rb = dma("sync",
                                   rowbuf[0:1, par * 384:(par + 1) * 384].rearrange(
                                       "o (k t) -> o k t", k=3),
                                   rowsd[h:h + 1, :].rearrange(
                                       "o (k t) -> o k t", k=3)[:, :, c * 128:(c + 1) * 128],
                                   f"rb{par}", waits=[rb_cons[par], t_rows_st])
                        brow = rowbuf[0:1, par * 384:par * 384 + 128]
                        bhrow = rowbuf[0:1, par * 384 + 128:par * 384 + 256]
                        nbrow = rowbuf[0:1, par * 384 + 256:par * 384 + 384]
                        b0 = 0 if par == 0 else 1
                        bap, w0 = PS.acquire(b0)
                        t_mm = mm(psum_t[b0][:, 0:128], kF, kF,
                                  waits=w0 + (first_wrk_wait if idx < 2 else []) +
                                  [t_qkv_done])
                        t_mm = mm(psum_t[b0][:, 128:256], kF, qF)
                        PS.wrote(b0, t_mm)
                        bfap, wbf = PS.acquire(7)
                        t_tt = transp(psum_bf[:, 0:128], kF, cst["ident_bf16"][:, :],
                                      waits=wbf)
                        t_tt = transp(psum_bf[:, 128:256], vF, cst["ident_bf16"][:, :])
                        PS.wrote(7, t_tt)
                        if p3_steps is not None and p3_steps <= 1:
                            continue
                        b1 = 2 if par == 0 else 3
                        bap, w1 = PS.acquire(b1)
                        t_ar1 = mm(psum_t[b1][:, 0:128], bhrow, cst["ones_row"][:, :],
                                   start=True, stop=False, waits=w1 + [t_rb, t_consts])
                        t_ar1 = mm(psum_t[b1][:, 0:128], cst["ones_row"][:, :], nbrow,
                                   start=False, stop=True)
                        t_ar2 = mm(psum_t[b1][:, 128:256], nbrow, cst["ones_row"][:, :],
                                   start=True, stop=False)
                        t_ar2 = mm(psum_t[b1][:, 128:256], cst["ones_row"][:, :], bhrow,
                                   start=False, stop=True)
                        t_ar3 = mm(psum_t[b1][:, 256:384], nbrow, cst["ones_row"][:, :],
                                   start=True, stop=False)
                        t_ar3 = mm(psum_t[b1][:, 256:384], cst["ones_row"][:, :], brow,
                                   start=False, stop=True)
                        t_ar4 = mm(psum_t[b1][:, 384:512], cst["ones_row"][:, :], brow,
                                   start=True, stop=True)
                        PS.wrote(b1, t_ar4)
                        rb_cons[par] = t_ar4
                        if p3_steps is not None and p3_steps <= 2:
                            continue
                        t_cl = vts(expf[:, 0:384], psum_t[b1][:, 0:384], 0.0, None,
                                   ALU.min, waits=PS.read_waits(b1))
                        PS.read_done(b1, t_cl)
                        t_eG = act(expf[:, 384:512], psum_t[b1][:, 384:512], AF.Exp,
                                   waits=PS.read_waits(b1) + s_w(S_LBC) + [t_cl])
                        s_wrote(S_LBC, t_eG)
                        PS.read_done(b1, t_eG)
                        t_ex = act(expf[:, 0:384], expf[:, 0:384], AF.Exp, waits=[t_cl])
                        if p3_steps is not None and p3_steps <= 3:
                            PS.read_done(b1, t_eG)
                            continue
                        t_kkl = vtt(WRK(S_KKL), psum_t[b0][:, 0:128], cst["mask_ls"][:, :],
                                    waits=PS.read_waits(b0) + s_w(S_KKL))
                        s_wrote(S_KKL, t_kkl)
                        t_kku = vtt(WRK(S_KKU), psum_t[b0][:, 0:128], cst["mask_us"][:, :],
                                    waits=s_w(S_KKU))
                        s_wrote(S_KKU, t_kku)
                        t_kqu = vtt(WRK(S_KQU), psum_t[b0][:, 128:256], cst["mask_ui"][:, :],
                                    waits=s_w(S_KQU))
                        s_wrote(S_KQU, t_kqu)
                        PS.read_done(b0, t_kqu)
                        t_kb = vts(WRK(S_KBAR), psum_bf[:, 0:128], colat(4),
                                   waits=PS.read_waits(7) + s_w(S_KBAR))
                        s_wrote(S_KBAR, t_kb)
                        t_rk = vts(WRK(S_RK), psum_bf[:, 0:128], colat(3),
                                   waits=s_w(S_RK))
                        s_wrote(S_RK, t_rk)
                        t_rv = vts(WRK(S_RV), psum_bf[:, 128:256], colat(1),
                                   waits=s_w(S_RV))
                        s_wrote(S_RV, t_rv)
                        PS.read_done(7, t_rv)
                        if p3_steps is not None and p3_steps <= 4:
                            continue
                        t_A = vtt(WRK(S_A), expf[:, 0:128], WRK(S_KKL),
                                  waits=[t_ex, t_kkl] + s_w(S_A))
                        s_wrote(S_A, t_A)
                        s_read(S_KKL, t_A)
                        t_AT = vtt(WRK(S_AT), expf[:, 128:256], WRK(S_KKU),
                                   waits=[t_kku] + s_w(S_AT))
                        s_wrote(S_AT, t_AT)
                        s_read(S_KKU, t_AT)
                        t_MT = vtt(WRK(S_MT), expf[:, 256:384], WRK(S_KQU),
                                   waits=[t_kqu] + s_w(S_MT))
                        s_wrote(S_MT, t_MT)
                        s_read(S_KQU, t_MT)
                        if p3_steps is not None and p3_steps <= 5:
                            continue
                        prevA, prevAT = S_A, S_AT
                        sqb = [4, 5]
                        for lvl in range(5):
                            bq = sqb[lvl % 2]
                            bap, wq = PS.acquire(bq)
                            t_s1m = mm(psum_t[bq][:, 0:128], WRK(prevAT), WRK(prevA),
                                       waits=wq + s_r(prevA) + s_r(prevAT))
                            t_s2m = mm(psum_t[bq][:, 128:256], WRK(prevA), WRK(prevAT))
                            s_read(prevA, t_s2m)
                            s_read(prevAT, t_s2m)
                            PS.wrote(bq, t_s2m)
                            sA, sAT = S_PW + 2 * lvl, S_PW + 2 * lvl + 1
                            t_e1 = vcopy(WRK(sA), psum_t[bq][:, 0:128],
                                         waits=PS.read_waits(bq) + s_w(sA))
                            s_wrote(sA, t_e1)
                            t_e2 = vcopy(WRK(sAT), psum_t[bq][:, 128:256],
                                         waits=s_w(sAT))
                            s_wrote(sAT, t_e2)
                            PS.read_done(bq, t_e2)
                            prevA, prevAT = sA, sAT
                        bq = sqb[1]
                        bap, wq = PS.acquire(bq)
                        t_s1m = mm(psum_t[bq][:, 0:128], WRK(prevA), WRK(prevAT),
                                   waits=wq + s_r(prevA) + s_r(prevAT))
                        s_read(prevA, t_s1m)
                        s_read(prevAT, t_s1m)
                        PS.wrote(bq, t_s1m)
                        t_a64 = vcopy(WRK(S_PW + 10), psum_t[bq][:, 0:128],
                                      waits=PS.read_waits(bq) + s_w(S_PW + 10))
                        s_wrote(S_PW + 10, t_a64)
                        PS.read_done(bq, t_a64)
                        if p3_steps is not None and p3_steps <= 6:
                            continue
                        xs = [S_X, S_X + 2]
                        t_x1 = vcopy(WRK(xs[0]), WRK(S_RK), waits=[t_rk] + s_w(xs[0]))
                        s_wrote(xs[0], t_x1)
                        s_read(S_RK, t_x1)
                        t_x2 = vcopy(WRK(xs[0] + 1), WRK(S_RV),
                                     waits=[t_rv] + s_w(xs[0] + 1))
                        s_wrote(xs[0] + 1, t_x2)
                        s_read(S_RV, t_x2)
                        cur = 0
                        apb = [6, 2]
                        fslots = [S_PW + 10, S_PW + 9, S_PW + 7, S_PW + 5,
                                  S_PW + 3, S_PW + 1, S_AT]
                        for fi, fsl in enumerate(fslots):
                            bq = apb[fi % 2]
                            bap, wq = PS.acquire(bq)
                            Xc = pc[:, xs[cur] * 128:(xs[cur] + 2) * 128]
                            t_ap = mm(psum_t[bq][:, 0:256], WRK(fsl), Xc,
                                      waits=wq + s_r(fsl) + s_r(xs[cur]) +
                                      s_r(xs[cur] + 1))
                            s_read(fsl, t_ap)
                            s_read(xs[cur], t_ap)
                            s_read(xs[cur] + 1, t_ap)
                            PS.wrote(bq, t_ap)
                            Xn = pc[:, xs[1 - cur] * 128:(xs[1 - cur] + 2) * 128]
                            ww = (PS.read_waits(bq) + s_w(xs[1 - cur]) +
                                  s_w(xs[1 - cur] + 1))
                            if fi == 6:
                                t_xe = vstt(Xn, psum_t[bq][:, 0:256], -1.0, Xc,
                                            op0=ALU.mult, op1=ALU.add, waits=ww)
                            else:
                                t_xe = vtt(Xn, Xc, psum_t[bq][:, 0:256], op=ALU.add,
                                           waits=ww)
                            s_wrote(xs[1 - cur], t_xe)
                            s_wrote(xs[1 - cur] + 1, t_xe)
                            s_read(xs[cur], t_xe)
                            s_read(xs[cur] + 1, t_xe)
                            PS.read_done(bq, t_xe)
                            cur = 1 - cur
                        sW, sU = xs[cur], xs[cur] + 1
                        if p3_steps is not None and p3_steps <= 7:
                            continue
                        b3 = 4 if par == 0 else 5
                        bap, w3 = PS.acquire(b3)
                        t_oB = mm(psum_t[b3][:, 0:128], WRK(S_KBAR), WRK(sU),
                                  waits=w3 + s_r(S_KBAR) + s_r(sU) + s_r(sW))
                        t_oW = mm(psum_t[b3][:, 128:256], WRK(sW), WRK(S_KBAR))
                        t_oG = mm(psum_t[b3][:, 256:384], WRK(sW), WRK(S_MT))
                        t_oO = mm(psum_t[b3][:, 384:512], WRK(sU), WRK(S_MT))
                        s_read(S_KBAR, t_oW)
                        s_read(S_MT, t_oO)
                        s_read(sW, t_oG)
                        s_read(sU, t_oO)
                        PS.wrote(b3, t_oO)
                        t_li = vts(WRK(S_LI), cst["ident_f32"][:, :], colat(5),
                                   waits=[t_sc_tiles[c][2]] + s_w(S_LI))
                        s_wrote(S_LI, t_li)
                        dcw = dstg_cons[par]
                        t_p1k = vcopy(DSTG(par)[:, 256:384], psum_t[b3][:, 0:128],
                                      waits=PS.read_waits(b3) + dcw)
                        t_p2k = vstt(DSTG(par)[:, 128:256], psum_t[b3][:, 128:256],
                                     -1.0, WRK(S_LI), op0=ALU.mult, op1=ALU.add,
                                     waits=[t_li])
                        s_read(S_LI, t_p2k)
                        t_t1 = vtt(WRK(S_T1), expf[:, 384:512], qF,
                                   waits=[t_eG] + s_w(S_T1))
                        s_wrote(S_T1, t_t1)
                        s_read(S_LBC, t_t1)
                        t_p3k = vstt(DSTG(par)[:, 0:128], psum_t[b3][:, 256:384],
                                     -1.0, WRK(S_T1), op0=ALU.mult, op1=ALU.add)
                        s_read(S_T1, t_p3k)
                        t_p4k = vcopy(DSTG(par)[:, 384:512], psum_t[b3][:, 384:512])
                        PS.read_done(b3, t_p4k)
                        if p3_steps is not None and p3_steps <= 8:
                            PS.read_done(b3, t_p4k)
                            dstg_cons[par] = [t_p4k]
                            continue
                        t_pk = dma("gpsimd", dpk[idx, :, :], DSTG(par),
                                   f"pk{par}", waits=[t_p4k])
                        t_pack_all.append(t_pk)
                        MTsl = DSTG(par)[:, 128:256]
                        if c == 0:
                            bap, wq = PS.acquire(7)
                            t_tp0 = transp(psum_bf[:, 256:384], MTsl,
                                           cst["ident_bf16"][:, :],
                                           waits=wq + [t_p2k])
                            PS.wrote(7, t_tp0)
                            t_cmp = vcopy(MB_P(h), psum_bf[:, 256:384],
                                          waits=PS.read_waits(7))
                            PS.read_done(7, t_cmp)
                            t_cb = vcopy(MB_B(h), DSTG(par)[:, 256:384], waits=[t_p1k])
                            comp_tok = [t_cmp, t_cb, t_tp0]
                        else:
                            bq = 6
                            bap, wq = PS.acquire(bq)
                            t_c1 = mm(psum_t[bq][:, 0:128], MTsl, MB_P(h),
                                      waits=wq + comp_tok + [t_p2k])
                            t_c2 = mm(psum_t[bq][:, 128:256], MTsl, MB_B(h))
                            PS.wrote(bq, t_c2)
                            t_cmp = vcopy(MB_P(h), psum_t[bq][:, 0:128],
                                          waits=PS.read_waits(bq))
                            t_cb = vtt(MB_B(h), psum_t[bq][:, 128:256],
                                       DSTG(par)[:, 256:384], op=ALU.add,
                                       waits=[t_p1k])
                            PS.read_done(bq, t_cb)
                            comp_tok = [t_cmp, t_cb, t_c2]
                        dstg_cons[par] = [t_pk, t_cb] + ([t_c2] if c > 0 else [t_tp0])
                        if c == NCH - 1:
                            bap, wq = PS.acquire(7)
                            t_tpf = transp(psum_bf[:, 256:384], MB_P(h),
                                           cst["ident_bf16"][:, :],
                                           waits=wq + comp_tok)
                            PS.wrote(7, t_tpf)
                            t_ptf = vcopy(MB_PT(h), psum_bf[:, 256:384],
                                          waits=PS.read_waits(7))
                            PS.read_done(7, t_ptf)
                            mb_done_h = [t_ptf, t_cb]
                if p3_limit is not None:
                    final_toks += t_pack_all[-2:]
                    stop_after = "p3cut"
                mbsrc = None if p3_limit is not None else pc[:, 7680:7680 + HV * 512].rearrange(
                    "i (hh w j) -> i hh w j", hh=HV, w=4)
                if p3_limit is None:
                    mbdst = mb_in.rearrange("(hh w i) j -> i hh w j", hh=HV, w=2, i=128)
                    t_mbst = dma("gpsimd", mbdst[:, :, 0, :], mbsrc[:, :, 1, :],
                                 "mbst", waits=mb_done_h)
                    t_mbst = dma("gpsimd", mbdst[:, :, 1, :], mbsrc[:, :, 2, :],
                                 "mbst", waits=mb_done_h)
                    t_ag_mb = coll("AllGather", ALU.bypass, [mb_in], [mb_out], [t_mbst])
                    final_toks += [t_ag_mb] + t_pack_all[-2:]
                    if tp_dpk is not None:
                        dma("gpsimd", tp_dpk, dpk, "dbg", waits=t_pack_all[-2:])
                    if tp_mb is not None:
                        dma("gpsimd", tp_mb, mb_in, "dbg", waits=[t_mbst])

        did_full = False
        if stop_after is None:
            stream_tok = lambda s: Tok(S.stream_map[s], S.cnt[S.stream_map[s]])
            pk_fin = [stream_tok("pk0"), stream_tok("pk1")]
            pbxq = pb[:, 0:NH * TPC].rearrange("p (c t) -> p c t", c=NH)
            t_xql = dma("sync", pb[:, 0:NH * TPC].rearrange("p (c t) -> p c t", c=NH),
                        xqh.transpose([1, 0, 2]), "xqld",
                        waits=pk_fin + [t_xq_store[-1]])
            SG, SU, SD = 28, 29, 30
            wr_cons = [wring_consumer[0], wring_consumer[1]]
            wr2_cons = [None, None]
            s1_stage = [relo, zrel]
            s1_cons = [None, t_zstore[0]]
            prev_tin = [None, None]
            prev_s1 = [None, None]
            gb_banks = (0, 1)
            ub_banks = (2, 3)
            s1b = (4, 5)
            t_s1_last = None
            for s in range(NFI):
                sl = s % 2
                t_wg = wload(wbuf, sl, agg_out, s * 128, [wr_cons[sl], t_ag_g], nk=16)
                t_wu = wload(wbuf2, sl, agu_out, s * 128, [wr2_cons[sl], t_ag_u], nk=16)
                stg = s1_stage[s % 2]
                t_evs = []
                for th in range(2):
                    bg, bu = gb_banks[th], ub_banks[th]
                    bap, wg_ = PS.acquire(bg)
                    t_lg = None
                    for fc in range(NH):
                        w0 = (wg_ + [t_wg] + ([t_xql] if s == 0 else [])) \
                            if fc == 0 else None
                        t_lg = mm(psum_t[bg][:, 0:512],
                                  wbuf[:, sl * 2048 + fc * 128:sl * 2048 + (fc + 1) * 128],
                                  pbxq[:, fc, th * 512:(th + 1) * 512],
                                  start=(fc == 0), stop=(fc == NH - 1), waits=w0)
                    PS.wrote(bg, t_lg)
                    bap, wu_ = PS.acquire(bu)
                    t_lu = None
                    for fc in range(NH):
                        w0 = (wu_ + [t_wu] + ([t_xql] if s == 0 else [])) \
                            if fc == 0 else None
                        t_lu = mm(psum_t[bu][:, 0:512],
                                  wbuf2[:, sl * 2048 + fc * 128:sl * 2048 + (fc + 1) * 128],
                                  pbxq[:, fc, th * 512:(th + 1) * 512],
                                  start=(fc == 0), stop=(fc == NH - 1), waits=w0)
                    PS.wrote(bu, t_lu)
                    wr_cons[sl] = t_lg
                    wr2_cons[sl] = t_lu
                    t_sg_ = act(tmA[:, th * 512:(th + 1) * 512], psum_t[bg][:, 0:512],
                                AF.Silu, scale=misc[:, SG:SG + 1],
                                waits=PS.read_waits(bg) + [t_sgv, prev_tin[th]])
                    PS.read_done(bg, t_sg_)
                    t_up = vts(tmB[:, th * 512:(th + 1) * 512], psum_t[bu][:, 0:512],
                               misc[:, SU:SU + 1], None, ALU.mult,
                               waits=PS.read_waits(bu) + [t_sgv])
                    PS.read_done(bu, t_up)
                    t_in = vtt(intm[:, th * 512:(th + 1) * 512],
                               tmA[:, th * 512:(th + 1) * 512],
                               tmB[:, th * 512:(th + 1) * 512], op=ALU.mult,
                               waits=[t_sg_, prev_s1[th]])
                    prev_tin[th] = t_in
                    bap, ws1 = PS.acquire(s1b[th])
                    t_s1 = mm(psum_t[s1b[th]][:, 0:512], cst["h128_bf"][:, :],
                              intm[:, th * 512:(th + 1) * 512],
                              waits=ws1 + [t_in, t_consts])
                    PS.wrote(s1b[th], t_s1)
                    prev_s1[th] = t_s1
                    t_s1_last = t_s1
                    t_ev = vcopy(stg[:, th * 512:(th + 1) * 512],
                                 psum_t[s1b[th]][:, 0:512],
                                 waits=PS.read_waits(s1b[th]) +
                                 ([s1_cons[s % 2]] if th == 0 else []))
                    t_lo = vstt(yq_t[:, th * 512:(th + 1) * 512],
                                stg[:, th * 512:(th + 1) * 512], -1.0,
                                psum_t[s1b[th]][:, 0:512], op0=ALU.mult,
                                op1=ALU.add,
                                waits=[t_ev, s1_cons[s % 2]] if th == 0 else [t_ev])
                    PS.read_done(s1b[th], t_lo)
                    t_evs.append(t_lo)
                t_zs = dma("gpsimd", zh[s, :, :], stg[:, 0:1024], "zst",
                           waits=t_evs)
                s1_cons[s % 2] = dma("gpsimd", zhl[s, :, :], yq_t[:, 0:1024], "zst",
                                     waits=t_evs)
            if tp_zh is not None:
                dma("gpsimd", tp_zh, zh, "dbg", waits=[s1_cons[0], s1_cons[1]])

            zst_fin = stream_tok("zst")
            gb_dead = [t_rows_st] + [t for sub in t_sc_tiles for t in sub]
            s2_cons = [s1_cons[0], s1_cons[1]]
            t_yq = [None] * NFI
            dwr_cons = [wr_cons[0], wr_cons[1]]
            ff_cons = [None, None]
            t_quantv_last = None
            t_trp_last = None
            t_ff_last = None
            srl_cons = None
            agdv = agd_out.rearrange("(s g i e) m -> e g s i m", s=64, g=4, i=16)
            t_dn_last = None
            for th in range(2):
                t_mx = None
                for ap in range(NFI):
                    sl = ap % 2
                    stg = s1_stage[sl]
                    wz = [s2_cons[sl]] + ([zst_fin] if ap < 2 else [])
                    t_l0 = dma("sync",
                               stg[0:64, 0:512].rearrange("p (x t) -> p x t", x=1),
                               zh[0:64, 2 * ap:2 * ap + 1, th * 512:(th + 1) * 512],
                               "z2a", waits=wz)
                    t_l1 = dma("sync",
                               stg[64:128, 0:512].rearrange("p (x t) -> p x t", x=1),
                               zh[0:64, 2 * ap + 1:2 * ap + 2, th * 512:(th + 1) * 512],
                               "z2b", waits=wz)
                    t_l2 = dma("sync",
                               stg[0:64, 512:1024].rearrange("p (x t) -> p x t", x=1),
                               zhl[0:64, 2 * ap:2 * ap + 1, th * 512:(th + 1) * 512],
                               "z2c", waits=wz)
                    t_l3 = dma("sync",
                               stg[64:128, 512:1024].rearrange("p (x t) -> p x t", x=1),
                               zhl[0:64, 2 * ap + 1:2 * ap + 2, th * 512:(th + 1) * 512],
                               "z2d", waits=wz)
                    bk = ap % 2
                    bap, w2 = PS.acquire(bk)
                    t_mm2 = mm(psum_t[bk][:, 0:512], cst["h64blk_bf"][:, :],
                               stg[:, 0:512], start=True, stop=False,
                               waits=w2 + [t_l0, t_l1, t_l2, t_l3])
                    t_mm2 = mm(psum_t[bk][:, 0:512], cst["h64blk_bf"][:, :],
                               stg[:, 512:1024], start=False, stop=True)
                    PS.wrote(bk, t_mm2)
                    s2_cons[sl] = t_mm2
                    t_ab = act(pf[:, 3584:4096], psum_t[bk][:, 0:512],
                               AF.Abs, waits=PS.read_waits(bk) + [t_mx] +
                               (gb_dead if (th == 0 and ap == 0) else []))
                    PS.read_done(bk, t_ab)
                    if ap == 0:
                        t_mx = vcopy(pf[:, 3072:3584], pf[:, 3584:4096],
                                     waits=[t_ab, t_trp_last])
                    else:
                        t_mx = vtt(pf[:, 3072:3584], pf[:, 3072:3584],
                                   pf[:, 3584:4096], op=ALU.max, waits=[t_ab])
                bap, wtp = PS.acquire(2)
                t_tp = None
                for b in range(4):
                    t_tp = transp(psum_t[2][:, b * 128:(b + 1) * 128],
                                  pf[:, 3072 + b * 128:3072 + (b + 1) * 128],
                                  cst["ident_f32"][:, :],
                                  waits=(wtp + [t_mx]) if b == 0 else None)
                PS.wrote(2, t_tp)
                t_trp_last = t_tp
                t_cpy = vcopy(mixc[:, 0:512], psum_t[2][:, 0:512],
                              waits=PS.read_waits(2) + ff_cons)
                PS.read_done(2, t_cpy)
                t_trs = []
                for b in range(4):
                    t_tr = tree_reduce(misc[:, 4 + b:5 + b],
                                       mixc[:, b * 128:(b + 1) * 128],
                                       cvo, cvo2, 128, op=ALU.max,
                                       waits=[t_cpy])
                    t_trs.append(t_tr)
                t_qs = dma("gpsimd", qsc[:, 0:4], misc[:, 4:8], "qss", waits=t_trs)
                t_qls = []
                for b in range(4):
                    t_qls.append(dma("sync", srl[0:1, b * 128:(b + 1) * 128],
                                     qsc[:, b:b + 1].transpose([1, 0]), "qsl",
                                     waits=[t_qs, srl_cons], slow=True))
                t_rm = vts(srl[0:1, 512:1024], srl[0:1, 0:512], float(Q_EPS), None,
                           ALU.max, waits=t_qls)
                t_rs = vop(lambda h, sems: h.reciprocal(
                    out=srl[0:1, 1024:1536], in_=srl[0:1, 512:1024]), waits=[t_rm])
                bap, wb3 = PS.acquire(3)
                t_b3 = mm(psum_t[3][:, 0:512], cst["ones_row"][:, :],
                          srl[0:1, 1024:1536], waits=wb3 + [t_rs])
                PS.wrote(3, t_b3)
                t_bcs = vts(pf[:, 0:512], psum_t[3][:, 0:512], 128.0, None, ALU.mult,
                            waits=PS.read_waits(3))
                PS.read_done(3, t_bcs)
                t_bi = vts(srl[0:1, 0:512], srl[0:1, 512:1024], INVSQ / 128.0,
                           waits=[t_bcs])
                bap, wb3 = PS.acquire(3)
                t_b3 = mm(psum_t[3][:, 0:512], cst["ones_row"][:, :],
                          srl[0:1, 0:512], waits=wb3 + [t_bi])
                PS.wrote(3, t_b3)
                t_bci = vts(pf[:, 512:1024], psum_t[3][:, 0:512], misc[:, SD:SD + 1],
                            None, ALU.mult, waits=PS.read_waits(3) + [t_sgv])
                PS.read_done(3, t_bci)
                srl_cons = t_b3
                pb_dead = [t_s1_last if th == 0 else t_dn_last]
                for ap in range(NFI):
                    pbt = pb[:, ap * 512:(ap + 1) * 512]
                    sl = ap % 2
                    stg = s1_stage[sl]
                    t_l0 = dma("sync",
                               stg[0:64, 0:512].rearrange("p (x t) -> p x t", x=1),
                               zh[0:64, 2 * ap:2 * ap + 1, th * 512:(th + 1) * 512],
                               "z2a", waits=[s2_cons[sl]])
                    t_l1 = dma("sync",
                               stg[64:128, 0:512].rearrange("p (x t) -> p x t", x=1),
                               zh[0:64, 2 * ap + 1:2 * ap + 2, th * 512:(th + 1) * 512],
                               "z2b")
                    t_l2 = dma("sync",
                               stg[0:64, 512:1024].rearrange("p (x t) -> p x t", x=1),
                               zhl[0:64, 2 * ap:2 * ap + 1, th * 512:(th + 1) * 512],
                               "z2c", waits=[s2_cons[sl]])
                    t_l3 = dma("sync",
                               stg[64:128, 512:1024].rearrange("p (x t) -> p x t", x=1),
                               zhl[0:64, 2 * ap + 1:2 * ap + 2, th * 512:(th + 1) * 512],
                               "z2d")
                    bk = ap % 2
                    bap, w2 = PS.acquire(bk)
                    t_mm2 = mm(psum_t[bk][:, 0:512], cst["h64blk_bf"][:, :],
                               stg[:, 0:512], start=True, stop=False,
                               waits=w2 + [t_l0, t_l1, t_l2, t_l3])
                    t_mm2 = mm(psum_t[bk][:, 0:512], cst["h64blk_bf"][:, :],
                               stg[:, 512:1024], start=False, stop=True)
                    PS.wrote(bk, t_mm2)
                    s2_cons[sl] = t_mm2
                    t_v0 = vtt(tmA[:, 0:512], psum_t[bk][:, 0:512], pf[:, 0:512],
                               op=ALU.mult, waits=PS.read_waits(bk) + [t_bcs])
                    PS.read_done(bk, t_v0)
                    t_v1 = vts(tmA[:, 512:1024], tmA[:, 0:512], 127.0, 128.0,
                               ALU.min, ALU.add, waits=[t_v0])
                    t_iI = vcopy(tmI[:, 0:512], tmA[:, 512:1024], waits=[t_v1])
                    t_f = vcopy(tmB[:, 0:512], tmI[:, 0:512],
                                waits=[t_iI, t_yq[ap - 1] if ap else None])
                    t_d = vtt(tmA[:, 0:512], tmA[:, 512:1024], tmB[:, 0:512],
                              op=ALU.subtract, waits=[t_f])
                    t_c = vts(tmB[:, 512:1024], tmA[:, 0:512], 0.5, None, ALU.is_ge,
                              waits=[t_d])
                    t_r = vtt(tmB[:, 0:512], tmB[:, 0:512], tmB[:, 512:1024],
                              op=ALU.add, waits=[t_c])
                    t_quantv_last = t_r
                    t_yq[ap] = act(pbt, tmB[:, 0:512], AF.Copy, bias=-128.0,
                                   waits=[t_r] + pb_dead)
                dnb = (4, 5)
                for m in range(NH):
                    bk = dnb[m % 2]
                    bap, wd = PS.acquire(bk)
                    t_lm = None
                    for g in range(4):
                        sl = (m * 4 + g) % 2
                        t_w1 = dma("sync",
                                   wbuf[0:64, sl * 2048:(sl + 1) * 2048]
                                   .rearrange("p (i c) -> p i c", i=16),
                                   agdv[0, g, :, :, m * 128:(m + 1) * 128], "LD",
                                   waits=[dwr_cons[sl], t_ag_d])
                        t_w2 = dma("sync",
                                   wbuf[64:128, sl * 2048:(sl + 1) * 2048]
                                   .rearrange("p (i c) -> p i c", i=16),
                                   agdv[1, g, :, :, m * 128:(m + 1) * 128], "LD")
                        for i in range(16):
                            ap = g * 16 + i
                            w0 = [t_w1, t_w2, t_yq[ap]]
                            if g == 0 and i == 0:
                                w0 += wd
                            t_lm = mm(psum_t[bk][:, 0:512],
                                      wbuf[:, sl * 2048 + i * 128:sl * 2048 + (i + 1) * 128],
                                      pb[:, ap * 512:(ap + 1) * 512],
                                      start=(g == 0 and i == 0),
                                      stop=(g == 3 and i == 15), waits=w0)
                        dwr_cons[sl] = t_lm
                    PS.wrote(bk, t_lm)
                    fstg = [cvo, cvo2][m % 2]
                    t_ff = vtt(fstg[:, 0:512], psum_t[bk][:, 0:512], pf[:, 512:1024],
                               op=ALU.mult, waits=PS.read_waits(bk) + [ff_cons[m % 2]])
                    PS.read_done(bk, t_ff)
                    t_ff_last = t_ff
                    t_dn_last = t_lm
                    ff_cons[m % 2] = dma("gpsimd", ffnh[m, :, th * 512:(th + 1) * 512],
                                         fstg[:, 0:512], "fst", waits=[t_ff])
            fst_fin = stream_tok("fst")
            t_dbg_ffn = None
            if tp_ffn is not None:
                t_dbg_ffn = dma("gpsimd", tp_ffn, ffnh, "dbg", waits=[fst_fin])

            xqst_fin = stream_tok("xqst")
            zst_fin2 = stream_tok("zst")
            t_ohi = []
            for j in range(1, 8):
                t_ohi.append(vts(intm[:, (j - 1) * 128:j * 128],
                                 cst["ident_bf16"][:, :],
                                 sm[:, SM_OHB + j:SM_OHB + j + 1], None, ALU.mult,
                                 waits=[t_smalls, t_consts, t_s1_last]))
            mb2_cons = [None, None]
            rr_cons = [zst_fin2, zst_fin2]
            t_s0 = [None] * HV
            for hh in range(HV):
                t_rl = dma("sync", yq_t[:, 0:128],
                           mb_out[hh * 256 + 128:hh * 256 + 256, :], "mbl",
                           waits=[t_ag_mb, rr_cons[0]])
                bap, w6 = PS.acquire(6)
                t_sel = mm(psum_t[6][:, 0:128], intm[:, 0:128], yq_t[:, 0:128],
                           start=True, stop=False,
                           waits=w6 + [t_ohi[-1], t_rl])
                rcur = 0
                for j in range(1, 7):
                    sl = j % 2
                    t_pl = dma("sync", wbuf2[:, sl * 2048:sl * 2048 + 128],
                               mb_out[j * 4096 + hh * 256:j * 4096 + hh * 256 + 128, :],
                               "mbl", waits=[mb2_cons[sl]])
                    t_bl = dma("sync", wbuf2[:, sl * 2048 + 128:sl * 2048 + 256],
                               mb_out[j * 4096 + hh * 256 + 128:j * 4096 + hh * 256 + 256, :],
                               "mbl")
                    bk = 4 + (j % 2)
                    bap, wp = PS.acquire(bk)
                    t_pr = mm(psum_t[bk][:, 0:128], wbuf2[:, sl * 2048:sl * 2048 + 128],
                              yq_t[:, rcur * 128:(rcur + 1) * 128],
                              waits=wp + [t_pl])
                    PS.wrote(bk, t_pr)
                    rnew = 1 - rcur
                    t_rn = vtt(yq_t[:, rnew * 128:(rnew + 1) * 128],
                               psum_t[bk][:, 0:128],
                               wbuf2[:, sl * 2048 + 128:sl * 2048 + 256], op=ALU.add,
                               waits=PS.read_waits(bk) + [t_bl, rr_cons[rnew]])
                    PS.read_done(bk, t_rn)
                    mb2_cons[sl] = t_rn
                    t_sel = mm(psum_t[6][:, 0:128], intm[:, j * 128:(j + 1) * 128],
                               yq_t[:, rnew * 128:(rnew + 1) * 128],
                               start=False, stop=(j == 6), waits=[t_rn])
                    rr_cons[rcur] = t_sel
                    rcur = rnew
                PS.wrote(6, t_sel)
                t_s0[hh] = vcopy(s0b[:, hh * 128:(hh + 1) * 128], psum_t[6][:, 0:128],
                                 waits=PS.read_waits(6))
                PS.read_done(6, t_s0[hh])
                rr_cons[rcur] = t_s0[hh]
            coreb_dead = pk_fin + [t_mbst, t_ag_mb]
            swp_cons = [dwr_cons[0], dwr_cons[1]]
            t_core_h = {hh: [] for hh in range(HV)}
            for hh in range(HV):
                t_state = t_s0[hh]
                for c in range(NCH):
                    idx = c * HV + hh
                    sl = (hh * NCH + c) % 2
                    t_pl = dma("sync", wbuf[:, sl * 2048:sl * 2048 + 512],
                               dpk[idx, :, :], "pkl",
                               waits=[swp_cons[sl]] + (pk_fin if idx < 2 else []))
                    GT = wbuf[:, sl * 2048:sl * 2048 + 128]
                    MTt = wbuf[:, sl * 2048 + 128:sl * 2048 + 256]
                    Bt = wbuf[:, sl * 2048 + 256:sl * 2048 + 384]
                    OTt = wbuf[:, sl * 2048 + 384:sl * 2048 + 512]
                    bap, wo = PS.acquire(sl)
                    t_mo = mm(psum_t[sl][:, 0:128], s0b[:, hh * 128:(hh + 1) * 128],
                              GT, waits=wo + [t_pl, t_state])
                    PS.wrote(sl, t_mo)
                    bap, ws_ = PS.acquire(2 + sl)
                    t_ms = mm(psum_t[2 + sl][:, 0:128], MTt,
                              s0b[:, hh * 128:(hh + 1) * 128], waits=ws_)
                    PS.wrote(2 + sl, t_ms)
                    t_co = vtt(coreb[:, hh * TPC + c * 128:hh * TPC + (c + 1) * 128],
                               psum_t[sl][:, 0:128], OTt, op=ALU.add,
                               waits=PS.read_waits(sl) +
                               (coreb_dead if (hh == 0 and c == 0) else []))
                    PS.read_done(sl, t_co)
                    t_core_h[hh].append(t_co)
                    t_state = vtt(s0b[:, hh * 128:(hh + 1) * 128],
                                  psum_t[2 + sl][:, 0:128], Bt, op=ALU.add,
                                  waits=PS.read_waits(2 + sl) + [t_mo])
                    PS.read_done(2 + sl, t_state)
                    swp_cons[sl] = t_state
            if tp_core is not None:
                dma("gpsimd", tp_core, coreb[:, 0:HV * TPC], "dbg",
                    waits=[t for hh in range(HV) for t in t_core_h[hh][-1:]])

            zrl_cons = [s2_cons[0], s2_cons[1]]
            t_gated = []
            tmA_dead = [t_quantv_last, t_yq[-1]]
            cvo_dead = [t_ff_last, fst_fin]
            t_sz_prev = None
            t_vm_prev = None
            for hh in range(HV):
                ring = s1_stage[hh % 2]
                t_zl = dma("sync", ring[:, 0:1024], zsc[hh, :, :], "zrl",
                           waits=[zrl_cons[hh % 2]] +
                           (t_z_done if hh < 2 else []))
                cb_h = coreb[:, hh * TPC:(hh + 1) * TPC]
                t_sq = act(tmA[:, 0:1024], cb_h, AF.Square,
                           waits=t_core_h[hh] + [t_vm_prev] +
                           (tmA_dead if hh == 0 else []))
                t_var = None
                for th in range(2):
                    bap, w6 = PS.acquire(6)
                    t_vm = mm(psum_t[6][0:1, 0:512], cst["ones_col"][:, :],
                              tmA[:, th * 512:(th + 1) * 512],
                              waits=w6 + [t_sq])
                    PS.wrote(6, t_vm)
                    t_vm_prev = t_vm
                    t_var = act(srl[0:1, th * 512:(th + 1) * 512],
                                psum_t[6][0:1, 0:512], AF.Sqrt,
                                scale=1.0 / 128.0, bias=float(RMS_EPS),
                                waits=PS.read_waits(6))
                    PS.read_done(6, t_var)
                t_rv = vop(lambda h, sems: h.reciprocal(
                    out=srl[0:1, 0:1024], in_=srl[0:1, 0:1024]), waits=[t_var])
                for th in range(2):
                    bap, w3 = PS.acquire(3)
                    t_bc = mm(psum_t[3][:, 0:512], cst["ones_row"][:, :],
                              srl[0:1, th * 512:(th + 1) * 512], waits=w3 + [t_rv])
                    PS.wrote(3, t_bc)
                    t_g1 = vtt(tmB[:, th * 512:(th + 1) * 512],
                               cb_h[:, th * 512:(th + 1) * 512],
                               psum_t[3][:, 0:512], op=ALU.mult,
                               waits=PS.read_waits(3))
                    PS.read_done(3, t_g1)
                t_g2 = vts(tmB[:, 0:1024], tmB[:, 0:1024],
                           sm[:, SM_NORMW:SM_NORMW + 1], None, ALU.mult,
                           waits=[t_g1])
                t_sz = act(cvo[:, 0:1024], ring[:, 0:1024], AF.Silu,
                           waits=[t_zl, t_g2] + (cvo_dead if hh == 0 else []) +
                           [t_sz_prev])
                t_sz_prev = t_sz
                t_gf = vtt(cb_h, tmB[:, 0:1024], cvo[:, 0:1024], op=ALU.mult,
                           waits=[t_sz])
                zrl_cons[hh % 2] = t_gf
                t_gated.append(t_gf)
            wout_cons = [swp_cons[0], swp_cons[1]]
            frel_dead = [t_trp_last, t_ff_last, t_bci]
            frel_cons = [None, None]
            oA_cons = [None, None]
            t_fs2 = []
            for m in range(NH):
                sl = m % 2
                t_wl = wload(wbuf, sl, wout_bf, m * 128,
                             [wout_cons[sl], awt["out"]])
                t_fl = dma("sync", pf[:, PF_FREL + sl * 1024:PF_FREL + (sl + 1) * 1024],
                           ffnh[m, :, :], "frl",
                           waits=[frel_cons[sl], fst_fin, t_dbg_ffn] +
                           (frel_dead if m < 2 else []))
                oba = tmA[:, sl * 1024:(sl + 1) * 1024]
                t_e2s = []
                for th in range(2):
                    bk = 4 + th
                    bap, wk = PS.acquire(bk)
                    t_lm = None
                    for fc in range(NH):
                        w0 = (wk + [t_wl] + t_gated) if fc == 0 else None
                        t_lm = mm(psum_t[bk][:, 0:512],
                                  wbuf[:, sl * 2048 + fc * 128:sl * 2048 + (fc + 1) * 128],
                                  coreb[:, fc * TPC + th * 512:fc * TPC + (th + 1) * 512],
                                  start=(fc == 0), stop=(fc == NH - 1), waits=w0)
                    PS.wrote(bk, t_lm)
                    wout_cons[sl] = t_lm
                    t_e1 = vts(oba[:, th * 512:(th + 1) * 512], psum_t[bk][:, 0:512],
                               sm[:, SM_BOUT + m:SM_BOUT + m + 1], None, ALU.add,
                               waits=PS.read_waits(bk) + [oA_cons[sl]])
                    PS.read_done(bk, t_e1)
                    t_e2 = vtt(oba[:, th * 512:(th + 1) * 512],
                               oba[:, th * 512:(th + 1) * 512],
                               pf[:, PF_FREL + sl * 1024 + th * 512:
                                  PF_FREL + sl * 1024 + (th + 1) * 512],
                               op=ALU.add, waits=[t_fl])
                    t_e2s.append(t_e2)
                t_st = dma("gpsimd", ffnh[m, :, :], oba[:, 0:1024], "fs2",
                           waits=t_e2s)
                frel_cons[sl] = t_e2s[-1]
                oA_cons[sl] = t_st
                t_fs2.append(t_st)
            fs2_fin = stream_tok("fs2")
            xre_dead = [t_ba["a"], t_ba["b"]]
            otm_cons = None
            mx_cons = None
            t_fin = []
            for tb in range(8):
                t_xl = dma("sync", pf[:, PF_XRE:PF_XRE + 2048],
                           x_d[tb * 128:(tb + 1) * 128, :], "xrl",
                           waits=(xre_dead if tb == 0 else []) + [otm_cons])
                t_os = []
                for mg in range(2):
                    t_mls = []
                    for mi in range(8):
                        m = mg * 8 + mi
                        t_ml = dma("sync", mixc[:, mi * 128:(mi + 1) * 128],
                                   ffnh[m, :, tb * 128:(tb + 1) * 128], "ml2",
                                   waits=[mx_cons] +
                                   ([fs2_fin] if (tb == 0 and mi == 0) else []))
                        t_mls.append(t_ml)
                    for bi in range(2):
                        bk = 2 + bi
                        bap, wk = PS.acquire(bk)
                        t_tp = None
                        for slot in range(4):
                            mi = bi * 4 + slot
                            t_tp = transp(psum_t[bk][:, slot * 128:(slot + 1) * 128],
                                          mixc[:, mi * 128:(mi + 1) * 128],
                                          cst["ident_f32"][:, :],
                                          waits=(wk + t_mls) if slot == 0 else None)
                        PS.wrote(bk, t_tp)
                        for slot in range(4):
                            m = mg * 8 + bi * 4 + slot
                            t_o = vtt(pf[:, PF_OTM + m * 128:PF_OTM + (m + 1) * 128],
                                      psum_t[bk][:, slot * 128:(slot + 1) * 128],
                                      pf[:, PF_XRE + m * 128:PF_XRE + (m + 1) * 128],
                                      op=ALU.add,
                                      waits=PS.read_waits(bk) + [t_xl])
                            PS.read_done(bk, t_o)
                            t_os.append(t_o)
                    mx_cons = t_tp
                t_fo = dma("gpsimd", out_d[tb * 128:(tb + 1) * 128, :],
                           pf[:, PF_OTM:PF_OTM + 2048], "outst", waits=t_os)
                otm_cons = t_fo
                t_fin.append(t_fo)
            did_full = True

        if not did_full:
            if hasattr(S, "stream_map") and "dbg" in S.stream_map:
                dsem = S.stream_map["dbg"]
                final_toks.append(Tok(dsem, S.cnt.get(dsem, 0)))
            t_fin = []
            for i in range(8):
                t_zz = vop(lambda h, sems: h.memset(pf[:, PF_OTM:PF_OTM + 2048], 0.0),
                           waits=(final_toks if i == 0 else [t_fin[-1]]))
                t_fo = dma("gpsimd", out_d[i * 128:(i + 1) * 128, :],
                           pf[:, PF_OTM:PF_OTM + 2048], "outst", waits=[t_zz])
                t_fin.append(t_fo)
        for sem_name in list(S.cnt.keys()):
            if sem_name.startswith("D") or sem_name == "CC":
                v = S.cnt[sem_name]
                S.emit("gpsimd", lambda h, sems, s2=sem_name, v=v:
                       h.wait_ge(sems[s2], v))
        S.emit("gpsimd", lambda h, sems: h.nop())

        S.finalize_check()

        with nc.Block() as block:
            @block.sync
            def _(h):
                for fn in S.ops["sync"]:
                    fn(h, sems)

            @block.tensor
            def _(h):
                for fn in S.ops["tensor"]:
                    fn(h, sems)

            @block.vector
            def _(h):
                for fn in S.ops["vector"]:
                    fn(h, sems)

            @block.scalar
            def _(h):
                for fn in S.ops["scalar"]:
                    fn(h, sems)

            @block.gpsimd
            def _(h):
                for fn in S.ops["gpsimd"]:
                    fn(h, sems)

    return nc, dbg_names


def make_inputs(inputs):
    x = np.asarray(inputs["L0_attn_hidden_states"])[0]
    T = lambda a: np.ascontiguousarray(np.asarray(a).T)
    wqkvT = T(inputs["L0_attn_Wqkv"])
    wzT = T(inputs["L0_attn_Wz"])
    woutT = T(inputs["L0_attn_Wout"])
    wbaT = np.ascontiguousarray(
        np.concatenate([np.asarray(inputs["L0_attn_Wb"]).T,
                        np.asarray(inputs["L0_attn_Wa"]).T], axis=1))
    wgT = T(inputs["L0_ffn_w_gate"])
    wuT = T(inputs["L0_ffn_w_up"])
    wdT = T(inputs["L0_ffn_w_down"])

    smalls = np.zeros((128, 256), dtype=np.float32)
    smalls[:, SM_BQKV:SM_BQKV + 32] = np.asarray(inputs["L0_attn_bqkv"]).reshape(32, 128).T
    smalls[:, SM_BZ:SM_BZ + 16] = np.asarray(inputs["L0_attn_bz"]).reshape(16, 128).T
    smalls[:, SM_BOUT:SM_BOUT + 16] = np.asarray(inputs["L0_attn_bout"]).reshape(16, 128).T
    smalls[:, SM_CONVB:SM_CONVB + 32] = np.asarray(inputs["L0_attn_conv_b"]).reshape(32, 128).T
    cw = np.asarray(inputs["L0_attn_conv_w"]).reshape(32, 128, 4)
    for c in range(32):
        smalls[:, SM_CONVW + 4 * c:SM_CONVW + 4 * c + 4] = cw[c]
    smalls[:, SM_NORMW] = np.asarray(inputs["L0_attn_norm_w"])
    smalls[0:16, SM_BB] = np.asarray(inputs["L0_attn_bb"])
    smalls[0:16, SM_BA] = np.asarray(inputs["L0_attn_ba"])
    smalls[0:16, SM_DTB] = np.asarray(inputs["L0_attn_dt_bias"])
    smalls[0:16, SM_ALOG] = np.asarray(inputs["L0_attn_A_log"])

    maps = []
    for ci in range(NCORES):
        smc = smalls.copy()
        if ci > 0:
            smc[ci - 1, SM_OH_PREV] = 1.0
        smc[ci, SM_OH_SELF] = 1.0
        if ci > 0:
            smc[:, SM_OHB + ci] = 1.0
        maps.append({
            "x_sl": np.ascontiguousarray(x[ci * TPC:(ci + 1) * TPC]),
            "wqkvT": wqkvT, "wzT": wzT, "woutT": woutT, "wbaT": wbaT,
            "wgT_sl": np.ascontiguousarray(wgT[ci * (HID // NCORES):(ci + 1) * (HID // NCORES)]),
            "wuT_sl": np.ascontiguousarray(wuT[ci * (HID // NCORES):(ci + 1) * (HID // NCORES)]),
            "wdT_sl": np.ascontiguousarray(wdT[ci * (FFN_I // NCORES):(ci + 1) * (FFN_I // NCORES)]),
            "smalls": smc,
        })
    return maps


_PROG_CACHE = {}


def kernel(**inputs):
    if "main" not in _PROG_CACHE:
        _PROG_CACHE["main"] = build_program()
    nc, _ = _PROG_CACHE["main"]
    maps = make_inputs(inputs)
    res = run_bass_kernel_spmd(nc, maps, core_ids=list(range(NCORES)))
    out = np.concatenate([res.results[ci]["out_sl"] for ci in range(NCORES)], axis=0)
    return out[None]


# version 06 (final)
# speedup vs baseline: 1.0100x; 1.0100x over previous
import numpy as np
import ml_dtypes
from contextlib import ExitStack

import concourse.bass as bass
import concourse.mybir as mybir
from concourse.bass_utils import run_bass_kernel_spmd

F32 = mybir.dt.float32
BF16 = mybir.dt.bfloat16
I32 = mybir.dt.int32
AF = mybir.ActivationFunctionType
ALU = mybir.AluOpType
AX = mybir.AxisListType

NCORES = 8
S_TOK = 8192
TPC = S_TOK // NCORES
HID = 2048
NH = HID // 128
CONV_DIM = 4096
NCV = CONV_DIM // 128
HK, HV, DK, DV = 8, 16, 128, 128
CHUNK = 128
NCH = TPC // CHUNK
FFN_I = 8192
NFI = FFN_I // 128
TT = 512
NTT = TPC // TT
INVSQ = float(1.0 / np.sqrt(np.float32(8192.0)))
RMS_EPS = 1e-6
L2_EPS = 1e-6
LN_EPS = 1e-5
Q_EPS = 1e-5
DK_SCALE = float(DK ** -0.5)

ENGINES = ("sync", "tensor", "vector", "scalar", "gpsimd")

SM_BQKV = 0
SM_BZ = 32
SM_BOUT = 48
SM_CONVB = 64
SM_CONVW = 96
SM_NORMW = 224
SM_BB = 225
SM_BA = 226
SM_DTB = 227
SM_ALOG = 228
SM_OH_PREV = 229
SM_OH_SELF = 230
SM_OHB = 232


class Tok:
    __slots__ = ("sem", "val")

    def __init__(self, sem, val):
        self.sem = sem
        self.val = val


class Sched:
    def __init__(self):
        self.ops = {e: [] for e in ENGINES}
        self.cnt = {}
        self.waited = {}
        self.sem_owner = {}
        self.max_wait = {}
        self.n_inst = 0

    def inc_tok(self, name, amt):
        self.cnt[name] = self.cnt.get(name, 0) + amt
        return Tok(name, self.cnt[name])

    def wait(self, eng, toks, force=False):
        if toks is None:
            return
        if isinstance(toks, Tok):
            toks = [toks]
        for t in toks:
            if t is None:
                continue
            assert t.val <= self.cnt.get(t.sem, 0), \
                f"wait {t.sem}>={t.val} > emitted {self.cnt.get(t.sem, 0)}"
            if (self.sem_owner.get(t.sem) == eng and not force
                    and not t.sem.startswith("D") and t.sem != "CC"):
                continue
            key = (eng, t.sem)
            if self.waited.get(key, -1) >= t.val:
                continue
            self.waited[key] = t.val
            self.max_wait[t.sem] = max(self.max_wait.get(t.sem, 0), t.val)
            self.ops[eng].append(
                lambda h, sems, s=t.sem, v=t.val: h.wait_ge(sems[s], v))
            self.n_inst += 1

    SERIAL = ("vector", "scalar")
    N_DSEM = 80

    def stream_sem(self, stream):
        if not hasattr(self, "stream_map"):
            self.stream_map = {}
        if stream not in self.stream_map:
            idx = len(self.stream_map)
            assert idx < self.N_DSEM, "out of DMA stream semaphores"
            self.stream_map[stream] = f"D{idx}"
        return self.stream_map[stream]

    def emit(self, eng, fn, waits=None, inc=None, inc_amt=1):
        if eng in self.SERIAL:
            prev = getattr(self, "last_tok", {}).get(eng)
            if prev is not None:
                self.wait(eng, [prev], force=True)
        self.wait(eng, waits)
        tok = None
        self.n_inst += 1
        if inc is not None:
            owner = self.sem_owner.setdefault(inc, eng)
            assert owner == eng, f"sem {inc}: owners {owner} and {eng}"
            tok = self.inc_tok(inc, inc_amt)

            def wrapped(h, sems, fn=fn, inc=inc, inc_amt=inc_amt):
                fn(h, sems).then_inc(sems[inc], inc_amt)
            self.ops[eng].append(wrapped)
            if eng in self.SERIAL:
                if not hasattr(self, "last_tok"):
                    self.last_tok = {}
                self.last_tok[eng] = tok
        else:
            self.ops[eng].append(lambda h, sems, fn=fn: fn(h, sems))
        return tok

    def finalize_check(self):
        for sem, v in self.max_wait.items():
            assert v <= self.cnt[sem], f"sem {sem}: wait {v} > total {self.cnt[sem]}"


class Banks:
    def __init__(self, aps):
        self.aps = aps
        self.writer = [None] * 8
        self.readers = [[] for _ in range(8)]

    def acquire(self, i):
        w = [t for t in self.readers[i] if t is not None]
        self.readers[i] = []
        return self.aps[i], w

    def wrote(self, i, tok):
        self.writer[i] = tok

    def read_waits(self, i):
        return [self.writer[i]] if self.writer[i] is not None else []

    def read_done(self, i, tok):
        self.readers[i].append(tok)


def _hadamard(n):
    H = np.array([[1.0]], dtype=np.float32)
    while H.shape[0] < n:
        H = np.block([[H, H], [H, -H]])
    return H.astype(np.float32)


def _const_arrays():
    bf = ml_dtypes.bfloat16
    h64 = _hadamard(64)
    h64b = np.zeros((128, 128), dtype=np.float32)
    h64b[:64, :64] = h64
    h64b[64:, 64:] = h64
    tril_s = np.tril(np.ones((128, 128), np.float32), -1)
    return {
        "ident_f32": (np.eye(128, dtype=np.float32), F32),
        "ident_bf16": (np.eye(128, dtype=np.float32).astype(bf), BF16),
        "h128_bf": (_hadamard(128).astype(bf), BF16),
        "h64blk_bf": (h64b.astype(bf), BF16),
        "mask_ls": (tril_s, F32),
        "mask_us": (np.ascontiguousarray(tril_s.T), F32),
        "mask_ui": (np.triu(np.ones((128, 128), np.float32)), F32),
        "ones_row": (np.ones((1, 128), dtype=np.float32), F32),
        "ones_col": (np.ones((128, 1), dtype=np.float32), F32),
    }


def build_program(debug_taps=(), stop_after=None, p3_limit=None, p3_steps=None):
    import os as _os
    p3v = _os.environ.get("P3V", "")
    nc = bass.Bass("TRN2", target_bir_lowering=False)
    S = Sched()
    taps = set(debug_taps)
    dbg_names = []

    x_d = nc.dram_tensor("x_sl", [TPC, HID], F32, kind="ExternalInput").ap()
    wqkvT_d = nc.dram_tensor("wqkvT", [HID, CONV_DIM], F32, kind="ExternalInput").ap()
    wzT_d = nc.dram_tensor("wzT", [HID, HID], F32, kind="ExternalInput").ap()
    woutT_d = nc.dram_tensor("woutT", [HID, HID], F32, kind="ExternalInput").ap()
    wbaT_d = nc.dram_tensor("wbaT", [HID, 32], F32, kind="ExternalInput").ap()
    wgT_d = nc.dram_tensor("wgT_sl", [HID // NCORES, FFN_I], F32, kind="ExternalInput").ap()
    wuT_d = nc.dram_tensor("wuT_sl", [HID // NCORES, FFN_I], F32, kind="ExternalInput").ap()
    wdT_d = nc.dram_tensor("wdT_sl", [FFN_I // NCORES, HID], F32, kind="ExternalInput").ap()
    smalls_d = nc.dram_tensor("smalls", [128, 256], F32, kind="ExternalInput").ap()
    out_d = nc.dram_tensor("out_sl", [TPC, HID], F32, kind="ExternalOutput").ap()

    wqkv_bf = nc.dram_tensor("wqkv_bf", [HID, CONV_DIM], BF16).ap()
    wz_bf = nc.dram_tensor("wz_bf", [HID, HID], BF16).ap()
    wout_bf = nc.dram_tensor("wout_bf", [HID, HID], BF16).ap()
    wba_bf = nc.dram_tensor("wba_bf", [HID, 32], BF16).ap()
    ar_in = nc.dram_tensor("ar_in", [1, 4], F32).ap()
    ar_out = nc.dram_tensor("ar_out", [1, 4], F32).ap()
    agg_in = nc.dram_tensor("agg_in", [HID // NCORES, FFN_I], BF16).ap()
    agg_out = nc.dram_tensor("agg_out", [HID, FFN_I], BF16).ap()
    agu_in = nc.dram_tensor("agu_in", [HID // NCORES, FFN_I], BF16).ap()
    agu_out = nc.dram_tensor("agu_out", [HID, FFN_I], BF16).ap()
    agd_in = nc.dram_tensor("agd_in", [FFN_I // NCORES, HID], BF16).ap()
    agd_out = nc.dram_tensor("agd_out", [FFN_I, HID], BF16).ap()
    xqh = nc.dram_tensor("xqh", [NH, 128, TPC], BF16).ap()
    halo_in = nc.dram_tensor("halo_in", [NCV, 128, 3], F32).ap()
    halo_out = nc.dram_tensor("halo_out", [NCORES * NCV, 128, 3], F32).ap()
    rowsd = nc.dram_tensor("rowsd", [16, 3 * TPC], F32).ap()
    dpk = nc.dram_tensor("dpk", [NCH * HV, 128, 512], BF16).ap()
    mb_in = nc.dram_tensor("mb_in", [HV * 256, 128], BF16).ap()
    mb_out = nc.dram_tensor("mb_out", [NCORES * HV * 256, 128], BF16).ap()
    zh = nc.dram_tensor("zh", [NFI, 128, TPC], BF16).ap()
    zhl = nc.dram_tensor("zhl", [NFI, 128, TPC], BF16).ap()
    qsc = nc.dram_tensor("qsc", [128, 8], F32).ap()
    yqh = nc.dram_tensor("yqh", [NFI, 128, TPC], BF16).ap()
    zsc = nc.dram_tensor("zsc", [HV, 128, TPC], BF16).ap()
    ffnh = nc.dram_tensor("ffnh", [NH, 128, TPC], F32).ap()

    const_arr = _const_arrays()
    const_dram = {k: nc.inline_tensor(np.asarray(v), name="c_" + k)
                  for k, (v, dt) in const_arr.items()}

    with ExitStack() as ctx:
        def sbuf(name, shape, dt):
            return ctx.enter_context(nc.sbuf_tensor(name, shape, dt))

        sm = sbuf("sm", [128, 256], F32)
        sconst = sbuf("sconst", [128, 8], F32)
        cst = {k: sbuf("k_" + k, list(v.shape), dt) for k, (v, dt) in const_arr.items()}
        ones16 = sbuf("ones16", [16, CHUNK], F32)
        bTp = sbuf("bTp", [128, 6 * NCH * 16], F32)
        ln_s = sbuf("ln_s", [128, 8 * 32], F32)
        srl = sbuf("srl", [1, TPC + 512], F32)
        rowbuf = sbuf("rowbuf", [1, 2 * 384], F32)
        bcrow = sbuf("bcrow", [1, NCH * 16], F32)
        edge = sbuf("edge", [128, NCV * 7], F32)
        hblk = sbuf("hblk", [128, 2 * NCV * 3], F32)
        hsel = sbuf("hsel", [128, NCV * 3 + 16], F32)
        misc = sbuf("misc", [128, 64], F32)
        prefb = sbuf("prefb", [128, 2 * 128], BF16)
        expf = sbuf("expf", [128, 4 * 128], F32)

        pb = sbuf("pb", [128, NCV * TPC], BF16)
        pc = sbuf("pc", [128, NH * TPC], BF16)
        pf = sbuf("pf", [128, 6144], F32)

        tmA = sbuf("tmA", [128, 2048], F32)
        tmB = sbuf("tmB", [128, 2048], F32)
        tmI = sbuf("tmI", [128, 1024], I32)
        mixc = sbuf("mixc", [128, 1024], F32)
        cvo = sbuf("cvo", [128, 1024], F32)
        cvo2 = sbuf("cvo2", [128, 1024], F32)
        wbuf = sbuf("wbuf", [128, 2 * 2048], BF16)
        wbuf2 = sbuf("wbuf2", [128, 2 * 2048], BF16)
        intm = sbuf("intm", [128, 2 * TT], BF16)
        yq_t = sbuf("yq_t", [128, 2048], BF16)
        relo = sbuf("relo", [128, 2 * 512], BF16)
        zrel = sbuf("zrel", [128, 1024], BF16)
        s0b = sbuf("s0b", [128, HV * 128], BF16)

        xt = pf[:, 4096:6144]
        xn = pc
        WRK0 = 0
        DSTG0 = 6144
        MBST0 = 7680
        coreb = pc
        PF_SGN = 0
        PF_ABS = 2048
        GB_B = 0
        GB_BHAT = 1024
        GB_NEGB = 2048
        GB_G = 3072
        GB_BETA = 4096
        GB_SCR = 5120
        PF_SGB = 0
        PF_YMX = 3072
        PF_YAB = 3584
        PF_OTM = 0
        PF_FREL = 2048
        PF_XRE = 4096

        psum_t = [ctx.enter_context(nc.psum_tensor(f"pb{i}", [128, 512], F32))
                  for i in range(7)]
        psum_bf = ctx.enter_context(nc.psum_tensor("pbf", [128, 1024], BF16))

        for i, v in enumerate([0.0, 1.0, float(LN_EPS), float(L2_EPS), -1.0, 1e-6,
                               -128.0, 1.0 / 128.0]):
            nc.const_aps.register(F32, v, sconst[:, i:i + 1])

        sems = {}
        for name in ["MM", "VE", "SC", "CC"] + [f"D{i}" for i in range(Sched.N_DSEM)]:
            sems[name] = ctx.enter_context(nc.semaphore(name))

        PS = Banks([p[:, :] for p in psum_t] + [psum_bf[:, :]])

        def dma(eng, out, in_, stream, waits=None, slow=False):
            sem = S.stream_sem(stream)
            prev = S.cnt.get(sem, 0)
            if prev > 0:
                S.wait(eng, [Tok(sem, prev)], force=True)
            return S.emit(eng, lambda h, sems, o=out, i=in_, sl=slow: h.dma_start(
                out=o, in_=i, allow_slow_non_contiguous=sl),
                waits=waits, inc=sem, inc_amt=16)

        def mm(out, lhsT, rhs, start=True, stop=True, waits=None, inc="MM"):
            return S.emit("tensor",
                          lambda h, sems, o=out, l=lhsT, r=rhs, a=start, b=stop:
                          h.matmul(o, l, r, start=a, stop=b),
                          waits=waits, inc=inc)

        def transp(out, in_, ident, waits=None, inc="MM"):
            return S.emit("tensor",
                          lambda h, sems, o=out, i=in_, d=ident: h.transpose(o, i, d),
                          waits=waits, inc=inc)

        def vop(fn, waits=None, inc="VE"):
            return S.emit("vector", fn, waits=waits, inc=inc)

        def act(out, in_, func, bias=0.0, scale=1.0, waits=None, inc="SC"):
            return S.emit("scalar",
                          lambda h, sems, o=out, i=in_, f=func, b=bias, s=scale:
                          h.activation(out=o, in_=i, func=f, bias=b, scale=s),
                          waits=waits, inc=inc)

        def scopy(out, in_, waits=None):
            return S.emit("scalar",
                          lambda h, sems, o=out, i=in_: h.copy(out=o, in_=i),
                          waits=waits, inc="SC")

        def vcopy(out, in_, waits=None):
            return vop(lambda h, sems, o=out, i=in_: h.tensor_copy(out=o, in_=i),
                       waits=waits)

        def vts(out, in0, s1, s2=None, op0=ALU.mult, op1=ALU.bypass, waits=None):
            return vop(lambda h, sems, o=out, i=in0, a=s1, b=s2, p=op0, q=op1:
                       h.tensor_scalar(out=o, in0=i, scalar1=a, scalar2=b, op0=p, op1=q),
                       waits=waits)

        def vtt(out, in0, in1, op=ALU.mult, waits=None):
            return vop(lambda h, sems, o=out, a=in0, b=in1, p=op:
                       h.tensor_tensor(out=o, in0=a, in1=b, op=p), waits=waits)

        def vstt(out, in0, scl, in1, op0=ALU.mult, op1=ALU.add, waits=None):
            return vop(lambda h, sems, o=out, a=in0, s=scl, b=in1, p=op0, q=op1:
                       h.scalar_tensor_tensor(out=o, in0=a, scalar=s, in1=b, op0=p, op1=q),
                       waits=waits)

        def vsem():
            tk = S.emit("vector", lambda h, sems: h.nop(), inc="VE")
            S.wait("vector", [tk], force=True)
            return tk

        def tree_reduce(dst, src, bufA, bufB, n, op=ALU.add, waits=None):
            pp = [bufA, bufB]
            t = vtt(pp[0][:, 0:n // 2], src[:, 0:n // 2], src[:, n // 2:n], op=op,
                    waits=waits)
            m = n // 2
            k = 0
            while m > 4:
                t = vtt(pp[(k + 1) % 2][:, 0:m // 2], pp[k % 2][:, 0:m // 2],
                        pp[k % 2][:, m // 2:m], op=op, waits=[t])
                m //= 2
                k += 1
            b = pp[k % 2]
            o = pp[(k + 1) % 2]
            t1 = vts(o[:, 0:1], b[:, 0:1], b[:, 1:2], None, op, waits=[t])
            t2 = vts(o[:, 1:2], b[:, 2:3], b[:, 3:4], None, op, waits=[t])
            return vts(dst, o[:, 0:1], o[:, 1:2], None, op, waits=[t1, t2])

        def coll(kind, op, ins, outs, waits):
            return S.emit("gpsimd",
                          lambda h, sems, k=kind, o=op, i=ins, u=outs:
                          h.collective_compute(k, o, replica_groups=[list(range(NCORES))],
                                               ins=i, outs=u),
                          waits=waits, inc="CC")

        def tapout(name, shape, dt):
            if name in taps:
                dbg_names.append(name)
                return nc.dram_tensor("dbg_" + name, shape, dt, kind="ExternalOutput").ap()
            return None

        tp_xn = tapout("xn", [128, NH * TPC], BF16)
        tp_xq = tapout("xq", [NH, 128, TPC], BF16)
        tp_qkv = tapout("qkv", [128, NCV * TPC], BF16)
        tp_gb = tapout("gb", [16, 6144], F32)
        tp_z = tapout("z", [HV, 128, TPC], BF16)
        tp_tern = tapout("tern", [HID, FFN_I], BF16)
        tp_core = tapout("core", [128, HV * TPC], BF16)
        tp_dpk = tapout("dpk", [NCH * HV, 128, 512], BF16)
        tp_mb = tapout("mb", [HV * 256, 128], BF16)
        tp_ffn = tapout("ffn", [NH, 128, TPC], F32)
        tp_zh = tapout("zh", [NFI, 128, TPC], BF16)
        tp_rows = tapout("rows", [16, 3 * TPC], F32)
        tp_srow = tapout("srow", [1, TPC], F32)

        t_smalls = dma("sync", sm[:, :], smalls_d, "smalls")
        t_consts = None
        for k in cst:
            t_consts = dma("sync", cst[k][:, :], const_dram[k].ap(), "consts")
        for i, v in enumerate([0.0, 1.0, float(LN_EPS), float(L2_EPS), -1.0, 1e-6,
                               -128.0, 1.0 / 128.0]):
            S.emit("vector", lambda h, sems, i=i, v=v: h.memset(sconst[:, i:i + 1], v),
                   inc="VE")
        S.emit("vector", lambda h, sems: h.memset(ones16[:, :], 1.0), inc="VE")
        t_cz = S.emit("vector", lambda h, sems: h.nop(), inc="VE")
        S.emit("scalar", lambda h, sems: h.nop(), waits=[t_cz], inc="SC")

        t_wcast = None
        for nm, src, dst_ in (("qkv", wqkvT_d, wqkv_bf), ("z", wzT_d, wz_bf),
                              ("out", woutT_d, wout_bf), ("ba", wbaT_d, wba_bf)):
            for r0 in range(0, HID, 512):
                t_wcast = dma("gpsimd", dst_[r0:r0 + 512, :], src[r0:r0 + 512, :],
                              "wcast")
        awt = {"qkv": t_wcast, "z": t_wcast, "out": t_wcast, "ba": t_wcast}

        xt_readers = []
        tb_rot = [2, 3]
        lnb = 6
        t_xq_store = []
        prev_tpxn = None
        prev_tpxq = None
        t_xn_all = None
        for i in range(8):
            c0 = i * 32
            t_x = dma("sync", xt[:, :], x_d[i * 128:(i + 1) * 128, :], "xt",
                      waits=xt_readers)
            xt_readers = []
            t_s = tree_reduce(ln_s[:, c0:c0 + 1], xt, cvo, cvo2, HID,
                              waits=[t_x, t_cz])
            t_m = vts(ln_s[:, c0 + 1:c0 + 2], ln_s[:, c0:c0 + 1], 1.0 / HID, waits=[t_s])
            t_sq = act(tmA[:, :], xt[:, :], AF.Square, waits=[t_x, prev_tpxn])
            t_s2 = tree_reduce(ln_s[:, c0 + 2:c0 + 3], tmA, cvo, cvo2, HID,
                               waits=[t_sq, t_s])
            t_ms = vts(ln_s[:, c0 + 3:c0 + 4], ln_s[:, c0 + 2:c0 + 3], 1.0 / HID,
                       waits=[t_s2])
            t_m2 = vts(ln_s[:, c0 + 4:c0 + 5], ln_s[:, c0 + 1:c0 + 2],
                       ln_s[:, c0 + 1:c0 + 2], waits=[t_m])
            t_var = vtt(ln_s[:, c0 + 5:c0 + 6], ln_s[:, c0 + 3:c0 + 4],
                        ln_s[:, c0 + 4:c0 + 5], op=ALU.subtract, waits=[t_ms, t_m2])
            t_sd = act(ln_s[:, c0 + 16:c0 + 17], ln_s[:, c0 + 5:c0 + 6], AF.Sqrt,
                       bias=float(LN_EPS), waits=[t_var])
            t_rstd = vop(lambda h, sems, c=c0: h.reciprocal(
                out=ln_s[:, c + 7:c + 8], in_=ln_s[:, c + 16:c + 17]), waits=[t_sd])
            t_negm = vts(ln_s[:, c0 + 13:c0 + 14], ln_s[:, c0 + 1:c0 + 2], -1.0,
                         waits=[t_m])
            t_xnt = vts(tmA[:, :], xt[:, :], ln_s[:, c0 + 13:c0 + 14],
                        ln_s[:, c0 + 7:c0 + 8], ALU.add, ALU.mult,
                        waits=[t_negm, t_rstd, t_x, prev_tpxn, t_s2])
            xt_readers = [t_s, t_xnt]
            t_ab = act(tmB[:, :], tmA[:, :], AF.Abs, waits=[t_xnt, prev_tpxq])
            t_am = tree_reduce(ln_s[:, c0 + 9:c0 + 10], tmB, cvo, cvo2, HID,
                               op=ALU.max, waits=[t_ab, t_s2])
            t_amc = vts(ln_s[:, c0 + 10:c0 + 11], ln_s[:, c0 + 9:c0 + 10],
                        float(Q_EPS), None, ALU.max, waits=[t_am])
            t_sinv = vts(ln_s[:, c0 + 11:c0 + 12], ln_s[:, c0 + 10:c0 + 11],
                         1.0 / 128.0, waits=[t_amc])
            t_sc = vop(lambda h, sems, c=c0: h.reciprocal(
                out=ln_s[:, c + 12:c + 13], in_=ln_s[:, c + 11:c + 12]),
                waits=[t_sinv])
            t_q0 = vts(tmB[:, :], tmA[:, :], ln_s[:, c0 + 12:c0 + 13], 127.0,
                       ALU.mult, ALU.min, waits=[t_sc, t_am])
            t_q0 = vts(tmB[:, :], tmB[:, :], 128.0, None, ALU.add, waits=[t_q0])
            t_q5s = []
            fbuf = [cvo, cvo2]
            for hh in range(2):
                sl_b = tmB[:, hh * 1024:(hh + 1) * 1024]
                fb = fbuf[hh][:, 0:1024]
                t_a1q = vcopy(tmI[:, :], sl_b, waits=[t_q0])
                t_a2q = vcopy(fb, tmI[:, :], waits=[t_a1q])
                t_a3q = vtt(sl_b, sl_b, fb, op=ALU.subtract, waits=[t_a2q])
                t_a4q = vts(sl_b, sl_b, 0.5, None, ALU.is_ge, waits=[t_a3q])
                t_a5q = vtt(sl_b, sl_b, fb, op=ALU.add, waits=[t_a4q])
                t_q5 = vts(sl_b, sl_b, -128.0, None, ALU.add, waits=[t_a5q])
                t_q5s.append(t_q5)
            t_q5 = t_q5s[-1]
            bb, wb = PS.acquire(lnb)
            t_r = mm(psum_t[lnb][0:1, 0:128], ln_s[:, c0 + 11:c0 + 12],
                     cst["ident_f32"][:, :], waits=wb + [t_sinv, t_consts])
            PS.wrote(lnb, t_r)
            t_rv = vcopy(srl[0:1, i * 128:(i + 1) * 128], psum_t[lnb][0:1, 0:128],
                         waits=PS.read_waits(lnb))
            PS.read_done(lnb, t_rv)
            for half, src_t in ((0, tmA), (1, tmB)):
                src_tok = t_xnt if half == 0 else t_q5
                for g in range(NH // 4):
                    bk = tb_rot[g % 2]
                    bap, wps = PS.acquire(bk)
                    t_tp = None
                    for slot in range(4):
                        c = g * 4 + slot
                        t_tp = transp(psum_t[bk][:, slot * 128:(slot + 1) * 128],
                                      src_t[:, c * 128:(c + 1) * 128],
                                      cst["ident_f32"][:, :],
                                      waits=(wps + [src_tok, t_consts] +
                                             ([t_q5s[0]] if half == 1 else []))
                                      if slot == 0 else None)
                    PS.wrote(bk, t_tp)
                    for slot in range(4):
                        c = g * 4 + slot
                        if half == 0:
                            dst_ap = xn[:, c * TPC + i * 128: c * TPC + (i + 1) * 128]
                            ew = PS.read_waits(bk)
                        else:
                            dst_ap = yq_t[:, c * 128:(c + 1) * 128]
                            ew = PS.read_waits(bk) + \
                                ([t_xq_store[-1]] if t_xq_store else [])
                        t_ev = vcopy(dst_ap, psum_t[bk][:, slot * 128:(slot + 1) * 128],
                                     waits=ew)
                        PS.read_done(bk, t_ev)
                    if half == 0:
                        prev_tpxn = t_tp
                        t_xn_all = t_ev
                    else:
                        prev_tpxq = t_tp
                        last_xq_ev = t_ev
            t_xqs = dma("gpsimd",
                        xqh[:, :, i * 128:(i + 1) * 128].transpose([1, 0, 2]),
                        yq_t[:, 0:2048].rearrange("p (c t) -> p c t", c=NH),
                        "xqst", waits=[last_xq_ev])
            t_xq_store.append(t_xqs)
        t_srow_all = t_rv
        t_xt_free = list(xt_readers)
        if tp_xn is not None:
            dma("gpsimd", tp_xn, xn[:, 0:NH * TPC], "dbg", waits=[t_xn_all])
        if tp_xq is not None:
            dma("gpsimd", tp_xq, xqh, "dbg", waits=[t_xq_store[-1]])
        if tp_srow is not None:
            dma("gpsimd", tp_srow, srl[0:1, 0:TPC], "dbg", waits=[t_srow_all])

        final_toks = []
        if stop_after == "p1":
            t_ag_g = t_ag_u = t_ag_d = None
            t_hag = None
            t_sc_tiles = [[None] * 4]
            t_rows_st = None
            final_toks += [t_xn_all, t_srow_all, t_xq_store[-1]]
        tern_jobs = [(0, wgT_d, agg_in, agg_out, HID // NCORES, FFN_I),
                     (1, wuT_d, agu_in, agu_out, HID // NCORES, FFN_I),
                     (2, wdT_d, agd_in, agd_out, FFN_I // NCORES, HID)]
        stage = [tmA, tmB]
        si = 0
        last_use = [prev_tpxn, prev_tpxq]
        first = [True, True, True]
        t_abs = None
        abscr = pf[:, PF_SGN:PF_SGN + 2048]
        pfA = pf[:, 2048:3072]
        pfB = pf[:, 3072:3584]
        prev_abscr = None
        for wi, src, agi, ago, nrow, ncol in tern_jobs:
            for r0 in range(0, nrow, 128):
                for cc0 in range(0, ncol, 2048):
                    st_t = stage[si % 2]
                    tl = dma("sync", st_t[:, :], src[r0:r0 + 128, cc0:cc0 + 2048],
                             f"tstg{si % 2}", waits=[last_use[si % 2]])
                    ta = act(abscr, st_t[:, :], AF.Abs, waits=[tl, prev_abscr])
                    last_use[si % 2] = ta
                    tr = tree_reduce(misc[:, 4:5], abscr, pfA, pfB, 2048,
                                     waits=[ta, prev_abscr])
                    prev_abscr = tr
                    if first[wi]:
                        t_abs = vcopy(misc[:, wi:wi + 1], misc[:, 4:5], waits=[tr])
                        first[wi] = False
                    else:
                        t_abs = vop(lambda h, sems, wi=wi: h.tensor_add(
                            out=misc[:, wi:wi + 1], in0=misc[:, wi:wi + 1],
                            in1=misc[:, 4:5]), waits=[tr])
                    si += 1
        bA, wA = PS.acquire(0)
        t_mm0 = mm(psum_t[0][0:1, 0:3], cst["ones_col"][:, :], misc[:, 0:3],
                   waits=wA + [t_abs, t_consts])
        PS.wrote(0, t_mm0)
        t_ps0 = vcopy(misc[0:1, 8:11], psum_t[0][0:1, 0:3], waits=PS.read_waits(0))
        PS.read_done(0, t_ps0)
        t_pz = vop(lambda h, sems: h.memset(misc[0:1, 11:12], 0.0), waits=[t_ps0])
        t_st0 = dma("gpsimd", ar_in[0:1, 0:4], misc[0:1, 8:12], "arst",
                    waits=[t_pz])
        t_ar = coll("AllReduce", ALU.add, [ar_in], [ar_out], [t_st0])
        t_lar = dma("sync", misc[0:1, 12:15], ar_out[0:1, 0:3], "arld", waits=[t_ar])
        cntw = float(HID * FFN_I)
        t_mean = vts(misc[0:1, 16:19], misc[0:1, 12:15], 1.0 / cntw, float(Q_EPS),
                     ALU.mult, ALU.max, waits=[t_lar])
        t_thr = vts(misc[0:1, 20:23], misc[0:1, 16:19], 0.5, waits=[t_mean])
        bB, wB = PS.acquire(1)
        t_b1 = None
        for wi in range(3):
            t_b1 = mm(psum_t[1][:, wi:wi + 1], cst["ones_row"][:, :],
                      misc[0:1, 20 + wi:21 + wi],
                      waits=(wB + [t_thr]) if wi == 0 else [t_thr])
            t_b1 = mm(psum_t[1][:, 4 + wi:5 + wi], cst["ones_row"][:, :],
                      misc[0:1, 16 + wi:17 + wi], waits=[t_mean])
        PS.wrote(1, t_b1)
        t_thrv = vcopy(misc[:, 24:27], psum_t[1][:, 0:3], waits=PS.read_waits(1))
        t_sgv = vcopy(misc[:, 28:31], psum_t[1][:, 4:7], waits=PS.read_waits(1))
        PS.read_done(1, t_sgv)

        si = 0
        ag_toks = {}
        t_tstore = None
        prev_tn = None
        sgn_s = pf[:, PF_SGN:PF_SGN + 2048]
        abs_s = pf[:, PF_ABS:PF_ABS + 2048]
        for wi, src, agi, ago, nrow, ncol in tern_jobs:
            for r0 in range(0, nrow, 128):
                for cc0 in range(0, ncol, 2048):
                    st_t = stage[si % 2]
                    tl = dma("sync", st_t[:, :], src[r0:r0 + 128, cc0:cc0 + 2048],
                             f"tstg{si % 2}", waits=[last_use[si % 2]])
                    t_sg = act(sgn_s, st_t[:, :], AF.Sign,
                               waits=[tl, prev_tn, prev_abscr])
                    t_ab = act(abs_s, st_t[:, :], AF.Abs, waits=[tl])
                    last_use[si % 2] = t_ab
                    t_gt = vts(abs_s, abs_s, misc[:, 24 + wi:25 + wi], None,
                               ALU.is_gt, waits=[t_ab, t_thrv])
                    for qh_ in range(2):
                        t_tn = vtt(zrel[:, 0:1024], abs_s[:, qh_ * 1024:(qh_ + 1) * 1024],
                                   sgn_s[:, qh_ * 1024:(qh_ + 1) * 1024], op=ALU.mult,
                                   waits=[t_gt, t_sg, t_tstore])
                        prev_tn = t_tn
                        t_tstore = dma("gpsimd",
                                       agi[r0:r0 + 128, cc0 + qh_ * 1024:cc0 + (qh_ + 1) * 1024],
                                       zrel[:, 0:1024], "ST", waits=[t_tn])
                    si += 1
            ag_toks[wi] = coll("AllGather", ALU.bypass, [agi], [ago], [t_tstore])
        t_ag_g, t_ag_u, t_ag_d = ag_toks[0], ag_toks[1], ag_toks[2]
        t_pf_free = [prev_tn, t_tstore]
        t_zrel_free = t_tstore
        if tp_tern is not None:
            dma("gpsimd", tp_tern, agg_out, "dbg", waits=[t_ag_g])
        if stop_after == "p0d":
            t_hag = None
            t_sc_tiles = [[None] * 4]
            t_rows_st = None
            final_toks += [t_ag_g, t_ag_u, t_ag_d]

        if stop_after not in ("p1", "p0d"):
            def wload(ring, slot, wsrc, ocol0, prev_consumer, nk=NH):
                dst = ring[:, slot * 2048: slot * 2048 + nk * 128]
                srcv = wsrc.rearrange("(fc p) o -> p fc o", p=128)[:, 0:nk, ocol0:ocol0 + 128]
                return dma("sync", dst.rearrange("p (fc c) -> p fc c", fc=nk), srcv,
                           "LD", waits=prev_consumer)

            qkv_ap = pb
            mm_bank_rot = [4, 5]
            t_conv_done = []
            t_edge_saved = []
            wring_consumer = [None, None]
            t_zstore = [None, None]
            prev_mix_readers = []
            prev_sql = None
            prev_cvo = []
            t_l2_bcast = {}

            def proj_mm(oc_abs, wsrc, ocol0, n_k, bank, rtok, extra=None):
                toks = []
                for th in range(2):
                    bap, wps = PS.acquire(bank[th])
                    t_last = None
                    for fc in range(n_k):
                        w0 = (wps + [rtok, t_xn_all]) if fc == 0 else None
                        t_last = mm(psum_t[bank[th]][:, 0:512],
                                    wbuf[:, (oc_abs % 2) * 2048 + fc * 128:
                                         (oc_abs % 2) * 2048 + (fc + 1) * 128],
                                    xn[:, 0:1][:, 0:0] if False else
                                    xn[:, 0:NH * TPC].rearrange("p (c t) -> p c t", c=NH)[:, fc, th * 512:(th + 1) * 512],
                                    start=(fc == 0), stop=(fc == n_k - 1), waits=w0)
                    PS.wrote(bank[th], t_last)
                    toks.append(t_last)
                return toks

            for oc in range(NCV):
                slot = oc % 2
                t_wl = wload(wbuf, slot, wqkv_bf, oc * 128,
                             [wring_consumer[slot], awt["qkv"]])
                bks = [mm_bank_rot[0], mm_bank_rot[1]]
                tmm = proj_mm(oc, wqkv_bf, oc * 128, NH, bks, t_wl)
                wring_consumer[slot] = tmm[-1]
                t_ev = []
                for th in range(2):
                    te = vts(mixc[:, th * 512:(th + 1) * 512], psum_t[bks[th]][:, 0:512],
                             sm[:, SM_BQKV + oc:SM_BQKV + oc + 1], None, ALU.add,
                             waits=PS.read_waits(bks[th]) + prev_mix_readers +
                             ([t_smalls] if oc == 0 else []))
                    PS.read_done(bks[th], te)
                    t_ev.append(te)
                t_eh = vcopy(edge[:, oc * 4:oc * 4 + 4], mixc[:, 0:4], waits=[t_ev[0]])
                t_et = vcopy(edge[:, NCV * 4 + oc * 3:NCV * 4 + oc * 3 + 3],
                             mixc[:, 1021:1024], waits=[t_ev[1]])
                t_edge_saved.append(t_et)
                wc = SM_CONVW + oc * 4
                t_c = vts(cvo[:, 0:1024], mixc[:, 0:1024], sm[:, wc + 3:wc + 4],
                          sm[:, SM_CONVB + oc:SM_CONVB + oc + 1], ALU.mult, ALU.add,
                          waits=[t_ev[0], t_ev[1]] + prev_cvo)
                t_c = vstt(cvo2[:, 1:1024], mixc[:, 0:1023], sm[:, wc + 2:wc + 3],
                           cvo[:, 1:1024], waits=[t_c])
                vsem()
                t_c = vstt(cvo[:, 2:1024], mixc[:, 0:1022], sm[:, wc + 1:wc + 2],
                           cvo2[:, 2:1024], waits=[t_c])
                vsem()
                t_c = vstt(cvo2[:, 3:1024], mixc[:, 0:1021], sm[:, wc:wc + 1],
                           cvo[:, 3:1024], waits=[t_c])
                prev_mix_readers = [t_c, t_eh, t_et]
                t_z0 = S.emit("scalar", lambda h, sems: h.memzero(cvo2[:, 0:3]),
                              waits=[t_c], inc="SC")
                t_si = act(cvo2[:, 3:1024], cvo2[:, 3:1024], AF.Silu, waits=[t_c])
                qsl = qkv_ap[:, oc * TPC:oc * TPC + TPC]
                if oc >= 16:
                    t_q = vcopy(qsl, cvo2[:, 0:1024], waits=[t_si])
                    prev_cvo = [t_q]
                    t_conv_done.append(t_q)
                else:
                    t_q = None
                    for th in range(2):
                        t_s1 = act(cvo[:, th * 512:(th + 1) * 512],
                                   cvo2[:, th * 512:(th + 1) * 512], AF.Square,
                                   waits=[t_si, prev_sql])
                        bk = 6
                        bap, wps = PS.acquire(bk)
                        t_ssq = mm(psum_t[bk][0:1, 0:512], cst["ones_col"][:, :],
                                   cvo[:, th * 512:(th + 1) * 512], waits=wps + [t_s1])
                        PS.wrote(bk, t_ssq)
                        t_sd2 = act(srl[0:1, TPC:TPC + 512],
                                    psum_t[bk][0:1, 0:512], AF.Sqrt, bias=float(L2_EPS),
                                    waits=PS.read_waits(bk))
                        PS.read_done(bk, t_sd2)
                        t_rc = vop(lambda h, sems, th=th: h.reciprocal(
                            out=srl[0:1, TPC:TPC + 512],
                            in_=srl[0:1, TPC:TPC + 512]),
                            waits=[t_sd2])
                        if oc < 8:
                            t_rc = vts(srl[0:1, TPC:TPC + 512],
                                       srl[0:1, TPC:TPC + 512],
                                       DK_SCALE, waits=[t_rc])
                        bk2 = 3
                        bap2, wps2 = PS.acquire(bk2)
                        t_bc = mm(psum_t[bk2][:, 0:512], cst["ones_row"][:, :],
                                  srl[0:1, TPC:TPC + 512],
                                  waits=wps2 + [t_rc])
                        PS.wrote(bk2, t_bc)
                        t_q = vtt(qsl[:, th * 512:(th + 1) * 512],
                                  cvo2[:, th * 512:(th + 1) * 512], psum_t[bk2][:, 0:512],
                                  op=ALU.mult, waits=PS.read_waits(bk2))
                        PS.read_done(bk2, t_q)
                        prev_sql = t_s1
                    prev_cvo = [t_q]
                    t_conv_done.append(t_q)

            t_hst = dma("gpsimd", halo_in.transpose([1, 0, 2]),
                        edge[:, NCV * 4:NCV * 4 + NCV * 3].rearrange("p (c t) -> p c t", c=NCV),
                        "halost", waits=t_edge_saved)
            t_hag = coll("AllGather", ALU.bypass, [halo_in], [halo_out], [t_hst])
            bk = 6
            bap, wps = PS.acquire(bk)
            t_ohr = mm(psum_t[bk][0:1, 0:128], sm[:, SM_OH_PREV:SM_OH_PREV + 1],
                       cst["ident_f32"][:, :], waits=wps + [t_smalls])
            PS.wrote(bk, t_ohr)
            t_ohrv = vcopy(misc[0:1, 32:40], psum_t[bk][0:1, 0:8], waits=PS.read_waits(bk))
            PS.read_done(bk, t_ohrv)
            bap, wps = PS.acquire(bk)
            t_ohb = mm(psum_t[bk][:, 0:8], cst["ones_row"][:, :], misc[0:1, 32:40],
                       waits=wps + [t_ohrv])
            PS.wrote(bk, t_ohb)
            t_ohbv = vcopy(misc[:, 32:40], psum_t[bk][:, 0:8], waits=PS.read_waits(bk))
            PS.read_done(bk, t_ohbv)
            t_hz = vop(lambda h, sems: h.memset(hsel[:, 0:NCV * 3], 0.0), waits=[t_ohbv])
            t_hldp = [None, None]
            for j in range(NCORES):
                hs = hblk[:, (j % 2) * NCV * 3:(j % 2 + 1) * NCV * 3]
                t_hl = dma("sync", hs.rearrange("p (c t) -> p c t", c=NCV),
                           halo_out[j * NCV:(j + 1) * NCV, :, :].transpose([1, 0, 2]),
                           f"hld{j % 2}", waits=[t_hag, t_hldp[j % 2]])
                t_hz = vstt(hsel[:, 0:NCV * 3], hs, misc[:, 32 + j:33 + j],
                            hsel[:, 0:NCV * 3], waits=[t_hl, t_hz])
                t_hldp[j % 2] = t_hz
            t_hsel = t_hz
            t_patch = []
            PH = NCV * 3
            for oc in range(NCV):
                wc = SM_CONVW + oc * 4
                t_p0 = vcopy(hsel[:, PH:PH + 3], hsel[:, oc * 3:oc * 3 + 3],
                             waits=[t_hsel] + t_patch[-1:])
                t_p0 = vcopy(hsel[:, PH + 3:PH + 7], edge[:, oc * 4:oc * 4 + 4],
                             waits=[t_p0])
                pad = hsel[:, PH:PH + 7]
                yA = misc[:, 40:44]
                yB = misc[:, 44:48]
                t_y = vts(yA, pad[:, 3:7], sm[:, wc + 3:wc + 4],
                          sm[:, SM_CONVB + oc:SM_CONVB + oc + 1], ALU.mult, ALU.add,
                          waits=[t_p0])
                t_y = vstt(yB, pad[:, 2:6], sm[:, wc + 2:wc + 3], yA, waits=[t_y])
                t_y = vstt(yA, pad[:, 1:5], sm[:, wc + 1:wc + 2], yB, waits=[t_y])
                t_y = vstt(yB, pad[:, 0:4], sm[:, wc:wc + 1], yA, waits=[t_y])
                t_y = act(misc[:, 48:52], yB, AF.Silu, waits=[t_y])
                qsl = qkv_ap[:, oc * TPC:oc * TPC + 4]
                if oc >= 16:
                    t_w = vcopy(qsl, misc[:, 48:52], waits=[t_y, t_conv_done[oc]])
                else:
                    t_s1 = act(misc[:, 52:56], misc[:, 48:52], AF.Square, waits=[t_y])
                    bap, wps = PS.acquire(bk)
                    t_ssq = mm(psum_t[bk][0:1, 0:4], cst["ones_col"][:, :], misc[:, 52:56],
                               waits=wps + [t_s1])
                    PS.wrote(bk, t_ssq)
                    t_sd2 = act(misc[0:1, 56:60], psum_t[bk][0:1, 0:4], AF.Sqrt,
                                bias=float(L2_EPS), waits=PS.read_waits(bk))
                    PS.read_done(bk, t_sd2)
                    t_rc = vop(lambda h, sems: h.reciprocal(
                        out=misc[0:1, 56:60], in_=misc[0:1, 56:60]), waits=[t_sd2])
                    if oc < 8:
                        t_rc = vts(misc[0:1, 56:60], misc[0:1, 56:60], DK_SCALE,
                                   waits=[t_rc])
                    bap, wps = PS.acquire(bk)
                    t_bc = mm(psum_t[bk][:, 0:4], cst["ones_row"][:, :], misc[0:1, 56:60],
                              waits=wps + [t_rc])
                    PS.wrote(bk, t_bc)
                    t_w = vtt(qsl, misc[:, 48:52], psum_t[bk][:, 0:4], op=ALU.mult,
                              waits=PS.read_waits(bk) + [t_conv_done[oc]])
                    PS.read_done(bk, t_w)
                t_patch.append(t_w)
            t_qkv_done = t_patch[-1]
            if tp_qkv is not None:
                dma("gpsimd", tp_qkv, qkv_ap[:, 0:NCV * TPC], "dbg",
                    waits=[t_qkv_done] + t_conv_done)

            for oc in range(NH):
                slot = oc % 2
                t_wl = wload(wbuf, slot, wz_bf, oc * 128,
                             [wring_consumer[slot], awt["z"]])
                bks = [mm_bank_rot[0], mm_bank_rot[1]]
                tmm = proj_mm(1000 + oc, wz_bf, oc * 128, NH, bks, t_wl)
                wring_consumer[slot] = tmm[-1]
                tze = []
                for th in range(2):
                    te = vts(zrel[:, th * 512:(th + 1) * 512],
                             psum_t[bks[th]][:, 0:512],
                             sm[:, SM_BZ + oc:SM_BZ + oc + 1], None, ALU.add,
                             waits=PS.read_waits(bks[th]) +
                             ([t_zstore[0]] if t_zstore[0] else []) +
                             ([t_zrel_free] if oc == 0 else []))
                    PS.read_done(bks[th], te)
                    tze.append(te)
                t_zstore[0] = dma("gpsimd", zsc[oc, :, :], zrel[:, 0:1024], "ST",
                                  waits=tze)
            t_z_done = [t for t in t_zstore if t is not None]
            if tp_z is not None:
                dma("gpsimd", tp_z, zsc, "dbg", waits=t_z_done)

            gb = pf[0:16, :]
            t_ba = {}
            for which, col0 in (("b", 0), ("a", 16)):
                slot = 0
                t_wl = dma("sync", wbuf[:, 0:NH * 16].rearrange("p (fc c) -> p fc c", fc=NH),
                           wba_bf.rearrange("(fc p) o -> p fc o", p=128)[:, :, col0:col0 + 16],
                           "LD", waits=[wring_consumer[0], awt["ba"]])
                bks = [mm_bank_rot[0], mm_bank_rot[1]]
                toks = []
                for th in range(2):
                    bap, wps = PS.acquire(bks[th])
                    t_last = None
                    for fc in range(NH):
                        w0 = (wps + [t_wl, t_xn_all]) if fc == 0 else None
                        t_last = mm(psum_t[bks[th]][0:16, 0:512],
                                    wbuf[:, fc * 16:(fc + 1) * 16],
                                    xn[:, 0:NH * TPC].rearrange("p (c t) -> p c t", c=NH)
                                    [:, fc, th * 512:(th + 1) * 512],
                                    start=(fc == 0), stop=(fc == NH - 1), waits=w0)
                    PS.wrote(bks[th], t_last)
                    toks.append(t_last)
                wring_consumer[0] = toks[-1]
                dstc = GB_SCR if which == "b" else GB_G
                tev = []
                for th in range(2):
                    te = vcopy(gb[:, dstc + th * 512: dstc + (th + 1) * 512],
                               psum_t[bks[th]][0:16, 0:512],
                               waits=PS.read_waits(bks[th]) + t_pf_free + t_xt_free)
                    PS.read_done(bks[th], te)
                    tev.append(te)
                t_ba[which] = tev[-1]

            bbv = sm[0:16, SM_BB:SM_BB + 1]
            bav = sm[0:16, SM_BA:SM_BA + 1]
            dtv = sm[0:16, SM_DTB:SM_DTB + 1]
            braw = gb[:, GB_SCR:GB_SCR + 1024]
            araw = gb[:, GB_G:GB_G + 1024]
            t_beta = act(gb[:, GB_BETA:GB_BETA + 1024], braw, AF.Sigmoid, bias=bbv,
                         waits=[t_ba["b"]])
            t_p1 = act(gb[:, GB_B:GB_B + 1024], braw, AF.Abs, bias=bbv, waits=[t_ba["b"]])
            t_p2 = act(gb[:, GB_B:GB_B + 1024], gb[:, GB_B:GB_B + 1024], AF.Exp, scale=-1.0,
                       waits=[t_p1])
            t_p3 = act(gb[:, GB_B:GB_B + 1024], gb[:, GB_B:GB_B + 1024], AF.Ln, bias=1.0,
                       waits=[t_p2])
            t_nbb = vts(misc[0:16, 60:61], bbv, -1.0, waits=[t_smalls])
            t_p4 = act(gb[:, GB_NEGB:GB_NEGB + 1024], braw, AF.Relu, scale=-1.0,
                       bias=misc[0:16, 60:61], waits=[t_ba["b"], t_nbb])
            t_pfull = vtt(gb[:, GB_NEGB:GB_NEGB + 1024], gb[:, GB_NEGB:GB_NEGB + 1024],
                          gb[:, GB_B:GB_B + 1024], op=ALU.add, waits=[t_p3, t_p4])
            t_a1 = act(gb[:, GB_BHAT:GB_BHAT + 1024], araw, AF.Abs, bias=dtv,
                       waits=[t_ba["a"]])
            t_a2 = act(gb[:, GB_BHAT:GB_BHAT + 1024], gb[:, GB_BHAT:GB_BHAT + 1024],
                       AF.Exp, scale=-1.0, waits=[t_a1])
            t_a3 = act(gb[:, GB_BHAT:GB_BHAT + 1024], gb[:, GB_BHAT:GB_BHAT + 1024],
                       AF.Ln, bias=1.0, waits=[t_a2])
            t_a4 = act(araw, araw, AF.Relu, bias=dtv, waits=[t_a3])
            t_sp = vtt(gb[:, GB_SCR:GB_SCR + 1024], araw, gb[:, GB_BHAT:GB_BHAT + 1024],
                       op=ALU.add, waits=[t_a4, t_beta, t_p4, t_p1])
            t_eA = act(misc[0:16, 57:58], sm[0:16, SM_ALOG:SM_ALOG + 1], AF.Exp,
                       waits=[t_smalls])
            t_eA = vts(misc[0:16, 57:58], misc[0:16, 57:58], -1.0, waits=[t_eA])
            vsem()
            t_g = vts(gb[:, GB_G:GB_G + 1024], gb[:, GB_SCR:GB_SCR + 1024],
                      misc[0:16, 57:58], None, ALU.mult, waits=[t_sp, t_eA])
            vsem()
            t_b = None
            for cch in range(NCH):
                t_b = vop(lambda h, sems, c=cch: h.tensor_tensor_scan(
                    out=gb[:, GB_B + c * 128:GB_B + (c + 1) * 128],
                    data0=ones16[:, :],
                    data1=gb[:, GB_G + c * 128:GB_G + (c + 1) * 128],
                    initial=0.0, op0=ALU.mult, op1=ALU.add), waits=[t_g, t_pfull])
            t_bh = vtt(gb[:, GB_BHAT:GB_BHAT + 1024], gb[:, GB_B:GB_B + 1024],
                       gb[:, GB_NEGB:GB_NEGB + 1024], op=ALU.subtract,
                       waits=[t_b, t_a3])
            vsem()
            t_nb = vts(gb[:, GB_NEGB:GB_NEGB + 1024], gb[:, GB_B:GB_B + 1024], -1.0,
                       waits=[t_bh])
            t_rows_st = dma("gpsimd", rowsd[:, :], gb[:, 0:3072], "rowsst", waits=[t_nb])
            if tp_gb is not None:
                dma("gpsimd", tp_gb, gb[:, :], "dbg", waits=[t_nb, t_beta, t_g])
            if tp_rows is not None:
                dma("gpsimd", tp_rows, rowsd, "dbg", waits=[t_rows_st])
            t_bcl = None
            P25B = 2
            for cch in range(NCH):
                t_bcl = dma("sync", bcrow[0:1, cch * 16:(cch + 1) * 16],
                            rowsd[:, cch * 128 + 127:cch * 128 + 128].transpose([1, 0]),
                            "bcld", waits=[t_rows_st], slow=True)
            t_sc_tiles = []
            for cch in range(NCH):
                bap, wps = PS.acquire(P25B)
                t_t1 = transp(psum_t[P25B][:, 0:16], gb[:, GB_B + cch * 128:GB_B + (cch + 1) * 128],
                              cst["ident_f32"][0:16, 0:16], waits=wps + [t_nb])
                t_t2 = transp(psum_t[P25B][:, 16:32],
                              gb[:, GB_BETA + cch * 128:GB_BETA + (cch + 1) * 128],
                              cst["ident_f32"][0:16, 0:16], waits=[t_beta])
                PS.wrote(P25B, t_t2)
                t_e1 = vcopy(bTp[:, cch * 16:(cch + 1) * 16], psum_t[P25B][:, 0:16],
                             waits=PS.read_waits(P25B))
                t_e2 = vcopy(bTp[:, 128 + cch * 16:128 + (cch + 1) * 16],
                             psum_t[P25B][:, 16:32], waits=PS.read_waits(P25B))
                PS.read_done(P25B, t_e1)
                PS.read_done(P25B, t_e2)
                t_lam = act(bTp[:, 256 + cch * 16:256 + (cch + 1) * 16],
                            bTp[:, cch * 16:(cch + 1) * 16], AF.Exp, waits=[t_e1])
                t_bl = vtt(bTp[:, 384 + cch * 16:384 + (cch + 1) * 16],
                           bTp[:, 128 + cch * 16:128 + (cch + 1) * 16],
                           bTp[:, 256 + cch * 16:256 + (cch + 1) * 16], op=ALU.mult,
                           waits=[t_lam, t_e2])
                bap, wps = PS.acquire(P25B)
                t_bcb = mm(psum_t[P25B][:, 0:16], cst["ones_row"][:, :],
                           bcrow[0:1, cch * 16:(cch + 1) * 16], waits=wps + [t_bcl])
                PS.wrote(P25B, t_bcb)
                t_lbc = act(bTp[:, 640 + cch * 16:640 + (cch + 1) * 16],
                            psum_t[P25B][:, 0:16], AF.Exp, waits=PS.read_waits(P25B))
                t_kd = vtt(misc[:, 48:64], psum_t[P25B][:, 0:16],
                           bTp[:, cch * 16:(cch + 1) * 16], op=ALU.subtract,
                           waits=PS.read_waits(P25B) + [t_e1, t_lbc])
                PS.read_done(P25B, t_lbc)
                PS.read_done(P25B, t_kd)
                t_kd = act(bTp[:, 512 + cch * 16:512 + (cch + 1) * 16],
                           misc[:, 48:64], AF.Exp, waits=[t_kd])
                t_sc_tiles.append([t_lam, t_bl, t_lbc, t_kd])
            final_toks += [t_sc_tiles[-1][3], t_sc_tiles[-1][0], t_rows_st, t_hag,
                           t_ag_g, t_ag_u, t_ag_d, t_qkv_done] + t_z_done

            if stop_after != "p2":
                t_pc_free = [t_ba["a"], t_ba["b"]]
                WRK = lambda i: pc[:, i * 128:(i + 1) * 128]
                DSTG = lambda par: pc[:, 6144 + par * 512: 6144 + (par + 1) * 512]
                MB_P = lambda h: pc[:, 7680 + h * 512: 7680 + h * 512 + 128]
                MB_PT = lambda h: pc[:, 7680 + h * 512 + 128: 7680 + h * 512 + 256]
                MB_B = lambda h: pc[:, 7680 + h * 512 + 256: 7680 + h * 512 + 384]
                S_KKL, S_KKU, S_KQU, S_KBAR, S_RK, S_RV = 0, 1, 2, 3, 4, 5
                S_A, S_AT, S_LBC, S_LI, S_T1 = 6, 7, 8, 9, 10
                S_PW = 11
                S_X = 23
                S_MT = 27
                SLOT_R = {}
                SLOT_W = {}

                def s_w(slot, extra=None):
                    w = SLOT_R.pop(slot, []) + ([SLOT_W[slot]] if slot in SLOT_W else [])
                    return w + (extra or [])

                def s_wrote(slot, tok):
                    SLOT_W[slot] = tok

                def s_r(slot):
                    return [SLOT_W[slot]] if slot in SLOT_W else []

                def s_read(slot, tok):
                    SLOT_R.setdefault(slot, []).append(tok)

                rb_cons = [None, None]
                first_wrk_wait = list(t_pc_free) + [t_qkv_done] + t_conv_done
                dstg_cons = {0: [], 1: []}
                t_pack_all = []
                hc_pairs = [(h, c) for h in range(HV) for c in range(NCH)]
                if p3_limit is not None:
                    hc_pairs = hc_pairs[:p3_limit]
                for h, c in hc_pairs:
                    qc = h // 2
                    if True:
                        par = (h * NCH + c) % 2
                        idx = c * HV + h
                        qF = qkv_ap[:, qc * TPC + c * 128: qc * TPC + (c + 1) * 128]
                        kF = qkv_ap[:, (8 + qc) * TPC + c * 128: (8 + qc) * TPC + (c + 1) * 128]
                        vF = qkv_ap[:, (16 + h) * TPC + c * 128: (16 + h) * TPC + (c + 1) * 128]
                        colat = lambda r: bTp[:, r * 128 + c * 16 + h: r * 128 + c * 16 + h + 1]
                        t_rb = dma("sync",
                                   rowbuf[0:1, par * 384:(par + 1) * 384].rearrange(
                                       "o (k t) -> o k t", k=3),
                                   rowsd[h:h + 1, :].rearrange(
                                       "o (k t) -> o k t", k=3)[:, :, c * 128:(c + 1) * 128],
                                   f"rb{par}", waits=[rb_cons[par], t_rows_st])
                        brow = rowbuf[0:1, par * 384:par * 384 + 128]
                        bhrow = rowbuf[0:1, par * 384 + 128:par * 384 + 256]
                        nbrow = rowbuf[0:1, par * 384 + 256:par * 384 + 384]
                        b0 = 0 if par == 0 else 1
                        bap, w0 = PS.acquire(b0)
                        t_mm = mm(psum_t[b0][:, 0:128], kF, kF,
                                  waits=w0 + (first_wrk_wait if idx < 2 else []) +
                                  [t_qkv_done])
                        t_mm = mm(psum_t[b0][:, 128:256], kF, qF)
                        PS.wrote(b0, t_mm)
                        bfap, wbf = PS.acquire(7)
                        t_tt = transp(psum_bf[:, 0:128], kF, cst["ident_bf16"][:, :],
                                      waits=wbf)
                        t_tt = transp(psum_bf[:, 128:256], vF, cst["ident_bf16"][:, :])
                        PS.wrote(7, t_tt)
                        if p3_steps is not None and p3_steps <= 1:
                            continue
                        b1 = 2 if par == 0 else 3
                        bap, w1 = PS.acquire(b1)
                        t_ar1 = mm(psum_t[b1][:, 0:128], bhrow, cst["ones_row"][:, :],
                                   start=True, stop=False, waits=w1 + [t_rb, t_consts])
                        t_ar1 = mm(psum_t[b1][:, 0:128], cst["ones_row"][:, :], nbrow,
                                   start=False, stop=True)
                        t_ar2 = mm(psum_t[b1][:, 128:256], nbrow, cst["ones_row"][:, :],
                                   start=True, stop=False)
                        t_ar2 = mm(psum_t[b1][:, 128:256], cst["ones_row"][:, :], bhrow,
                                   start=False, stop=True)
                        t_ar3 = mm(psum_t[b1][:, 256:384], nbrow, cst["ones_row"][:, :],
                                   start=True, stop=False)
                        t_ar3 = mm(psum_t[b1][:, 256:384], cst["ones_row"][:, :], brow,
                                   start=False, stop=True)
                        t_ar4 = mm(psum_t[b1][:, 384:512], cst["ones_row"][:, :], brow,
                                   start=True, stop=True)
                        PS.wrote(b1, t_ar4)
                        rb_cons[par] = t_ar4
                        if p3_steps is not None and p3_steps <= 2:
                            continue
                        t_cl = vts(expf[:, 0:384], psum_t[b1][:, 0:384], 0.0, None,
                                   ALU.min, waits=PS.read_waits(b1))
                        PS.read_done(b1, t_cl)
                        t_eG = act(expf[:, 384:512], psum_t[b1][:, 384:512], AF.Exp,
                                   waits=PS.read_waits(b1) + s_w(S_LBC) + [t_cl])
                        s_wrote(S_LBC, t_eG)
                        PS.read_done(b1, t_eG)
                        t_ex = act(expf[:, 0:384], expf[:, 0:384], AF.Exp, waits=[t_cl])
                        if p3_steps is not None and p3_steps <= 3:
                            PS.read_done(b1, t_eG)
                            continue
                        t_kkl = vtt(WRK(S_KKL), psum_t[b0][:, 0:128], cst["mask_ls"][:, :],
                                    waits=PS.read_waits(b0) + s_w(S_KKL))
                        s_wrote(S_KKL, t_kkl)
                        t_kku = vtt(WRK(S_KKU), psum_t[b0][:, 0:128], cst["mask_us"][:, :],
                                    waits=s_w(S_KKU))
                        s_wrote(S_KKU, t_kku)
                        t_kqu = vtt(WRK(S_KQU), psum_t[b0][:, 128:256], cst["mask_ui"][:, :],
                                    waits=s_w(S_KQU))
                        s_wrote(S_KQU, t_kqu)
                        PS.read_done(b0, t_kqu)
                        t_kb = vts(WRK(S_KBAR), psum_bf[:, 0:128], colat(4),
                                   waits=PS.read_waits(7) + s_w(S_KBAR))
                        s_wrote(S_KBAR, t_kb)
                        t_rk = vts(WRK(S_RK), psum_bf[:, 0:128], colat(3),
                                   waits=s_w(S_RK))
                        s_wrote(S_RK, t_rk)
                        t_rv = vts(WRK(S_RV), psum_bf[:, 128:256], colat(1),
                                   waits=s_w(S_RV))
                        s_wrote(S_RV, t_rv)
                        PS.read_done(7, t_rv)
                        if p3_steps is not None and p3_steps <= 4:
                            continue
                        t_A = vtt(WRK(S_A), expf[:, 0:128], WRK(S_KKL),
                                  waits=[t_ex, t_kkl] + s_w(S_A))
                        s_wrote(S_A, t_A)
                        s_read(S_KKL, t_A)
                        t_AT = vtt(WRK(S_AT), expf[:, 128:256], WRK(S_KKU),
                                   waits=[t_kku] + s_w(S_AT))
                        s_wrote(S_AT, t_AT)
                        s_read(S_KKU, t_AT)
                        t_MT = vtt(WRK(S_MT), expf[:, 256:384], WRK(S_KQU),
                                   waits=[t_kqu] + s_w(S_MT))
                        s_wrote(S_MT, t_MT)
                        s_read(S_KQU, t_MT)
                        if p3_steps is not None and p3_steps <= 5:
                            continue
                        prevA, prevAT = S_A, S_AT
                        sqb = [4, 5]
                        for lvl in range(5):
                            bq = sqb[lvl % 2]
                            bap, wq = PS.acquire(bq)
                            t_s1m = mm(psum_t[bq][:, 0:128], WRK(prevAT), WRK(prevA),
                                       waits=wq + s_r(prevA) + s_r(prevAT))
                            t_s2m = mm(psum_t[bq][:, 128:256], WRK(prevA), WRK(prevAT))
                            s_read(prevA, t_s2m)
                            s_read(prevAT, t_s2m)
                            PS.wrote(bq, t_s2m)
                            sA, sAT = S_PW + 2 * lvl, S_PW + 2 * lvl + 1
                            t_e1 = vcopy(WRK(sA), psum_t[bq][:, 0:128],
                                         waits=PS.read_waits(bq) + s_w(sA))
                            s_wrote(sA, t_e1)
                            t_e2 = vcopy(WRK(sAT), psum_t[bq][:, 128:256],
                                         waits=s_w(sAT))
                            s_wrote(sAT, t_e2)
                            PS.read_done(bq, t_e2)
                            prevA, prevAT = sA, sAT
                        bq = sqb[1]
                        bap, wq = PS.acquire(bq)
                        t_s1m = mm(psum_t[bq][:, 0:128], WRK(prevA), WRK(prevAT),
                                   waits=wq + s_r(prevA) + s_r(prevAT))
                        s_read(prevA, t_s1m)
                        s_read(prevAT, t_s1m)
                        PS.wrote(bq, t_s1m)
                        t_a64 = vcopy(WRK(S_PW + 10), psum_t[bq][:, 0:128],
                                      waits=PS.read_waits(bq) + s_w(S_PW + 10))
                        s_wrote(S_PW + 10, t_a64)
                        PS.read_done(bq, t_a64)
                        if p3_steps is not None and p3_steps <= 6:
                            continue
                        xs = [S_X, S_X + 2]
                        t_x1 = vcopy(WRK(xs[0]), WRK(S_RK), waits=[t_rk] + s_w(xs[0]))
                        s_wrote(xs[0], t_x1)
                        s_read(S_RK, t_x1)
                        t_x2 = vcopy(WRK(xs[0] + 1), WRK(S_RV),
                                     waits=[t_rv] + s_w(xs[0] + 1))
                        s_wrote(xs[0] + 1, t_x2)
                        s_read(S_RV, t_x2)
                        cur = 0
                        apb = [6, 2]
                        fslots = [S_PW + 10, S_PW + 9, S_PW + 7, S_PW + 5,
                                  S_PW + 3, S_PW + 1, S_AT]
                        for fi, fsl in enumerate(fslots):
                            bq = apb[fi % 2]
                            bap, wq = PS.acquire(bq)
                            Xc = pc[:, xs[cur] * 128:(xs[cur] + 2) * 128]
                            t_ap = mm(psum_t[bq][:, 0:256], WRK(fsl), Xc,
                                      waits=wq + s_r(fsl) + s_r(xs[cur]) +
                                      s_r(xs[cur] + 1))
                            s_read(fsl, t_ap)
                            s_read(xs[cur], t_ap)
                            s_read(xs[cur] + 1, t_ap)
                            PS.wrote(bq, t_ap)
                            Xn = pc[:, xs[1 - cur] * 128:(xs[1 - cur] + 2) * 128]
                            ww = (PS.read_waits(bq) + s_w(xs[1 - cur]) +
                                  s_w(xs[1 - cur] + 1))
                            if fi == 6:
                                t_xe = vstt(Xn, psum_t[bq][:, 0:256], -1.0, Xc,
                                            op0=ALU.mult, op1=ALU.add, waits=ww)
                            else:
                                t_xe = vtt(Xn, Xc, psum_t[bq][:, 0:256], op=ALU.add,
                                           waits=ww)
                            s_wrote(xs[1 - cur], t_xe)
                            s_wrote(xs[1 - cur] + 1, t_xe)
                            s_read(xs[cur], t_xe)
                            s_read(xs[cur] + 1, t_xe)
                            PS.read_done(bq, t_xe)
                            cur = 1 - cur
                        sW, sU = xs[cur], xs[cur] + 1
                        if p3_steps is not None and p3_steps <= 7:
                            continue
                        b3 = 4 if par == 0 else 5
                        bap, w3 = PS.acquire(b3)
                        t_oB = mm(psum_t[b3][:, 0:128], WRK(S_KBAR), WRK(sU),
                                  waits=w3 + s_r(S_KBAR) + s_r(sU) + s_r(sW))
                        t_oW = mm(psum_t[b3][:, 128:256], WRK(sW), WRK(S_KBAR))
                        t_oG = mm(psum_t[b3][:, 256:384], WRK(sW), WRK(S_MT))
                        t_oO = mm(psum_t[b3][:, 384:512], WRK(sU), WRK(S_MT))
                        s_read(S_KBAR, t_oW)
                        s_read(S_MT, t_oO)
                        s_read(sW, t_oG)
                        s_read(sU, t_oO)
                        PS.wrote(b3, t_oO)
                        t_li = vts(WRK(S_LI), cst["ident_f32"][:, :], colat(5),
                                   waits=[t_sc_tiles[c][2]] + s_w(S_LI))
                        s_wrote(S_LI, t_li)
                        dcw = dstg_cons[par]
                        t_p1k = vcopy(DSTG(par)[:, 256:384], psum_t[b3][:, 0:128],
                                      waits=PS.read_waits(b3) + dcw)
                        t_p2k = vstt(DSTG(par)[:, 128:256], psum_t[b3][:, 128:256],
                                     -1.0, WRK(S_LI), op0=ALU.mult, op1=ALU.add,
                                     waits=[t_li])
                        s_read(S_LI, t_p2k)
                        t_t1 = vtt(WRK(S_T1), expf[:, 384:512], qF,
                                   waits=[t_eG] + s_w(S_T1))
                        s_wrote(S_T1, t_t1)
                        s_read(S_LBC, t_t1)
                        t_p3k = vstt(DSTG(par)[:, 0:128], psum_t[b3][:, 256:384],
                                     -1.0, WRK(S_T1), op0=ALU.mult, op1=ALU.add)
                        s_read(S_T1, t_p3k)
                        t_p4k = vcopy(DSTG(par)[:, 384:512], psum_t[b3][:, 384:512])
                        PS.read_done(b3, t_p4k)
                        if p3_steps is not None and p3_steps <= 8:
                            PS.read_done(b3, t_p4k)
                            dstg_cons[par] = [t_p4k]
                            continue
                        t_pk = dma("gpsimd", dpk[idx, :, :], DSTG(par),
                                   f"pk{par}", waits=[t_p4k])
                        t_pack_all.append(t_pk)
                        MTsl = DSTG(par)[:, 128:256]
                        if c == 0:
                            bap, wq = PS.acquire(7)
                            t_tp0 = transp(psum_bf[:, 256:384], MTsl,
                                           cst["ident_bf16"][:, :],
                                           waits=wq + [t_p2k])
                            PS.wrote(7, t_tp0)
                            t_cmp = vcopy(MB_P(h), psum_bf[:, 256:384],
                                          waits=PS.read_waits(7))
                            PS.read_done(7, t_cmp)
                            t_cb = vcopy(MB_B(h), DSTG(par)[:, 256:384], waits=[t_p1k])
                            comp_tok = [t_cmp, t_cb, t_tp0]
                        else:
                            bq = 6
                            bap, wq = PS.acquire(bq)
                            t_c1 = mm(psum_t[bq][:, 0:128], MTsl, MB_P(h),
                                      waits=wq + comp_tok + [t_p2k])
                            t_c2 = mm(psum_t[bq][:, 128:256], MTsl, MB_B(h))
                            PS.wrote(bq, t_c2)
                            t_cmp = vcopy(MB_P(h), psum_t[bq][:, 0:128],
                                          waits=PS.read_waits(bq))
                            t_cb = vtt(MB_B(h), psum_t[bq][:, 128:256],
                                       DSTG(par)[:, 256:384], op=ALU.add,
                                       waits=[t_p1k])
                            PS.read_done(bq, t_cb)
                            comp_tok = [t_cmp, t_cb, t_c2]
                        dstg_cons[par] = [t_pk, t_cb] + ([t_c2] if c > 0 else [t_tp0])
                        if c == NCH - 1:
                            bap, wq = PS.acquire(7)
                            t_tpf = transp(psum_bf[:, 256:384], MB_P(h),
                                           cst["ident_bf16"][:, :],
                                           waits=wq + comp_tok)
                            PS.wrote(7, t_tpf)
                            t_ptf = vcopy(MB_PT(h), psum_bf[:, 256:384],
                                          waits=PS.read_waits(7))
                            PS.read_done(7, t_ptf)
                            mb_done_h = [t_ptf, t_cb]
                if p3_limit is not None:
                    final_toks += t_pack_all[-2:]
                    stop_after = "p3cut"
                mbsrc = None if p3_limit is not None else pc[:, 7680:7680 + HV * 512].rearrange(
                    "i (hh w j) -> i hh w j", hh=HV, w=4)
                if p3_limit is None:
                    mbdst = mb_in.rearrange("(hh w i) j -> i hh w j", hh=HV, w=2, i=128)
                    t_mbst = dma("gpsimd", mbdst[:, :, 0, :], mbsrc[:, :, 1, :],
                                 "mbst", waits=mb_done_h)
                    t_mbst = dma("gpsimd", mbdst[:, :, 1, :], mbsrc[:, :, 2, :],
                                 "mbst", waits=mb_done_h)
                    t_ag_mb = coll("AllGather", ALU.bypass, [mb_in], [mb_out], [t_mbst])
                    final_toks += [t_ag_mb] + t_pack_all[-2:]
                    if tp_dpk is not None:
                        dma("gpsimd", tp_dpk, dpk, "dbg", waits=t_pack_all[-2:])
                    if tp_mb is not None:
                        dma("gpsimd", tp_mb, mb_in, "dbg", waits=[t_mbst])

        did_full = False
        if stop_after is None:
            stream_tok = lambda s: Tok(S.stream_map[s], S.cnt[S.stream_map[s]])
            pk_fin = [stream_tok("pk0"), stream_tok("pk1")]
            pbxq = pb[:, 0:NH * TPC].rearrange("p (c t) -> p c t", c=NH)
            t_xql = dma("sync", pb[:, 0:NH * TPC].rearrange("p (c t) -> p c t", c=NH),
                        xqh.transpose([1, 0, 2]), "xqld",
                        waits=pk_fin + [t_xq_store[-1]])
            SG, SU, SD = 28, 29, 30
            t_sbc = None
            for bh in range(2):
                bap, w6 = PS.acquire(6)
                t_b6 = mm(psum_t[6][:, 0:512], cst["ones_row"][:, :],
                          srl[0:1, bh * 512:(bh + 1) * 512],
                          waits=w6 + [t_srow_all, t_consts])
                PS.wrote(6, t_b6)
                t_sbc = vcopy(cvo2[:, bh * 512:(bh + 1) * 512],
                              psum_t[6][:, 0:512], waits=PS.read_waits(6))
                PS.read_done(6, t_sbc)
            wr_cons = [wring_consumer[0], wring_consumer[1]]
            wr2_cons = [None, None]
            s1_stage = [relo, zrel]
            s1_cons = [None, t_zstore[0]]
            prev_tin = [None, None]
            prev_s1 = [None, None]
            gb_banks = (0, 1)
            ub_banks = (2, 3)
            s1b = (4, 5)
            t_s1_last = None
            for s in range(NFI):
                sl = s % 2
                t_wg = wload(wbuf, sl, agg_out, s * 128, [wr_cons[sl], t_ag_g], nk=16)
                t_wu = wload(wbuf2, sl, agu_out, s * 128, [wr2_cons[sl], t_ag_u], nk=16)
                stg = s1_stage[s % 2]
                t_evs = []
                for th in range(2):
                    bg, bu = gb_banks[th], ub_banks[th]
                    bap, wg_ = PS.acquire(bg)
                    t_lg = None
                    for fc in range(NH):
                        w0 = (wg_ + [t_wg] + ([t_xql] if s == 0 else [])) \
                            if fc == 0 else None
                        t_lg = mm(psum_t[bg][:, 0:512],
                                  wbuf[:, sl * 2048 + fc * 128:sl * 2048 + (fc + 1) * 128],
                                  pbxq[:, fc, th * 512:(th + 1) * 512],
                                  start=(fc == 0), stop=(fc == NH - 1), waits=w0)
                    PS.wrote(bg, t_lg)
                    bap, wu_ = PS.acquire(bu)
                    t_lu = None
                    for fc in range(NH):
                        w0 = (wu_ + [t_wu] + ([t_xql] if s == 0 else [])) \
                            if fc == 0 else None
                        t_lu = mm(psum_t[bu][:, 0:512],
                                  wbuf2[:, sl * 2048 + fc * 128:sl * 2048 + (fc + 1) * 128],
                                  pbxq[:, fc, th * 512:(th + 1) * 512],
                                  start=(fc == 0), stop=(fc == NH - 1), waits=w0)
                    PS.wrote(bu, t_lu)
                    wr_cons[sl] = t_lg
                    wr2_cons[sl] = t_lu
                    t_pg = vtt(tmB[:, th * 512:(th + 1) * 512], psum_t[bg][:, 0:512],
                               cvo2[:, th * 512:(th + 1) * 512], op=ALU.mult,
                               waits=PS.read_waits(bg) +
                               ([t_sbc] if s == 0 else []) + [prev_tin[th]])
                    PS.read_done(bg, t_pg)
                    t_sg_ = act(tmA[:, th * 512:(th + 1) * 512],
                                tmB[:, th * 512:(th + 1) * 512],
                                AF.Silu, scale=misc[:, SG:SG + 1],
                                waits=[t_pg, t_sgv])
                    t_up = vtt(tmB[:, th * 512:(th + 1) * 512], psum_t[bu][:, 0:512],
                               cvo2[:, th * 512:(th + 1) * 512], op=ALU.mult,
                               waits=PS.read_waits(bu) + [t_sg_])
                    t_up = vts(tmB[:, th * 512:(th + 1) * 512],
                               tmB[:, th * 512:(th + 1) * 512],
                               misc[:, SU:SU + 1], None, ALU.mult, waits=[t_sgv])
                    PS.read_done(bu, t_up)
                    t_in = vtt(intm[:, th * 512:(th + 1) * 512],
                               tmA[:, th * 512:(th + 1) * 512],
                               tmB[:, th * 512:(th + 1) * 512], op=ALU.mult,
                               waits=[t_sg_, prev_s1[th]])
                    prev_tin[th] = t_in
                    bap, ws1 = PS.acquire(s1b[th])
                    t_s1 = mm(psum_t[s1b[th]][:, 0:512], cst["h128_bf"][:, :],
                              intm[:, th * 512:(th + 1) * 512],
                              waits=ws1 + [t_in, t_consts])
                    PS.wrote(s1b[th], t_s1)
                    prev_s1[th] = t_s1
                    t_s1_last = t_s1
                    t_ev = vcopy(stg[:, th * 512:(th + 1) * 512],
                                 psum_t[s1b[th]][:, 0:512],
                                 waits=PS.read_waits(s1b[th]) +
                                 ([s1_cons[s % 2]] if th == 0 else []))
                    t_lo = vstt(yq_t[:, th * 512:(th + 1) * 512],
                                stg[:, th * 512:(th + 1) * 512], -1.0,
                                psum_t[s1b[th]][:, 0:512], op0=ALU.mult,
                                op1=ALU.add,
                                waits=[t_ev, s1_cons[s % 2]] if th == 0 else [t_ev])
                    PS.read_done(s1b[th], t_lo)
                    t_evs.append(t_lo)
                t_zs = dma("gpsimd", zh[s, :, :], stg[:, 0:1024], "zst",
                           waits=t_evs)
                s1_cons[s % 2] = dma("gpsimd", zhl[s, :, :], yq_t[:, 0:1024], "zst",
                                     waits=t_evs)
            if tp_zh is not None:
                dma("gpsimd", tp_zh, zh, "dbg", waits=[s1_cons[0], s1_cons[1]])

            zst_fin = stream_tok("zst")
            gb_dead = [t_rows_st] + [t for sub in t_sc_tiles for t in sub]
            s2_cons = [s1_cons[0], s1_cons[1]]
            t_yq = [None] * NFI
            dwr_cons = [wr_cons[0], wr_cons[1]]
            ff_cons = [None, None]
            t_quantv_last = None
            t_trp_last = None
            t_ff_last = None
            srl_cons = None
            agdv = agd_out.rearrange("(s g i e) m -> e g s i m", s=64, g=4, i=16)
            t_dn_last = None
            for th in range(2):
                t_mx = None
                for ap in range(NFI):
                    sl = ap % 2
                    stg = s1_stage[sl]
                    wz = [s2_cons[sl]] + ([zst_fin] if ap < 2 else [])
                    t_l0 = dma("sync",
                               stg[0:64, 0:512].rearrange("p (x t) -> p x t", x=1),
                               zh[0:64, 2 * ap:2 * ap + 1, th * 512:(th + 1) * 512],
                               "z2a", waits=wz)
                    t_l1 = dma("sync",
                               stg[64:128, 0:512].rearrange("p (x t) -> p x t", x=1),
                               zh[0:64, 2 * ap + 1:2 * ap + 2, th * 512:(th + 1) * 512],
                               "z2b", waits=wz)
                    t_l2 = dma("sync",
                               stg[0:64, 512:1024].rearrange("p (x t) -> p x t", x=1),
                               zhl[0:64, 2 * ap:2 * ap + 1, th * 512:(th + 1) * 512],
                               "z2c", waits=wz)
                    t_l3 = dma("sync",
                               stg[64:128, 512:1024].rearrange("p (x t) -> p x t", x=1),
                               zhl[0:64, 2 * ap + 1:2 * ap + 2, th * 512:(th + 1) * 512],
                               "z2d", waits=wz)
                    bk = ap % 2
                    bap, w2 = PS.acquire(bk)
                    t_mm2 = mm(psum_t[bk][:, 0:512], cst["h64blk_bf"][:, :],
                               stg[:, 0:512], start=True, stop=False,
                               waits=w2 + [t_l0, t_l1, t_l2, t_l3])
                    t_mm2 = mm(psum_t[bk][:, 0:512], cst["h64blk_bf"][:, :],
                               stg[:, 512:1024], start=False, stop=True)
                    PS.wrote(bk, t_mm2)
                    s2_cons[sl] = t_mm2
                    t_ab = act(pf[:, 3584:4096], psum_t[bk][:, 0:512],
                               AF.Abs, waits=PS.read_waits(bk) + [t_mx] +
                               (gb_dead if (th == 0 and ap == 0) else []))
                    PS.read_done(bk, t_ab)
                    if ap == 0:
                        t_mx = vcopy(pf[:, 3072:3584], pf[:, 3584:4096],
                                     waits=[t_ab, t_trp_last])
                    else:
                        t_mx = vtt(pf[:, 3072:3584], pf[:, 3072:3584],
                                   pf[:, 3584:4096], op=ALU.max, waits=[t_ab])
                bap, wtp = PS.acquire(2)
                t_tp = None
                for b in range(4):
                    t_tp = transp(psum_t[2][:, b * 128:(b + 1) * 128],
                                  pf[:, 3072 + b * 128:3072 + (b + 1) * 128],
                                  cst["ident_f32"][:, :],
                                  waits=(wtp + [t_mx]) if b == 0 else None)
                PS.wrote(2, t_tp)
                t_trp_last = t_tp
                t_cpy = vcopy(mixc[:, 0:512], psum_t[2][:, 0:512],
                              waits=PS.read_waits(2) + ff_cons)
                PS.read_done(2, t_cpy)
                t_trs = []
                for b in range(4):
                    t_tr = tree_reduce(misc[:, 4 + b:5 + b],
                                       mixc[:, b * 128:(b + 1) * 128],
                                       cvo, cvo2, 128, op=ALU.max,
                                       waits=[t_cpy])
                    t_trs.append(t_tr)
                t_qs = dma("gpsimd", qsc[:, 0:4], misc[:, 4:8], "qss", waits=t_trs)
                t_qls = []
                for b in range(4):
                    t_qls.append(dma("sync", srl[0:1, b * 128:(b + 1) * 128],
                                     qsc[:, b:b + 1].transpose([1, 0]), "qsl",
                                     waits=[t_qs, srl_cons], slow=True))
                t_rm = vts(srl[0:1, 512:1024], srl[0:1, 0:512], float(Q_EPS), None,
                           ALU.max, waits=t_qls)
                t_rs = vop(lambda h, sems: h.reciprocal(
                    out=srl[0:1, 1024:1536], in_=srl[0:1, 512:1024]), waits=[t_rm])
                bap, wb3 = PS.acquire(3)
                t_b3 = mm(psum_t[3][:, 0:512], cst["ones_row"][:, :],
                          srl[0:1, 1024:1536], waits=wb3 + [t_rs])
                PS.wrote(3, t_b3)
                t_bcs = vts(pf[:, 0:512], psum_t[3][:, 0:512], 128.0, None, ALU.mult,
                            waits=PS.read_waits(3))
                PS.read_done(3, t_bcs)
                t_bi = vts(srl[0:1, 0:512], srl[0:1, 512:1024], INVSQ / 128.0,
                           waits=[t_bcs])
                bap, wb3 = PS.acquire(3)
                t_b3 = mm(psum_t[3][:, 0:512], cst["ones_row"][:, :],
                          srl[0:1, 0:512], waits=wb3 + [t_bi])
                PS.wrote(3, t_b3)
                t_bci = vts(pf[:, 512:1024], psum_t[3][:, 0:512], misc[:, SD:SD + 1],
                            None, ALU.mult, waits=PS.read_waits(3) + [t_sgv])
                PS.read_done(3, t_bci)
                srl_cons = t_b3
                pb_dead = [t_s1_last if th == 0 else t_dn_last]
                for ap in range(NFI):
                    pbt = pb[:, ap * 512:(ap + 1) * 512]
                    sl = ap % 2
                    stg = s1_stage[sl]
                    t_l0 = dma("sync",
                               stg[0:64, 0:512].rearrange("p (x t) -> p x t", x=1),
                               zh[0:64, 2 * ap:2 * ap + 1, th * 512:(th + 1) * 512],
                               "z2a", waits=[s2_cons[sl]])
                    t_l1 = dma("sync",
                               stg[64:128, 0:512].rearrange("p (x t) -> p x t", x=1),
                               zh[0:64, 2 * ap + 1:2 * ap + 2, th * 512:(th + 1) * 512],
                               "z2b")
                    t_l2 = dma("sync",
                               stg[0:64, 512:1024].rearrange("p (x t) -> p x t", x=1),
                               zhl[0:64, 2 * ap:2 * ap + 1, th * 512:(th + 1) * 512],
                               "z2c", waits=[s2_cons[sl]])
                    t_l3 = dma("sync",
                               stg[64:128, 512:1024].rearrange("p (x t) -> p x t", x=1),
                               zhl[0:64, 2 * ap + 1:2 * ap + 2, th * 512:(th + 1) * 512],
                               "z2d")
                    bk = ap % 2
                    bap, w2 = PS.acquire(bk)
                    t_mm2 = mm(psum_t[bk][:, 0:512], cst["h64blk_bf"][:, :],
                               stg[:, 0:512], start=True, stop=False,
                               waits=w2 + [t_l0, t_l1, t_l2, t_l3])
                    t_mm2 = mm(psum_t[bk][:, 0:512], cst["h64blk_bf"][:, :],
                               stg[:, 512:1024], start=False, stop=True)
                    PS.wrote(bk, t_mm2)
                    s2_cons[sl] = t_mm2
                    t_v0 = vtt(tmA[:, 0:512], psum_t[bk][:, 0:512], pf[:, 0:512],
                               op=ALU.mult, waits=PS.read_waits(bk) + [t_bcs])
                    PS.read_done(bk, t_v0)
                    t_v1 = vts(tmA[:, 512:1024], tmA[:, 0:512], 127.0, 128.0,
                               ALU.min, ALU.add, waits=[t_v0])
                    t_iI = vcopy(tmI[:, 0:512], tmA[:, 512:1024], waits=[t_v1])
                    t_f = vcopy(tmB[:, 0:512], tmI[:, 0:512],
                                waits=[t_iI, t_yq[ap - 1] if ap else None])
                    t_d = vtt(tmA[:, 0:512], tmA[:, 512:1024], tmB[:, 0:512],
                              op=ALU.subtract, waits=[t_f])
                    t_c = vts(tmB[:, 512:1024], tmA[:, 0:512], 0.5, None, ALU.is_ge,
                              waits=[t_d])
                    t_r = vtt(tmB[:, 0:512], tmB[:, 0:512], tmB[:, 512:1024],
                              op=ALU.add, waits=[t_c])
                    t_quantv_last = t_r
                    t_yq[ap] = act(pbt, tmB[:, 0:512], AF.Copy, bias=-128.0,
                                   waits=[t_r] + pb_dead)
                dnb = (4, 5)
                for m in range(NH):
                    bk = dnb[m % 2]
                    bap, wd = PS.acquire(bk)
                    t_lm = None
                    for g in range(4):
                        sl = (m * 4 + g) % 2
                        t_w1 = dma("sync",
                                   wbuf[0:64, sl * 2048:(sl + 1) * 2048]
                                   .rearrange("p (i c) -> p i c", i=16),
                                   agdv[0, g, :, :, m * 128:(m + 1) * 128], "LD",
                                   waits=[dwr_cons[sl], t_ag_d])
                        t_w2 = dma("sync",
                                   wbuf[64:128, sl * 2048:(sl + 1) * 2048]
                                   .rearrange("p (i c) -> p i c", i=16),
                                   agdv[1, g, :, :, m * 128:(m + 1) * 128], "LD")
                        for i in range(16):
                            ap = g * 16 + i
                            w0 = [t_w1, t_w2, t_yq[ap]]
                            if g == 0 and i == 0:
                                w0 += wd
                            t_lm = mm(psum_t[bk][:, 0:512],
                                      wbuf[:, sl * 2048 + i * 128:sl * 2048 + (i + 1) * 128],
                                      pb[:, ap * 512:(ap + 1) * 512],
                                      start=(g == 0 and i == 0),
                                      stop=(g == 3 and i == 15), waits=w0)
                        dwr_cons[sl] = t_lm
                    PS.wrote(bk, t_lm)
                    fstg = [cvo, cvo2][m % 2]
                    t_ff = vtt(fstg[:, 0:512], psum_t[bk][:, 0:512], pf[:, 512:1024],
                               op=ALU.mult, waits=PS.read_waits(bk) + [ff_cons[m % 2]])
                    PS.read_done(bk, t_ff)
                    t_ff_last = t_ff
                    t_dn_last = t_lm
                    ff_cons[m % 2] = dma("gpsimd", ffnh[m, :, th * 512:(th + 1) * 512],
                                         fstg[:, 0:512], "fst", waits=[t_ff])
            fst_fin = stream_tok("fst")
            t_dbg_ffn = None
            if tp_ffn is not None:
                t_dbg_ffn = dma("gpsimd", tp_ffn, ffnh, "dbg", waits=[fst_fin])

            xqst_fin = stream_tok("xqst")
            zst_fin2 = stream_tok("zst")
            t_ohi = []
            for j in range(1, 8):
                t_ohi.append(vts(intm[:, (j - 1) * 128:j * 128],
                                 cst["ident_bf16"][:, :],
                                 sm[:, SM_OHB + j:SM_OHB + j + 1], None, ALU.mult,
                                 waits=[t_smalls, t_consts, t_s1_last]))
            mb2_cons = [None, None]
            rr_cons = [zst_fin2, zst_fin2]
            t_s0 = [None] * HV
            for hh in range(HV):
                t_rl = dma("sync", yq_t[:, 0:128],
                           mb_out[hh * 256 + 128:hh * 256 + 256, :], "mbl",
                           waits=[t_ag_mb, rr_cons[0]])
                bap, w6 = PS.acquire(6)
                t_sel = mm(psum_t[6][:, 0:128], intm[:, 0:128], yq_t[:, 0:128],
                           start=True, stop=False,
                           waits=w6 + [t_ohi[-1], t_rl])
                rcur = 0
                for j in range(1, 7):
                    sl = j % 2
                    t_pl = dma("sync", wbuf2[:, sl * 2048:sl * 2048 + 128],
                               mb_out[j * 4096 + hh * 256:j * 4096 + hh * 256 + 128, :],
                               "mbl", waits=[mb2_cons[sl]])
                    t_bl = dma("sync", wbuf2[:, sl * 2048 + 128:sl * 2048 + 256],
                               mb_out[j * 4096 + hh * 256 + 128:j * 4096 + hh * 256 + 256, :],
                               "mbl")
                    bk = 4 + (j % 2)
                    bap, wp = PS.acquire(bk)
                    t_pr = mm(psum_t[bk][:, 0:128], wbuf2[:, sl * 2048:sl * 2048 + 128],
                              yq_t[:, rcur * 128:(rcur + 1) * 128],
                              waits=wp + [t_pl])
                    PS.wrote(bk, t_pr)
                    rnew = 1 - rcur
                    t_rn = vtt(yq_t[:, rnew * 128:(rnew + 1) * 128],
                               psum_t[bk][:, 0:128],
                               wbuf2[:, sl * 2048 + 128:sl * 2048 + 256], op=ALU.add,
                               waits=PS.read_waits(bk) + [t_bl, rr_cons[rnew]])
                    PS.read_done(bk, t_rn)
                    mb2_cons[sl] = t_rn
                    t_sel = mm(psum_t[6][:, 0:128], intm[:, j * 128:(j + 1) * 128],
                               yq_t[:, rnew * 128:(rnew + 1) * 128],
                               start=False, stop=(j == 6), waits=[t_rn])
                    rr_cons[rcur] = t_sel
                    rcur = rnew
                PS.wrote(6, t_sel)
                t_s0[hh] = vcopy(s0b[:, hh * 128:(hh + 1) * 128], psum_t[6][:, 0:128],
                                 waits=PS.read_waits(6))
                PS.read_done(6, t_s0[hh])
                rr_cons[rcur] = t_s0[hh]
            coreb_dead = pk_fin + [t_mbst, t_ag_mb]
            swp_cons = [dwr_cons[0], dwr_cons[1]]
            t_core_h = {hh: [] for hh in range(HV)}
            for hh in range(HV):
                t_state = t_s0[hh]
                for c in range(NCH):
                    idx = c * HV + hh
                    sl = (hh * NCH + c) % 2
                    t_pl = dma("sync", wbuf[:, sl * 2048:sl * 2048 + 512],
                               dpk[idx, :, :], "pkl",
                               waits=[swp_cons[sl]] + (pk_fin if idx < 2 else []))
                    GT = wbuf[:, sl * 2048:sl * 2048 + 128]
                    MTt = wbuf[:, sl * 2048 + 128:sl * 2048 + 256]
                    Bt = wbuf[:, sl * 2048 + 256:sl * 2048 + 384]
                    OTt = wbuf[:, sl * 2048 + 384:sl * 2048 + 512]
                    bap, wo = PS.acquire(sl)
                    t_mo = mm(psum_t[sl][:, 0:128], s0b[:, hh * 128:(hh + 1) * 128],
                              GT, waits=wo + [t_pl, t_state])
                    PS.wrote(sl, t_mo)
                    bap, ws_ = PS.acquire(2 + sl)
                    t_ms = mm(psum_t[2 + sl][:, 0:128], MTt,
                              s0b[:, hh * 128:(hh + 1) * 128], waits=ws_)
                    PS.wrote(2 + sl, t_ms)
                    t_co = vtt(coreb[:, hh * TPC + c * 128:hh * TPC + (c + 1) * 128],
                               psum_t[sl][:, 0:128], OTt, op=ALU.add,
                               waits=PS.read_waits(sl) +
                               (coreb_dead if (hh == 0 and c == 0) else []))
                    PS.read_done(sl, t_co)
                    t_core_h[hh].append(t_co)
                    t_state = vtt(s0b[:, hh * 128:(hh + 1) * 128],
                                  psum_t[2 + sl][:, 0:128], Bt, op=ALU.add,
                                  waits=PS.read_waits(2 + sl) + [t_mo])
                    PS.read_done(2 + sl, t_state)
                    swp_cons[sl] = t_state
            if tp_core is not None:
                dma("gpsimd", tp_core, coreb[:, 0:HV * TPC], "dbg",
                    waits=[t for hh in range(HV) for t in t_core_h[hh][-1:]])

            zrl_cons = [s2_cons[0], s2_cons[1]]
            t_gated = []
            tmA_dead = [t_quantv_last, t_yq[-1]]
            cvo_dead = [t_ff_last, fst_fin]
            t_sz_prev = None
            t_vm_prev = None
            for hh in range(HV):
                ring = s1_stage[hh % 2]
                t_zl = dma("sync", ring[:, 0:1024], zsc[hh, :, :], "zrl",
                           waits=[zrl_cons[hh % 2]] +
                           (t_z_done if hh < 2 else []))
                cb_h = coreb[:, hh * TPC:(hh + 1) * TPC]
                t_sq = act(tmA[:, 0:1024], cb_h, AF.Square,
                           waits=t_core_h[hh] + [t_vm_prev] +
                           (tmA_dead if hh == 0 else []))
                t_var = None
                for th in range(2):
                    bap, w6 = PS.acquire(6)
                    t_vm = mm(psum_t[6][0:1, 0:512], cst["ones_col"][:, :],
                              tmA[:, th * 512:(th + 1) * 512],
                              waits=w6 + [t_sq])
                    PS.wrote(6, t_vm)
                    t_vm_prev = t_vm
                    t_var = act(srl[0:1, th * 512:(th + 1) * 512],
                                psum_t[6][0:1, 0:512], AF.Sqrt,
                                scale=1.0 / 128.0, bias=float(RMS_EPS),
                                waits=PS.read_waits(6))
                    PS.read_done(6, t_var)
                t_rv = vop(lambda h, sems: h.reciprocal(
                    out=srl[0:1, 0:1024], in_=srl[0:1, 0:1024]), waits=[t_var])
                for th in range(2):
                    bap, w3 = PS.acquire(3)
                    t_bc = mm(psum_t[3][:, 0:512], cst["ones_row"][:, :],
                              srl[0:1, th * 512:(th + 1) * 512], waits=w3 + [t_rv])
                    PS.wrote(3, t_bc)
                    t_g1 = vtt(tmB[:, th * 512:(th + 1) * 512],
                               cb_h[:, th * 512:(th + 1) * 512],
                               psum_t[3][:, 0:512], op=ALU.mult,
                               waits=PS.read_waits(3))
                    PS.read_done(3, t_g1)
                t_g2 = vts(tmB[:, 0:1024], tmB[:, 0:1024],
                           sm[:, SM_NORMW:SM_NORMW + 1], None, ALU.mult,
                           waits=[t_g1])
                t_sz = act(cvo[:, 0:1024], ring[:, 0:1024], AF.Silu,
                           waits=[t_zl, t_g2] + (cvo_dead if hh == 0 else []) +
                           [t_sz_prev])
                t_sz_prev = t_sz
                t_gf = vtt(cb_h, tmB[:, 0:1024], cvo[:, 0:1024], op=ALU.mult,
                           waits=[t_sz])
                zrl_cons[hh % 2] = t_gf
                t_gated.append(t_gf)
            wout_cons = [swp_cons[0], swp_cons[1]]
            frel_dead = [t_trp_last, t_ff_last, t_bci]
            frel_cons = [None, None]
            oA_cons = [None, None]
            t_fs2 = []
            for m in range(NH):
                sl = m % 2
                t_wl = wload(wbuf, sl, wout_bf, m * 128,
                             [wout_cons[sl], awt["out"]])
                t_fl = dma("sync", pf[:, PF_FREL + sl * 1024:PF_FREL + (sl + 1) * 1024],
                           ffnh[m, :, :], "frl",
                           waits=[frel_cons[sl], fst_fin, t_dbg_ffn] +
                           (frel_dead if m < 2 else []))
                oba = tmA[:, sl * 1024:(sl + 1) * 1024]
                t_e2s = []
                for th in range(2):
                    bk = 4 + th
                    bap, wk = PS.acquire(bk)
                    t_lm = None
                    for fc in range(NH):
                        w0 = (wk + [t_wl] + t_gated) if fc == 0 else None
                        t_lm = mm(psum_t[bk][:, 0:512],
                                  wbuf[:, sl * 2048 + fc * 128:sl * 2048 + (fc + 1) * 128],
                                  coreb[:, fc * TPC + th * 512:fc * TPC + (th + 1) * 512],
                                  start=(fc == 0), stop=(fc == NH - 1), waits=w0)
                    PS.wrote(bk, t_lm)
                    wout_cons[sl] = t_lm
                    t_e1 = vts(oba[:, th * 512:(th + 1) * 512], psum_t[bk][:, 0:512],
                               sm[:, SM_BOUT + m:SM_BOUT + m + 1], None, ALU.add,
                               waits=PS.read_waits(bk) + [oA_cons[sl]])
                    PS.read_done(bk, t_e1)
                    t_e2 = vtt(oba[:, th * 512:(th + 1) * 512],
                               oba[:, th * 512:(th + 1) * 512],
                               pf[:, PF_FREL + sl * 1024 + th * 512:
                                  PF_FREL + sl * 1024 + (th + 1) * 512],
                               op=ALU.add, waits=[t_fl])
                    t_e2s.append(t_e2)
                t_st = dma("gpsimd", ffnh[m, :, :], oba[:, 0:1024], "fs2",
                           waits=t_e2s)
                frel_cons[sl] = t_e2s[-1]
                oA_cons[sl] = t_st
                t_fs2.append(t_st)
            fs2_fin = stream_tok("fs2")
            xre_dead = [t_ba["a"], t_ba["b"]]
            otm_cons = None
            mx_cons = None
            t_fin = []
            for tb in range(8):
                t_xl = dma("sync", pf[:, PF_XRE:PF_XRE + 2048],
                           x_d[tb * 128:(tb + 1) * 128, :], "xrl",
                           waits=(xre_dead if tb == 0 else []) + [otm_cons])
                t_os = []
                for mg in range(2):
                    t_mls = []
                    for mi in range(8):
                        m = mg * 8 + mi
                        t_ml = dma("sync", mixc[:, mi * 128:(mi + 1) * 128],
                                   ffnh[m, :, tb * 128:(tb + 1) * 128], "ml2",
                                   waits=[mx_cons] +
                                   ([fs2_fin] if (tb == 0 and mi == 0) else []))
                        t_mls.append(t_ml)
                    for bi in range(2):
                        bk = 2 + bi
                        bap, wk = PS.acquire(bk)
                        t_tp = None
                        for slot in range(4):
                            mi = bi * 4 + slot
                            t_tp = transp(psum_t[bk][:, slot * 128:(slot + 1) * 128],
                                          mixc[:, mi * 128:(mi + 1) * 128],
                                          cst["ident_f32"][:, :],
                                          waits=(wk + t_mls) if slot == 0 else None)
                        PS.wrote(bk, t_tp)
                        for slot in range(4):
                            m = mg * 8 + bi * 4 + slot
                            t_o = vtt(pf[:, PF_OTM + m * 128:PF_OTM + (m + 1) * 128],
                                      psum_t[bk][:, slot * 128:(slot + 1) * 128],
                                      pf[:, PF_XRE + m * 128:PF_XRE + (m + 1) * 128],
                                      op=ALU.add,
                                      waits=PS.read_waits(bk) + [t_xl])
                            PS.read_done(bk, t_o)
                            t_os.append(t_o)
                    mx_cons = t_tp
                t_fo = dma("gpsimd", out_d[tb * 128:(tb + 1) * 128, :],
                           pf[:, PF_OTM:PF_OTM + 2048], "outst", waits=t_os)
                otm_cons = t_fo
                t_fin.append(t_fo)
            did_full = True

        if not did_full:
            if hasattr(S, "stream_map") and "dbg" in S.stream_map:
                dsem = S.stream_map["dbg"]
                final_toks.append(Tok(dsem, S.cnt.get(dsem, 0)))
            t_fin = []
            for i in range(8):
                t_zz = vop(lambda h, sems: h.memset(pf[:, PF_OTM:PF_OTM + 2048], 0.0),
                           waits=(final_toks if i == 0 else [t_fin[-1]]))
                t_fo = dma("gpsimd", out_d[i * 128:(i + 1) * 128, :],
                           pf[:, PF_OTM:PF_OTM + 2048], "outst", waits=[t_zz])
                t_fin.append(t_fo)
        for sem_name in list(S.cnt.keys()):
            if sem_name.startswith("D") or sem_name == "CC":
                v = S.cnt[sem_name]
                S.emit("gpsimd", lambda h, sems, s2=sem_name, v=v:
                       h.wait_ge(sems[s2], v))
        S.emit("gpsimd", lambda h, sems: h.nop())

        S.finalize_check()

        with nc.Block() as block:
            @block.sync
            def _(h):
                for fn in S.ops["sync"]:
                    fn(h, sems)

            @block.tensor
            def _(h):
                for fn in S.ops["tensor"]:
                    fn(h, sems)

            @block.vector
            def _(h):
                for fn in S.ops["vector"]:
                    fn(h, sems)

            @block.scalar
            def _(h):
                for fn in S.ops["scalar"]:
                    fn(h, sems)

            @block.gpsimd
            def _(h):
                for fn in S.ops["gpsimd"]:
                    fn(h, sems)

    return nc, dbg_names


def make_inputs(inputs):
    x = np.asarray(inputs["L0_attn_hidden_states"])[0]
    T = lambda a: np.ascontiguousarray(np.asarray(a).T)
    wqkvT = T(inputs["L0_attn_Wqkv"])
    wzT = T(inputs["L0_attn_Wz"])
    woutT = T(inputs["L0_attn_Wout"])
    wbaT = np.ascontiguousarray(
        np.concatenate([np.asarray(inputs["L0_attn_Wb"]).T,
                        np.asarray(inputs["L0_attn_Wa"]).T], axis=1))
    wgT = T(inputs["L0_ffn_w_gate"])
    wuT = T(inputs["L0_ffn_w_up"])
    wdT = T(inputs["L0_ffn_w_down"])

    smalls = np.zeros((128, 256), dtype=np.float32)
    smalls[:, SM_BQKV:SM_BQKV + 32] = np.asarray(inputs["L0_attn_bqkv"]).reshape(32, 128).T
    smalls[:, SM_BZ:SM_BZ + 16] = np.asarray(inputs["L0_attn_bz"]).reshape(16, 128).T
    smalls[:, SM_BOUT:SM_BOUT + 16] = np.asarray(inputs["L0_attn_bout"]).reshape(16, 128).T
    smalls[:, SM_CONVB:SM_CONVB + 32] = np.asarray(inputs["L0_attn_conv_b"]).reshape(32, 128).T
    cw = np.asarray(inputs["L0_attn_conv_w"]).reshape(32, 128, 4)
    for c in range(32):
        smalls[:, SM_CONVW + 4 * c:SM_CONVW + 4 * c + 4] = cw[c]
    smalls[:, SM_NORMW] = np.asarray(inputs["L0_attn_norm_w"])
    smalls[0:16, SM_BB] = np.asarray(inputs["L0_attn_bb"])
    smalls[0:16, SM_BA] = np.asarray(inputs["L0_attn_ba"])
    smalls[0:16, SM_DTB] = np.asarray(inputs["L0_attn_dt_bias"])
    smalls[0:16, SM_ALOG] = np.asarray(inputs["L0_attn_A_log"])

    maps = []
    for ci in range(NCORES):
        smc = smalls.copy()
        if ci > 0:
            smc[ci - 1, SM_OH_PREV] = 1.0
        smc[ci, SM_OH_SELF] = 1.0
        if ci > 0:
            smc[:, SM_OHB + ci] = 1.0
        maps.append({
            "x_sl": np.ascontiguousarray(x[ci * TPC:(ci + 1) * TPC]),
            "wqkvT": wqkvT, "wzT": wzT, "woutT": woutT, "wbaT": wbaT,
            "wgT_sl": np.ascontiguousarray(wgT[ci * (HID // NCORES):(ci + 1) * (HID // NCORES)]),
            "wuT_sl": np.ascontiguousarray(wuT[ci * (HID // NCORES):(ci + 1) * (HID // NCORES)]),
            "wdT_sl": np.ascontiguousarray(wdT[ci * (FFN_I // NCORES):(ci + 1) * (FFN_I // NCORES)]),
            "smalls": smc,
        })
    return maps


_PROG_CACHE = {}


def kernel(**inputs):
    if "main" not in _PROG_CACHE:
        _PROG_CACHE["main"] = build_program()
    nc, _ = _PROG_CACHE["main"]
    maps = make_inputs(inputs)
    res = run_bass_kernel_spmd(nc, maps, core_ids=list(range(NCORES)))
    out = np.concatenate([res.results[ci]["out_sl"] for ci in range(NCORES)], axis=0)
    return out[None]
